# version 12
# baseline (speedup 1.0000x reference)
import os

import numpy as np
from contextlib import ExitStack

import concourse.bass as bass
import concourse.mybir as mybir

F32 = mybir.dt.float32
BF16 = mybir.dt.bfloat16
AF = mybir.ActivationFunctionType
ALU = mybir.AluOpType
AXX = mybir.AxisListType.X

DM = 1024
DI = 2048
DS = 64
NE = 8
DH = 2048
NCORE = 8
EPS = 1e-5


class _Trk:

    def __init__(self, eng):
        self._e = eng
        self.last = None

    def __getattr__(self, n):
        f = getattr(self._e, n)
        if callable(f):
            def w(*a, **k):
                r = f(*a, **k)
                if hasattr(r, "then_inc"):
                    self.last = r
                return r
            return w
        return f


DEBUG_SPILL = False


def build_program(n_tok):
    NG = n_tok // 128
    NT = n_tok // 256
    SEG = n_tok // 2
    NGH = NG // 2
    NTH = NT // 2
    assert n_tok % 512 == 0 and NG <= 16

    nc = bass.Bass("TRN2", target_bir_lowering=False)

    x_in = nc.dram_tensor("x_in", [n_tok, DM], F32, kind="ExternalInput").ap()
    W_in = nc.dram_tensor("W_in", [DM, 2 * DI], F32, kind="ExternalInput").ap()
    b_in = nc.dram_tensor("b_in", [1, 2 * DI], F32, kind="ExternalInput").ap()
    W_xp = nc.dram_tensor("W_xp", [DI, DS], F32, kind="ExternalInput").ap()
    b_xp = nc.dram_tensor("b_xp", [1, DS], F32, kind="ExternalInput").ap()
    W_dt = nc.dram_tensor("W_dt", [DI, DM], F32, kind="ExternalInput").ap()
    b_dt = nc.dram_tensor("b_dt", [1, DM], F32, kind="ExternalInput").ap()
    W_out = nc.dram_tensor("W_out", [DI, DM], F32, kind="ExternalInput").ap()
    b_out = nc.dram_tensor("b_out", [1, DM], F32, kind="ExternalInput").ap()
    A_log = nc.dram_tensor("A_log", [1, DS], F32, kind="ExternalInput").ap()
    D_vec = nc.dram_tensor("D_vec", [1, DM], F32, kind="ExternalInput").ap()
    gate_W = nc.dram_tensor("gate_W", [DM, NE], F32, kind="ExternalInput").ap()
    W1 = nc.dram_tensor("W1", [NE, DM, DH], F32, kind="ExternalInput").ap()
    b1 = nc.dram_tensor("b1", [NE, DH], F32, kind="ExternalInput").ap()
    W2 = nc.dram_tensor("W2", [NE, DH, DM], F32, kind="ExternalInput").ap()
    b2 = nc.dram_tensor("b2", [NE, DM], F32, kind="ExternalInput").ap()
    ident = nc.dram_tensor("ident", [128, 128], F32, kind="ExternalInput").ap()
    ones_in = nc.dram_tensor("ones_in", [128, 128], F32, kind="ExternalInput").ap()
    zeros_in = nc.dram_tensor("zeros_in", [128, 4], F32, kind="ExternalInput").ap()
    chain_in = nc.dram_tensor("chain_in", [DS, NCORE], F32, kind="ExternalInput").ap()
    eye8_in = nc.dram_tensor("eye8_in", [NE, NE * 64], F32, kind="ExternalInput").ap()
    binc_in = nc.dram_tensor("binc_in", [128, 32], F32, kind="ExternalInput").ap()
    bdtc_in = nc.dram_tensor("bdtc_in", [128, 8], F32, kind="ExternalInput").ap()
    bxpc_in = nc.dram_tensor("bxpc_in", [64, 1], F32, kind="ExternalInput").ap()
    alog2_in = nc.dram_tensor("alog2_in", [128, 1], F32, kind="ExternalInput").ap()
    b1c_in = nc.dram_tensor("b1c_in", [128, NE * 16], F32, kind="ExternalInput").ap()
    bob_in = nc.dram_tensor("bob_in", [128, DM], F32, kind="ExternalInput").ap()
    dvb_in = nc.dram_tensor("dvb_in", [128, DM], F32, kind="ExternalInput").ap()
    y_out = nc.dram_tensor("y_out", [n_tok, DM], F32, kind="ExternalOutput").ap()

    q_spill = nc.dram_tensor("q_spill", [n_tok, DM], BF16).ap()
    x1_spill = nc.dram_tensor("x1_spill", [n_tok, DM], BF16).ap()
    cc_in_t = nc.dram_tensor("cc_in", [1, 2 * DS], F32)
    cc_out_t = nc.dram_tensor("cc_out", [NCORE, 2 * DS], F32)
    if DEBUG_SPILL:
        dbg_xi = nc.dram_tensor("dbg_xi", [128, 16 * n_tok], BF16, kind="ExternalOutput").ap()
        dbg_md = nc.dram_tensor("dbg_md", [1, n_tok], BF16, kind="ExternalOutput").ap()
        dbg_bx = nc.dram_tensor("dbg_bx", [128, n_tok // 2], F32, kind="ExternalOutput").ap()
        dbg_a = nc.dram_tensor("dbg_a", [128, n_tok // 2], F32, kind="ExternalOutput").ap()
        dbg_sfin = nc.dram_tensor("dbg_sfin", [128, n_tok // 2], F32, kind="ExternalOutput").ap()
        dbg_cols = nc.dram_tensor("dbg_cols", [128, n_tok // 128], F32, kind="ExternalOutput").ap()
        dbg_sres = nc.dram_tensor("dbg_sres", [128, 2048], BF16, kind="ExternalOutput").ap()
        dbg_xtm = nc.dram_tensor("dbg_xtm", [128, 2048], BF16, kind="ExternalOutput").ap()
        dbg_c2 = nc.dram_tensor("dbg_c2", [128, 64], F32, kind="ExternalOutput").ap()
        dbg_w = nc.dram_tensor("dbg_w", [128, 2048], BF16, kind="ExternalOutput").ap()
        dbg_xnfm = nc.dram_tensor("dbg_xnfm", [128, 2048], BF16, kind="ExternalOutput").ap()

    CNT = {}

    def W(eng, sem, val):
        eng.wait_ge(sem[0], val)

    def mk_inc(sem):
        def inc(ins, n=1):
            CNT[sem[1]] += n
            ins.then_inc(sem[0], n)
            return CNT[sem[1]]
        return inc

    st0 = ExitStack()
    with st0:
        e0 = st0.enter_context

        def sem(name):
            h = e0(nc.semaphore(name))
            CNT[name] = 0
            return (h, name)

        SX = sem("sx")
        SWA = sem("swa")
        SWB = sem("swb")
        SW1 = sem("sw1")
        SLD = sem("sld")
        SMM = sem("smm")
        STP = sem("stp")
        SMD = sem("smd")
        SEV = sem("sev")
        SDV = sem("sdv")
        SCC = sem("scc")
        SQ = [sem("sq0"), sem("sq1")]
        SR = [sem("sr0"), sem("sr1")]
        SSP = sem("ssp")
        SW1B = sem("sw1b")
        SB2 = sem("sb2")
        SCD = sem("scd")
        SXX = sem("sxx")
        SWDT = sem("swdt")
        SWXP = sem("swxp")
        SWO = sem("swo")
        SOUT = sem("sout")

        def val(s):
            return CNT[s[1]]

        bigA = e0(nc.sbuf_tensor("bigA", [128, 16 * n_tok], BF16))
        regW1 = e0(nc.sbuf_tensor("regW1", [128, 16384], BF16))
        regW2 = e0(nc.sbuf_tensor("regW2", [128, 16384], BF16))
        eye8_sb = e0(nc.sbuf_tensor("eye8_sb", [NE, NE * 64], BF16))
        zcol = e0(nc.sbuf_tensor("zcol", [128, 4], F32))
        cols = e0(nc.sbuf_tensor("cols", [128, 256], F32))
        b1c = e0(nc.sbuf_tensor("b1c", [128, NE * 16], F32))
        cwrows = e0(nc.sbuf_tensor("cwrows", [NE, n_tok], BF16))

        C_BIN, C_BDT, C_BXP, C_A2, C_S0 = 0, 32, 40, 41, 42
        C_M1, C_R1, C_SC = 48, 64, 80
        C_M3, C_R3, C_M4, C_R4 = 96, 112, 128, 144
        C_SUM, C_SSQ, C_T0, C_T1, C_T2 = 160, 176, 192, 208, 224

        def cc(i, n=1):
            return cols[:, i:i + n]

        from concourse.hw_specs import get_activation_tables
        _lnexp_id = None
        for _idx, _funcs in enumerate(get_activation_tables(nc.m.arch).values()):
            if AF.Ln in _funcs and AF.Exp in _funcs:
                _lnexp_id = _idx
                break
        assert _lnexp_id is not None

        def load_lnexp(seng):
            seng.add_instruction(mybir.InstLoadActFuncSet(
                name=nc.get_next_instruction_name(),
                act_func_set_id=_lnexp_id,
                ins=[], outs=[]))

        def xi(k, t0, nt):
            return bigA[:, k * n_tok + t0: k * n_tok + t0 + nt]

        def xi3(t0, nt):
            return bigA[:, :].rearrange("p (k t) -> p k t", t=n_tok)[:, :, t0:t0 + nt]

        oacc_all = bigA[:, 8 * n_tok: 16 * n_tok].bitcast(F32)

        def oacc(gh):
            return oacc_all[:, gh * DM:(gh + 1) * DM]

        stm = ExitStack()
        with stm:
            em = stm.enter_context
            x_tm = em(nc.sbuf_tensor("x_tm", [128, NG * DM], BF16))
            gate_sb = em(nc.sbuf_tensor("gate_sb", [128, 8 * NE], BF16))
            id_bf = em(nc.sbuf_tensor("id_bf", [128, 128], BF16))
            id_f32 = em(nc.sbuf_tensor("id_f32", [128, 128], F32))
            ones_f = em(nc.sbuf_tensor("ones_f", [128, 128], F32))
            ones_b = em(nc.sbuf_tensor("ones_b", [128, 128], BF16))
            bob_bc = em(nc.sbuf_tensor("bob_bc", [128, DM], BF16))
            dvb_bc = em(nc.sbuf_tensor("dvb_bc", [128, DM], BF16))
            cmask = em(nc.sbuf_tensor("cmask", [64, NCORE], F32))
            stab = ExitStack()
            eab = stab.enter_context
            stb = eab(nc.sbuf_tensor("stb", [128, SEG], F32))
            mdrow = eab(nc.sbuf_tensor("mdrow", [1, n_tok], BF16))
            fdrow = eab(nc.sbuf_tensor("fdrow", [1, 2 * DS], F32))

            def xg(g):
                return x_tm[:, g * DM:(g + 1) * DM]

            sti = ExitStack()
            with sti:
                ei = sti.enter_context
                scr0 = ei(nc.sbuf_tensor("scr0", [128, DM], BF16))
                tcol = ei(nc.sbuf_tensor("tcol", [128, 1], F32))

                with nc.Block() as blk:
                    i_sx, i_sld, i_sev, i_sdv = (mk_inc(s) for s in (SX, SLD, SEV, SDV))

                    @blk.gpsimd
                    def _(g_raw):
                        g = _Trk(g_raw)
                        ins_x = g.dma_start(out=x_tm[:, :].rearrange("p (g f) -> p g f", f=DM),
                                            in_=x_in.rearrange("(g p) f -> p g f", p=128))
                        ins_x.then_inc(SXX[0], 16)
                        CNT["sxx"] += 16
                        i_sx(g.dma_start(out=id_bf[:, :], in_=ident[:, :]), 16)
                        i_sx(g.dma_start(out=ones_b[:, :], in_=ones_in[:, :]), 16)
                        i_sx(g.dma_start(out=gate_sb[:, :].rearrange("p (k j) -> p k j", j=NE),
                                         in_=gate_W.rearrange("(k p) j -> p k j", p=128)), 16)
                        i_sx(g.dma_start(out=eye8_sb[:, :], in_=eye8_in[:, :]), 16)
                        i_sx(g.dma_start(out=bob_bc[:, :], in_=bob_in[:, :]), 16)
                        i_sx(g.dma_start(out=dvb_bc[:, :], in_=dvb_in[:, :]), 16)

                    @blk.sync
                    def _(s_raw):
                        s = _Trk(s_raw)
                        i_sld(s.dma_start(out=id_f32[:, :], in_=ident[:, :]), 16)
                        i_sld(s.dma_start(out=ones_f[:, :], in_=ones_in[:, :]), 16)
                        i_sld(s.dma_start(out=zcol[:, :], in_=zeros_in[:, :]), 16)
                        i_sld(s.dma_start(out=cc(C_BIN, 32), in_=binc_in[:, :]), 16)
                        i_sld(s.dma_start(out=cc(C_BDT, 8), in_=bdtc_in[:, :]), 16)
                        i_sld(s.dma_start(out=cols[0:64, C_BXP:C_BXP + 1], in_=bxpc_in[:, :]), 16)
                        i_sld(s.dma_start(out=tcol[:, :], in_=alog2_in[:, :]), 16)
                        i_sld(s.dma_start(out=b1c[:, :], in_=b1c_in[:, :]), 16)
                        i_sld(s.dma_start(out=cmask[:, :], in_=chain_in[:, :]), 16)
                    LD_ALL = val(SLD)

                    @blk.scalar
                    def _(s_raw):
                        s = _Trk(s_raw)
                        load_lnexp(s)
                        W(s, SLD, LD_ALL)
                        i_sev(s.activation(tcol[:, :], tcol[:, :], AF.Exp))
                        W(s, SXX, 16)
                        for g in range(NG):
                            s.drain()
                            s.activation(scr0[:, :], xg(g), AF.Square,
                                         accum_out=cc(C_SSQ + g))
                        i_sev(s.last)
                    EV_I = val(SEV)

                    @blk.vector
                    def _(v_raw):
                        v = _Trk(v_raw)
                        W(v, SEV, EV_I - 1)
                        v.drain()
                        v.tensor_scalar(cc(C_A2), tcol[:, 0:1], -1.0 / DM, None, ALU.mult)
                        W(v, SXX, 16)
                        for g in range(NG):
                            v.reduce_sum(cc(C_SUM + g), xg(g), axis=AXX)
                        v.drain()
                        W(v, SEV, EV_I)
                        v.tensor_scalar(cc(C_M1, NG), cc(C_SUM, NG), 1.0 / DM, None, ALU.mult)
                        v.drain()
                        v.tensor_mul(cc(C_T0, NG), cc(C_M1, NG), cc(C_M1, NG))
                        v.tensor_scalar(cc(C_T1, NG), cc(C_SSQ, NG), 1.0 / DM, None, ALU.mult)
                        v.drain()
                        v.tensor_sub(cc(C_T2, NG), cc(C_T1, NG), cc(C_T0, NG))
                        i_sdv(v.last)
                    DV_I = val(SDV)

                    @blk.scalar
                    def _(s_raw):
                        s = _Trk(s_raw)
                        load_lnexp(s)
                        W(s, SDV, DV_I)
                        s.activation(cc(C_T0, NG), cc(C_T2, NG), AF.Ln, bias=zcol[:, 1:2])
                        s.drain()
                        s.activation(cc(C_R1, NG), cc(C_T0, NG), AF.Exp, scale=-0.5)
                        i_sev(s.last)
            SX_INIT = val(SX)
            SLD_INIT = val(SLD)

            DV_A0 = val(SDV)
            TP_A0 = val(STP)
            MM_A0 = val(SMM)
            EV_A0 = val(SEV)

            def DVA(t, j):
                return DV_A0 + 2 * t + j

            def TPA(t):
                return TP_A0 + t + 1

            def MMA(t, j):
                return MM_A0 + 13 * t + j

            def EVA(t, j):
                return EV_A0 + 15 * t + j

            stpa = ExitStack()
            with stpa:
                ea = stpa.enter_context
                wxp_sb = ea(nc.sbuf_tensor("wxp_sb", [128, 16 * DS], BF16))
                xn_tm = [ea(nc.sbuf_tensor(f"xn_tm{i}", [128, 2048], BF16)) for i in range(2)]
                xn_fm = [ea(nc.sbuf_tensor(f"xn_fm{i}", [128, 2048], BF16)) for i in range(2)]
                ez_sc = ea(nc.sbuf_tensor("ez_sc", [128, 1024], BF16))
                sp_sc = ea(nc.sbuf_tensor("sp_sc", [128, 2048], BF16))

                with nc.Block() as blk, \
                     nc.psum_tensor("ptp", [128, 2048], BF16) as ptp, \
                     nc.psum_tensor("pxi", [128, 1024], F32) as pxi, \
                     nc.psum_tensor("pdl", [128, 1024], F32) as pdl, \
                     nc.psum_tensor("pmd", [1, 256], F32) as pmd, \
                     nc.psum_tensor("pbx", [64, 256], F32) as pbx:
                    i_swa, i_swb, i_smm, i_stp, i_smd, i_sev, i_sdv = \
                        (mk_inc(s) for s in (SWA, SWB, SMM, STP, SMD, SEV, SDV))

                    @blk.vector
                    def _(v_raw):
                        v = _Trk(v_raw)
                        W(v, SX, SX_INIT)
                        W(v, SLD, SLD_INIT)
                        for t in range(NT):
                            v.drain()
                            if t >= 2:
                                W(v, STP, TPA(t - 2))
                            for g2 in range(2):
                                g = 2 * t + g2
                                v.tensor_scalar(xn_tm[t % 2][:, g2 * 1024:(g2 + 1) * 1024],
                                                xg(g), cc(C_M1 + g), cc(C_R1 + g),
                                                ALU.subtract, ALU.mult)
                            i_sdv(v.last)
                            W(v, STP, TPA(t))
                            if t >= 2:
                                W(v, SMM, MMA(t - 2, 13))
                            v.drain()
                            v.tensor_copy(
                                xn_fm[t % 2][:, :].rearrange("p (f g2 tok) -> p f g2 tok",
                                                             g2=2, tok=128),
                                ptp[:, :].rearrange("p (g2 f tok) -> p g2 f tok",
                                                    f=8, tok=128).transpose([0, 2, 1, 3]))
                            i_sdv(v.last)
                        assert val(SDV) == DVA(NT - 1, 2)

                    @blk.tensor
                    def _(t_raw):
                        t_ = _Trk(t_raw)
                        W(t_, SX, SX_INIT)
                        W(t_, SWA, 16)
                        for t in range(NT):
                            if t == 0:
                                pass
                            W(t_, SDV, DVA(t, 1))
                            if t >= 1:
                                W(t_, SDV, DVA(t - 1, 2))
                            for g2 in range(2):
                                for f in range(8):
                                    t_.transpose(ptp[:, g2 * 1024 + f * 128:g2 * 1024 + (f + 1) * 128],
                                                 xn_tm[t % 2][:, g2 * 1024 + f * 128:g2 * 1024 + (f + 1) * 128],
                                                 id_bf[:, :])
                            i_stp(t_.last)
                            W(t_, SDV, DVA(t, 2))
                            for np_ in range(8):
                                bank = np_ % 2
                                if np_ >= 2:
                                    W(t_, SEV, EVA(t, np_ - 1))
                                for n2 in range(2):
                                    n = np_ * 2 + n2
                                    for k in range(8):
                                        t_.matmul(pxi[:, bank * 512 + n2 * 256: bank * 512 + n2 * 256 + 256],
                                                  regW1[:, k * 2048 + n * 128: k * 2048 + (n + 1) * 128],
                                                  xn_fm[t % 2][:, k * 256:(k + 1) * 256],
                                                  start=(k == 0), stop=(k == 7))
                                i_smm(t_.last)
                            W(t_, SEV, EVA(t, 8))
                            if t == 0:
                                W(t_, SWDT, 16)
                            for f in range(4):
                                bank = f % 2
                                if f >= 2:
                                    W(t_, SEV, EVA(t, 8 + f - 1))
                                for n2 in range(2):
                                    nn = f * 2 + n2
                                    for k in range(16):
                                        t_.matmul(pdl[:, bank * 512 + n2 * 256: bank * 512 + n2 * 256 + 256],
                                                  regW2[:, k * 1024 + nn * 128: k * 1024 + (nn + 1) * 128],
                                                  xi(k, t * 256, 256),
                                                  start=(k == 0), stop=(k == 15))
                                i_smm(t_.last)
                            if t >= 1:
                                W(t_, SEV, EVA(t - 1, 15))
                            W(t_, SEV, EVA(t, 11))
                            for ch in range(4):
                                t_.matmul(pmd[:, 0:256], ones_b[0:128, 0:1],
                                          sp_sc[:, ch * 256:(ch + 1) * 256],
                                          start=(ch == 0), stop=False)
                            W(t_, SEV, EVA(t, 14))
                            for ch in range(4, 8):
                                t_.matmul(pmd[:, 0:256], ones_b[0:128, 0:1],
                                          sp_sc[:, ch * 256:(ch + 1) * 256],
                                          start=False, stop=(ch == 7))
                            i_smd(t_.last)
                            if t == 0:
                                W(t_, SWXP, 16)
                            for k in range(16):
                                t_.matmul(pbx[:, 0:256], wxp_sb[:, k * DS:(k + 1) * DS],
                                          xi(k, t * 256, 256), start=(k == 0), stop=(k == 15))
                            i_smm(t_.last)
                        assert val(SMM) == MMA(NT - 1, 13)

                    @blk.scalar
                    def _(s_raw):
                        s = _Trk(s_raw)
                        for t in range(NT):
                            s.drain()
                            s.preload_act_table(AF.Silu)
                            for np_ in range(8):
                                bank = np_ % 2
                                W(s, SMM, MMA(t, np_ + 1))
                                for n2 in range(2):
                                    n = np_ * 2 + n2
                                    s.activation(xi(n, t * 256, 256),
                                                 pxi[:, bank * 512 + n2 * 256: bank * 512 + n2 * 256 + 256],
                                                 AF.Silu, bias=cc(C_BIN + n))
                                i_sev(s.last)
                            load_lnexp(s)
                            for half in range(2):
                                s.drain()
                                for f2 in range(2):
                                    f = half * 2 + f2
                                    bank = f % 2
                                    W(s, SMM, MMA(t, 8 + f + 1))
                                    for n2 in range(2):
                                        nn = f * 2 + n2
                                        s.activation(ez_sc[:, f2 * 512 + n2 * 256: f2 * 512 + n2 * 256 + 256],
                                                     pdl[:, bank * 512 + n2 * 256: bank * 512 + n2 * 256 + 256],
                                                     AF.Exp, bias=cc(C_BDT + nn))
                                    i_sev(s.last)
                                s.drain()
                                i_sev(s.activation(sp_sc[:, half * 1024:(half + 1) * 1024],
                                                   ez_sc[:, :], AF.Ln, bias=zcol[:, 2:3]))
                            W(s, SMD, t + 1)
                            s.activation(mdrow[0:1, t * 256:(t + 1) * 256], pmd[:, 0:256], AF.Copy)
                            W(s, SMM, MMA(t, 13))
                            seg = (t * 256) // SEG
                            off = (t * 256) % SEG
                            s.activation(stb[seg * 64:(seg + 1) * 64, off:off + 256], pbx[:, 0:256],
                                         AF.Identity, bias=cols[0:64, C_BXP:C_BXP + 1])
                            i_sev(s.last)
                        assert val(SEV) == EVA(NT - 1, 15)

                    @blk.gpsimd
                    def _(g_raw):
                        g = _Trk(g_raw)
                        i_swa(g.dma_start(
                            out=regW1[:, :].rearrange("p (k n) -> p k n", n=2048),
                            in_=W_in.rearrange("(k p) n -> p k n", p=128)[:, :, 0:DI]), 16)
                        g.dma_start(
                            out=regW2[:, :].rearrange("p (k n) -> p k n", n=1024),
                            in_=W_dt.rearrange("(k p) n -> p k n", p=128)).then_inc(SWDT[0], 16)
                        CNT["swdt"] += 16
                        g.dma_start(
                            out=wxp_sb[:, :].rearrange("p (k n) -> p k n", n=DS),
                            in_=W_xp.rearrange("(k p) n -> p k n", p=128)).then_inc(SWXP[0], 16)
                        CNT["swxp"] += 16
                        W(g, SMM, MMA(NT - 1, 8))
                        i_swb(g.dma_start(
                            out=regW1[:, :].rearrange("p (k n) -> p k n", n=2048),
                            in_=W_in.rearrange("(k p) n -> p k n", p=128)[:, :, DI:2 * DI]), 16)
                        W(g, SMM, MMA(NT - 1, 12))
                        g.dma_start(
                            out=regW2[:, :].rearrange("p (k n) -> p k n", n=1024),
                            in_=W_out.rearrange("(k p) n -> p k n", p=128)).then_inc(SWO[0], 16)
                        CNT["swo"] += 16

            DV_B0 = val(SDV)
            TP_B0 = val(STP)
            MM_B0 = val(SMM)
            EV_B0 = val(SEV)

            def DVB(g, j):
                return DV_B0 + 4 + 3 * g + j

            def MMB(g, j):
                return MM_B0 + 1 + 5 * g + (1 if g >= 1 else 0) + j

            def EVB(g, j):
                return EV_B0 + 1 + 5 * g + (1 if g >= 1 else 0) + j

            CC_TAILS = val(SCC) + 64
            CC_CIN = CC_TAILS + 16
            CC_GATH = CC_CIN + 16 * 2 * NCORE
            CC_S0 = CC_GATH + 32

            sta2 = ExitStack()
            with sta2:
                e2 = sta2.enter_context
                a_sc = e2(nc.sbuf_tensor("a_sc", [128, SEG], F32))
                stc = e2(nc.sbuf_tensor("stc", [128, SEG], F32))
                _xn2tm = e2(nc.sbuf_tensor("xn2tm", [128, 1024], BF16))
                _xn2fm = e2(nc.sbuf_tensor("xn2fm", [128, 1024], BF16))
                xn2_tm = [_xn2tm, _xn2tm]
                xn2_fm = [_xn2fm, _xn2fm]
                sres = e2(nc.sbuf_tensor("sres", [128, 2048], BF16))
                p_sc = e2(nc.sbuf_tensor("p_sc", [128, 2048], BF16))
                q_sb = [e2(nc.sbuf_tensor(f"q_sb{i}", [128, DM], BF16)) for i in range(2)]
                tails = e2(nc.sbuf_tensor("tails", [64, 4], F32))
                fdcol = e2(nc.sbuf_tensor("fdcol", [128, 1], F32))
                dgt = e2(nc.sbuf_tensor("dgt", [64, NCORE], F32))
                fgt = e2(nc.sbuf_tensor("fgt", [64, NCORE], F32))
                dgp = e2(nc.sbuf_tensor("dgp", [64, NCORE], F32))
                fgp = e2(nc.sbuf_tensor("fgp", [64, NCORE], F32))
                srun = e2(nc.sbuf_tensor("srun", [64, 3], F32))

                with nc.Block() as blk, \
                     nc.psum_tensor("pmdb", [128, min(SEG, 1024)], F32) as pmdb, \
                     nc.psum_tensor("psc", [128, 512], F32) as psc, \
                     nc.psum_tensor("ptp2", [128, 1024], BF16) as ptp2, \
                     nc.psum_tensor("pres", [128, 1024], F32) as pres, \
                     nc.psum_tensor("pq", [128, 1024], F32) as pq:
                    i_smm, i_stp, i_sev, i_sdv, i_scc = \
                        (mk_inc(s) for s in (SMM, STP, SEV, SDV, SCC))
                    i_sq = [mk_inc(SQ[0]), mk_inc(SQ[1])]

                    @blk.vector
                    def _(v_raw):
                        v = _Trk(v_raw)
                        W(v, SMM, MM_B0 + 1)
                        v.tensor_scalar(stc[:, :], pmdb[:, 0:SEG], cc(C_A2), None, ALU.mult)
                        i_sdv(v.last)
                        W(v, SEV, EV_B0 + 1)
                        v.drain()
                        if DEBUG_SPILL:
                            W(v, SOUT, val(SOUT) + 144)
                        v.tensor_tensor_scan(stc[:, :], a_sc[:, :], stb[:, :], 0.0,
                                             ALU.mult, ALU.add)
                        i_sdv(v.last)
                        v.drain()
                        v.tensor_tensor_scan(stb[:, :], a_sc[:, :], a_sc[:, :], 1.0,
                                             ALU.mult, ALU.bypass)
                        i_sdv(v.last)
                        W(v, SCC, CC_TAILS)
                        v.tensor_mul(fdcol[0:64, 0:1], tails[:, 0:1], tails[:, 1:2])
                        v.tensor_mul(srun[:, 0:1], tails[:, 1:2], tails[:, 2:3])
                        v.drain()
                        v.tensor_add(fdcol[64:128, 0:1], tails[:, 3:4], srun[:, 0:1])
                        i_sdv(v.last)
                        for g in range(NG):
                            v.drain()
                            if g >= 1:
                                W(v, STP, TP_B0 + g)
                            v.tensor_scalar(xn2_tm[g % 2][:, :], xg(g), cc(C_M1 + g),
                                            cc(C_R1 + g), ALU.subtract, ALU.mult)
                            i_sdv(v.last)
                            W(v, STP, TP_B0 + g + 1)
                            if g >= 1:
                                W(v, SMM, MMB(g - 1, 4))
                            v.drain()
                            v.tensor_copy(xn2_fm[g % 2][:, :], ptp2[:, :])
                            i_sdv(v.last)
                            W(v, SEV, EVB(g, 4))
                            if g >= 1:
                                W(v, SMM, MMB(g - 1, 5))
                            v.drain()
                            v.tensor_mul(p_sc[:, :].rearrange("p (k t) -> p k t", t=128),
                                         xi3(g * 128, 128),
                                         sres[:, :].rearrange("p (k t) -> p k t", t=128))
                            i_sdv(v.last)
                        W(v, SCC, CC_GATH)
                        v.tensor_mul(dgp[:, :], dgt[:, :], cmask[:, :])
                        v.tensor_mul(fgp[:, :], fgt[:, :], cmask[:, :])
                        v.drain()
                        v.tensor_sub(dgp[:, :], dgp[:, :], cmask[:, :])
                        v.drain()
                        v.tensor_scalar(dgp[:, :], dgp[:, :], 1.0, None, ALU.add)
                        v.tensor_copy(srun[:, 0:1], zcol[0:64, 0:1])
                        v.drain()
                        cur = 0
                        for r in range(NCORE):
                            v.tensor_mul(srun[:, 2:3], dgp[:, r:r + 1], srun[:, cur:cur + 1])
                            v.drain()
                            v.tensor_add(srun[:, 1 - cur:2 - cur], srun[:, 2:3], fgp[:, r:r + 1])
                            v.drain()
                            cur = 1 - cur
                        assert cur == 0
                        v.tensor_mul(srun[:, 2:3], tails[:, 0:1], srun[:, 0:1])
                        v.drain()
                        v.tensor_add(srun[:, 1:2], srun[:, 2:3], tails[:, 2:3])
                        i_sdv(v.last)
                        W(v, SCC, CC_S0)
                        v.drain()
                        v.scalar_tensor_tensor(a_sc[:, :], stb[:, :], cc(C_S0),
                                               stc[:, :], ALU.mult, ALU.add)
                        i_sdv(v.last)
                        assert val(SDV) == DV_B0 + 4 + 3 * NG + 2

                    @blk.tensor
                    def _(t_raw):
                        t_ = _Trk(t_raw)
                        for seg in range(2):
                            for h in range((SEG + 511) // 512):
                                w_ = min(512, SEG - h * 512)
                                t_.matmul(pmdb[seg * 64:(seg + 1) * 64, h * 512:h * 512 + w_],
                                          ones_b[0:1, 0:64],
                                          mdrow[0:1, seg * SEG + h * 512: seg * SEG + h * 512 + w_],
                                          start=True, stop=True)
                        i_smm(t_.last)
                        for g in range(NG):
                            W(t_, SDV, DVB(g, 1))
                            if g >= 1:
                                W(t_, SDV, DVB(g - 1, 2))
                            for f in range(8):
                                t_.transpose(ptp2[:, f * 128:(f + 1) * 128],
                                             xn2_tm[g % 2][:, f * 128:(f + 1) * 128], id_bf[:, :])
                            i_stp(t_.last)
                            W(t_, SDV, DVB(g, 2))
                            if g == 0:
                                W(t_, SWB, 16)
                            for f in range(4):
                                bank = f % 2
                                if f >= 2:
                                    W(t_, SEV, EVB(g, f - 1))
                                for n4 in range(4):
                                    n = f * 4 + n4
                                    for k in range(8):
                                        t_.matmul(pres[:, bank * 512 + n4 * 128: bank * 512 + (n4 + 1) * 128],
                                                  regW1[:, k * 2048 + n * 128: k * 2048 + (n + 1) * 128],
                                                  xn2_fm[g % 2][:, k * 128:(k + 1) * 128],
                                                  start=(k == 0), stop=(k == 7))
                                i_smm(t_.last)
                            W(t_, SDV, DVB(g, 3))
                            if g == 0:
                                W(t_, SWO, 16)
                            if g >= 1:
                                W(t_, SEV, EVB(g - 1, 5))
                            for n2 in range(2):
                                for k in range(16):
                                    t_.matmul(pq[:, n2 * 512:(n2 + 1) * 512],
                                              p_sc[:, k * 128:(k + 1) * 128],
                                              regW2[:, k * 1024 + n2 * 512: k * 1024 + n2 * 512 + 512],
                                              start=(k == 0), stop=(k == 15))
                            i_smm(t_.last)
                            if g == 0:
                                W(t_, SDV, DV_B0 + 4)
                                t_.matmul(psc[0:1, 0:128], fdcol[:, 0:1], id_f32[:, :],
                                          start=True, stop=True)
                                i_smm(t_.last)
                        W(t_, SDV, DV_B0 + 4 + 3 * NG + 2)
                        for g in range(NG):
                            seg = (g * 128) // SEG
                            off = (g * 128) % SEG
                            t_.matmul(psc[:, 256 + g:256 + g + 1],
                                      a_sc[seg * 64:(seg + 1) * 64, off:off + 128],
                                      ones_f[seg * 64:(seg + 1) * 64, 0:1],
                                      start=True, stop=True)
                        i_smm(t_.last)
                        assert val(SMM) == MM_B0 + 1 + 5 * NG + 2

                    @blk.scalar
                    def _(s_raw):
                        s = _Trk(s_raw)
                        load_lnexp(s)
                        W(s, SDV, DV_B0 + 1)
                        i_sev(s.activation(a_sc[:, :], stc[:, :], AF.Exp))
                        s.preload_act_table(AF.Silu)
                        for g in range(NG):
                            s.drain()
                            for f in range(4):
                                bank = f % 2
                                W(s, SMM, MMB(g, f + 1))
                                for n4 in range(4):
                                    n = f * 4 + n4
                                    s.activation(sres[:, n * 128:(n + 1) * 128],
                                                 pres[:, bank * 512 + n4 * 128: bank * 512 + (n4 + 1) * 128],
                                                 AF.Silu, bias=cc(C_BIN + 16 + n))
                                i_sev(s.last)
                            W(s, SMM, MMB(g, 5))
                            if g >= 2:
                                W(s, SQ[g % 2], 16 * (g // 2))
                            i_sev(s.activation(q_sb[g % 2][:, :], pq[:, :], AF.Copy))
                            if g == 0:
                                W(s, SMM, MMB(0, 6))
                                s.drain()
                                i_sev(s.activation(fdrow[:, :], psc[0:1, 0:128], AF.Copy))
                        W(s, SMM, MM_B0 + 1 + 5 * NG + 2)
                        s.drain()
                        i_sev(s.activation(cc(C_SC, NG), psc[:, 256:256 + NG], AF.Copy))
                        assert val(SEV) == EV_B0 + 1 + 5 * NG + 1 + 1

                    @blk.gpsimd
                    def _(g_raw):
                        g = _Trk(g_raw)
                        W(g, SDV, DV_B0 + 3)
                        i_scc(g.dma_start(out=tails[:, 0:1], in_=stb[0:64, SEG - 1:SEG]), 16)
                        i_scc(g.dma_start(out=tails[:, 1:2], in_=stb[64:128, SEG - 1:SEG]), 16)
                        i_scc(g.dma_start(out=tails[:, 2:3], in_=stc[0:64, SEG - 1:SEG]), 16)
                        i_scc(g.dma_start(out=tails[:, 3:4], in_=stc[64:128, SEG - 1:SEG]), 16)
                        assert val(SCC) == CC_TAILS
                        W(g, SEV, EVB(0, 6))
                        i_scc(g.dma_start(out=cc_in_t.ap(), in_=fdrow[:, :]), 16)
                        assert val(SCC) == CC_CIN
                        W(g, SCC, CC_CIN)
                        g.collective_compute(
                            "AllGather", ALU.bypass,
                            replica_groups=[list(range(NCORE))],
                            ins=[cc_in_t.ap().opt()], outs=[cc_out_t.ap().opt()]).then_inc(SCD[0], 1)
                        CNT["scd"] += 1
                        W(g, SCD, 1)
                        for r in range(NCORE):
                            i_scc(g.dma_start(out=dgt[:, r:r + 1],
                                              in_=cc_out_t.ap()[r:r + 1, 0:DS]), 16)
                            i_scc(g.dma_start(out=fgt[:, r:r + 1],
                                              in_=cc_out_t.ap()[r:r + 1, DS:2 * DS]), 16)
                        assert val(SCC) == CC_GATH
                        W(g, SDV, DV_B0 + 4 + 3 * NG + 1)
                        i_scc(g.dma_start(out=cols[0:64, C_S0:C_S0 + 1], in_=srun[:, 0:1]), 16)
                        i_scc(g.dma_start(out=cols[64:128, C_S0:C_S0 + 1], in_=srun[:, 1:2]), 16)
                        assert val(SCC) == CC_S0

                    @blk.sync
                    def _(s_raw):
                        s = _Trk(s_raw)
                        if DEBUG_SPILL:
                            s.dma_start(out=dbg_xi[:, :], in_=bigA[:, :]).then_inc(SOUT[0], 16)
                            s.dma_start(out=dbg_md[:, :], in_=mdrow[:, :]).then_inc(SOUT[0], 16)
                            s.dma_start(out=dbg_bx[:, :], in_=stb[:, :]).then_inc(SOUT[0], 16)
                            s.dma_start(out=dbg_xtm[:, :], in_=x_tm[:, 0:2048]).then_inc(SOUT[0], 16)
                            s.dma_start(out=dbg_c2[:, 0:16], in_=cols[:, 48:64]).then_inc(SOUT[0], 16)
                            s.dma_start(out=dbg_c2[:, 16:32], in_=cols[:, 64:80]).then_inc(SOUT[0], 16)
                            s.dma_start(out=dbg_c2[:, 32:64], in_=cols[:, 0:32]).then_inc(SOUT[0], 16)
                            s.dma_start(out=dbg_w[:, :], in_=regW1[:, 0:2048]).then_inc(SOUT[0], 16)
                            CNT["sout"] += 128
                            W(s, SEV, EV_B0 + 1)
                            s.dma_start(out=dbg_a[:, :], in_=a_sc[:, :]).then_inc(SOUT[0], 16)
                            CNT["sout"] += 16
                        for g in range(NG):
                            W(s, SEV, EVB(g, 5))
                            i_sq[g % 2](s.dma_start(out=q_spill.rearrange("(g p) f -> g p f", p=128)[g],
                                                    in_=q_sb[g % 2][:, :]), 16)
                        if DEBUG_SPILL:
                            W(s, SDV, DV_B0 + 4 + 3 * NG + 2)
                            s.dma_start(out=dbg_sfin[:, :], in_=a_sc[:, :]).then_inc(SOUT[0], 16)
                            s.dma_start(out=dbg_sres[:, :], in_=sres[:, :]).then_inc(SOUT[0], 16)
                            CNT["sout"] += 32
                            W(s, SOUT, val(SOUT))
                        W(s, SQ[0], val(SQ[0]))
                        W(s, SQ[1], val(SQ[1]))

            stab.close()

            DV_C0 = val(SDV)
            EV_C0 = val(SEV)
            MM_C0 = val(SMM)
            TP_C0 = val(STP)
            SQ_TOT = [val(SQ[0]), val(SQ[1])]
            SR_C0 = [val(SR[0]), val(SR[1])]
            SSP_C0 = val(SSP)
            NB = min(int(os.environ.get('KNB', '8')), NG)
            NBAT = NG // NB

            def DVB0(b):
                return DV_C0 + (4 * NB + 2) * b

            def EVB0(b):
                return EV_C0 + 2 + (3 * NB + 2) * b

            def DV_ATTN(g):
                return DVB0(g // NB) + (g % NB) + 1

            def DV_X1(g):
                return DVB0(g // NB) + NB + 1 + (g % NB) + 1

            def DV_X1N(g):
                return DVB0(g // NB) + 2 * NB + 2 + 2 * (g % NB) + 1

            def DV_EVC3(g):
                return DVB0(g // NB) + 2 * NB + 2 + 2 * (g % NB) + 2

            def EV_LOG(g):
                return EVB0(g // NB) + 2 * NB + 2 + (g % NB) + 1

            DV_SMAX = DV_C0 + (4 * NB + 2) * NBAT
            EV_ALL = EV_C0 + 2 + (3 * NB + 2) * NBAT

            stc_sc = ExitStack()
            with stc_sc:
                ec = stc_sc.enter_context
                qstage = [ec(nc.sbuf_tensor(f"qstage{i}", [128, DM], BF16)) for i in range(2)]
                attn8 = ec(nc.sbuf_tensor("attn8", [128, NB * DM], BF16))
                _x1ntm = ec(nc.sbuf_tensor("x1ntm", [128, DM], BF16))
                x1n_tm = [_x1ntm, _x1ntm]
                scr1 = ec(nc.sbuf_tensor("scr1", [128, DM], BF16))
                logits = ec(nc.sbuf_tensor("logits", [128, NG * NE], F32))
                gsm = ec(nc.sbuf_tensor("gsm", [128, NG * NE], F32))
                gsm2 = ec(nc.sbuf_tensor("gsm2", [128, NG * NE], F32))
                cw_tm = ec(nc.sbuf_tensor("cw_tm", [128, NG * NE], F32))
                gcol = ec(nc.sbuf_tensor("gcol", [128, 5 * NG], F32))

                def at8(i):
                    return attn8[:, i * DM:(i + 1) * DM]

                def lg3(t):
                    return t[:, :].rearrange("p (g j) -> p g j", j=NE)

                def gc(i):
                    return gcol[:, i * NG:(i + 1) * NG]

                def gc3(i):
                    return gcol[:, i * NG:(i + 1) * NG].unsqueeze(2).broadcast_to([128, NG, NE])

                with nc.Block() as blk, \
                     nc.psum_tensor("pbc", [128, DM], F32) as pbc, \
                     nc.psum_tensor("ptp3", [128, 1024], BF16) as ptp3, \
                     nc.psum_tensor("pgt", [128, NE], F32) as pgt, \
                     nc.psum_tensor("pcw", [8, 512], F32) as pcw:
                    i_sw1, i_smm, i_stp, i_sev, i_sdv, i_ssp = \
                        (mk_inc(s) for s in (SW1, SMM, STP, SEV, SDV, SSP))
                    i_sr = [mk_inc(SR[0]), mk_inc(SR[1])]

                    @blk.vector
                    def _(v_raw):
                        v = _Trk(v_raw)
                        for b in range(NBAT):
                            g0 = b * NB
                            for i in range(NB):
                                g = g0 + i
                                v.drain()
                                if b >= 1:
                                    W(v, SEV, EVB0(b - 1) + i + 1)
                                v.tensor_scalar(scr1[:, :], xg(g), cc(C_M1 + g), cc(C_R1 + g),
                                                ALU.subtract, ALU.mult)
                                v.drain()
                                v.tensor_mul(scr1[:, :], scr1[:, :], dvb_bc[:, :])
                                v.drain()
                                v.tensor_add(scr1[:, :], scr1[:, :], bob_bc[:, :])
                                v.drain()
                                W(v, SR[g % 2], SR_C0[g % 2] + 16 * (g // 2 + 1))
                                v.scalar_tensor_tensor(at8(i), qstage[g % 2][:, :],
                                                       cc(C_SC + g), scr1[:, :],
                                                       ALU.mult, ALU.add,
                                                       accum_out=cc(C_SUM + g))
                                i_sdv(v.last)
                            W(v, SEV, EVB0(b) + NB)
                            v.drain()
                            v.tensor_scalar(cc(C_T0, NB), cc(C_SUM + g0, NB), 1.0 / DM,
                                            None, ALU.mult)
                            v.drain()
                            v.tensor_mul(cc(C_T2, NB), cc(C_T0, NB), cc(C_T0, NB))
                            v.drain()
                            v.tensor_scalar(cc(C_T1, NB), cc(C_SSQ + g0, NB), 1.0 / DM,
                                            None, ALU.mult)
                            v.drain()
                            v.tensor_sub(cc(C_T2, NB), cc(C_T1, NB), cc(C_T2, NB))
                            i_sdv(v.last)
                            for i in range(NB):
                                g = g0 + i
                                if i == 0:
                                    W(v, SEV, EVB0(b) + NB + 1)
                                v.drain()
                                v.tensor_scalar(scr1[:, :], at8(i), cc(C_T0 + i), cc(C_T1 + i),
                                                ALU.subtract, ALU.mult)
                                v.drain()
                                v.scalar_tensor_tensor(xg(g), scr1[:, :], 1.0, xg(g),
                                                       ALU.bypass, ALU.add,
                                                       accum_out=cc(C_SUM + g))
                                i_sdv(v.last)
                            W(v, SEV, EVB0(b) + 2 * NB + 1)
                            v.drain()
                            v.tensor_scalar(cc(C_M3 + g0, NB), cc(C_SUM + g0, NB), 1.0 / DM,
                                            None, ALU.mult)
                            v.drain()
                            v.tensor_mul(cc(C_T2, NB), cc(C_M3 + g0, NB), cc(C_M3 + g0, NB))
                            v.drain()
                            v.tensor_scalar(cc(C_T1, NB), cc(C_SSQ + g0, NB), 1.0 / DM,
                                            None, ALU.mult)
                            v.drain()
                            v.tensor_sub(cc(C_T2, NB), cc(C_T1, NB), cc(C_T2, NB))
                            i_sdv(v.last)
                            for i in range(NB):
                                g = g0 + i
                                if i == 0:
                                    W(v, SEV, EVB0(b) + 2 * NB + 2)
                                if g >= 1:
                                    W(v, STP, TP_C0 + g)
                                v.drain()
                                v.tensor_scalar(x1n_tm[g % 2][:, :], xg(g), cc(C_M3 + g),
                                                cc(C_R3 + g), ALU.subtract, ALU.mult)
                                i_sdv(v.last)
                                W(v, STP, TP_C0 + g + 1)
                                v.drain()
                                v.tensor_copy(
                                    bigA[:, :].rearrange("p (k t) -> p k t", t=n_tok)[:, 0:8, g * 128:(g + 1) * 128],
                                    ptp3[:, :].rearrange("p (k t) -> p k t", t=128))
                                i_sdv(v.last)
                        W(v, SEV, EV_ALL)
                        v.drain()
                        v.reduce_max(gc(0), lg3(logits), axis=AXX)
                        v.drain()
                        v.tensor_sub(lg3(gsm), lg3(logits), gc3(0))
                        i_sdv(v.last)
                        W(v, SEV, EV_ALL + 1)
                        v.drain()
                        v.reduce_sum(gc(1), lg3(gsm2), axis=AXX)
                        v.drain()
                        v.reciprocal(gc(2), gc(1))
                        v.drain()
                        v.tensor_mul(lg3(gsm), lg3(gsm2), gc3(2))
                        v.drain()
                        v.reduce_max(gc(0), lg3(gsm), axis=AXX)
                        v.drain()
                        v.tensor_tensor(lg3(gsm2), lg3(gsm), gc3(0), ALU.is_lt)
                        v.drain()
                        v.tensor_mul(lg3(gsm2), lg3(gsm2), lg3(gsm))
                        v.drain()
                        v.reduce_max(gc(1), lg3(gsm2), axis=AXX)
                        v.drain()
                        v.tensor_tensor(lg3(gsm2), lg3(gsm), gc3(1), ALU.is_ge)
                        v.tensor_add(gc(3), gc(0), gc(1))
                        v.drain()
                        v.tensor_scalar(gc(3), gc(3), 1e-8, None, ALU.add)
                        v.drain()
                        v.reciprocal(gc(4), gc(3))
                        v.drain()
                        v.tensor_mul(lg3(cw_tm), lg3(gsm), lg3(gsm2))
                        v.drain()
                        v.tensor_mul(lg3(cw_tm), lg3(cw_tm), gc3(4))
                        i_sdv(v.last)
                        assert val(SDV) == DV_SMAX + 2

                    @blk.tensor
                    def _(t_raw):
                        t_ = _Trk(t_raw)
                        i_smm(t_.nop())
                        for g in range(NG):
                            W(t_, SDV, DV_X1N(g))
                            if g >= 1:
                                W(t_, SDV, DV_EVC3(g - 1))
                            for f in range(8):
                                t_.transpose(ptp3[:, f * 128:(f + 1) * 128],
                                             x1n_tm[g % 2][:, f * 128:(f + 1) * 128], id_bf[:, :])
                            i_stp(t_.last)
                            W(t_, SDV, DV_EVC3(g))
                            for k in range(8):
                                t_.matmul(pgt[:, 0:NE], bigA[:, k * n_tok + g * 128: k * n_tok + (g + 1) * 128],
                                          gate_sb[:, k * NE:(k + 1) * NE],
                                          start=(k == 0), stop=(k == 7))
                            i_smm(t_.last)
                            if g < NG - 1:
                                W(t_, SEV, EV_LOG(g))
                        W(t_, SDV, DV_SMAX + 2)
                        for pk in range(4):
                            if pk >= 1:
                                W(t_, SEV, EV_ALL + 1 + pk)
                            for g4 in range(4):
                                g = pk * 4 + g4
                                if g >= NG:
                                    continue
                                t_.transpose(pcw[:, g4 * 128:(g4 + 1) * 128],
                                             cw_tm[:, g * NE:(g + 1) * NE], id_f32[:, :])
                            i_smm(t_.last)
                        assert val(SMM) == MM_C0 + 1 + NG + 4

                    @blk.scalar
                    def _(s_raw):
                        s = _Trk(s_raw)
                        load_lnexp(s)
                        i_sev(s.nop())
                        i_sev(s.nop())
                        for b in range(NBAT):
                            g0 = b * NB
                            for i in range(NB):
                                g = g0 + i
                                W(s, SDV, DVB0(b) + i + 1)
                                s.drain()
                                s.activation(pbc[:, 0:DM], at8(i), AF.Square,
                                             accum_out=cc(C_SSQ + g))
                                i_sev(s.last)
                            W(s, SDV, DVB0(b) + NB + 1)
                            s.drain()
                            s.activation(cc(C_T1, NB), cc(C_T2, NB), AF.Ln, bias=zcol[:, 1:2])
                            s.drain()
                            s.activation(cc(C_T1, NB), cc(C_T1, NB), AF.Exp, scale=-0.5)
                            i_sev(s.last)
                            for i in range(NB):
                                g = g0 + i
                                W(s, SDV, DVB0(b) + NB + 1 + i + 1)
                                s.drain()
                                s.activation(pbc[:, 0:DM], xg(g), AF.Square,
                                             accum_out=cc(C_SSQ + g))
                                i_sev(s.last)
                            W(s, SDV, DVB0(b) + 2 * NB + 2)
                            s.drain()
                            s.activation(cc(C_T2, NB), cc(C_T2, NB), AF.Ln, bias=zcol[:, 1:2])
                            s.drain()
                            s.activation(cc(C_R3 + g0, NB), cc(C_T2, NB), AF.Exp, scale=-0.5)
                            i_sev(s.last)
                            for i in range(NB):
                                g = g0 + i
                                W(s, SMM, MM_C0 + 1 + g + 1)
                                s.drain()
                                s.activation(logits[:, g * NE:(g + 1) * NE], pgt[:, 0:NE], AF.Copy)
                                i_sev(s.last)
                        W(s, SDV, DV_SMAX + 1)
                        s.drain()
                        s.activation(gsm2[:, :], gsm[:, :], AF.Exp)
                        i_sev(s.last)
                        for pk in range(4):
                            s.drain()
                            W(s, SMM, MM_C0 + 1 + NG + pk + 1)
                            ng4 = min(4, NG - pk * 4)
                            if ng4 > 0:
                                s.activation(
                                    cwrows[:, :].rearrange("e (g t) -> e g t", t=128)[:, pk * 4:pk * 4 + ng4, :],
                                    pcw[:, 0:ng4 * 128].rearrange("e (g t) -> e g t", t=128),
                                    AF.Copy)
                            i_sev(s.last)
                        assert val(SEV) == EV_ALL + 5

                    @blk.gpsimd
                    def _(g_raw):
                        g = _Trk(g_raw)
                        i_sw1(g.dma_start(out=regW1[:, :].rearrange("p (k n) -> p k n", n=DH),
                                          in_=W1[0].rearrange("(k p) n -> p k n", p=128)), 16)
                        i_sw1(g.dma_start(out=regW2[:, :].rearrange("p (k n) -> p k n", n=DM),
                                          in_=W2[0].rearrange("(k p) n -> p k n", p=128)), 16)

                    @blk.sync
                    def _(s_raw):
                        s = _Trk(s_raw)
                        W(s, SQ[0], SQ_TOT[0])
                        W(s, SQ[1], SQ_TOT[1])
                        if DEBUG_SPILL:
                            s.dma_start(out=dbg_cols[:, :], in_=cols[:, 80:80 + NG]).then_inc(SOUT[0], 16)
                            CNT["sout"] += 16
                            W(s, SOUT, val(SOUT))
                        for g in range(NG):
                            if g >= 2:
                                W(s, SDV, DV_ATTN(g - 2))
                            i_sr[g % 2](s.dma_start(out=qstage[g % 2][:, :],
                                                    in_=q_spill.rearrange("(g p) f -> g p f", p=128)[g]), 16)
                        for g in range(NG):
                            W(s, SDV, DV_X1(g))
                            i_ssp(s.dma_start(out=x1_spill.rearrange("(g p) f -> g p f", p=128)[g],
                                              in_=xg(g)), 16)
                        W(s, SSP, val(SSP))
                        W(s, SR[0], val(SR[0]))
                        W(s, SR[1], val(SR[1]))

        stmoe = ExitStack()
        with stmoe:
            e3 = stmoe.enter_context
            regW1b = e3(nc.sbuf_tensor("regW1b", [128, 16384], BF16))
            regW2b = e3(nc.sbuf_tensor("regW2b", [128, 16384], BF16))
            hbuf = e3(nc.sbuf_tensor("hbuf", [128, 2048], BF16))
            hbuf2 = e3(nc.sbuf_tensor("hbuf2", [128, 2048], BF16))
            cwb_sb = e3(nc.sbuf_tensor("cwb_sb", [128, 256], BF16))

            w1bufs = [regW1, regW1b]
            w2bufs = [regW2, regW2b]

            for half in range(2):
                DV_D0 = val(SDV)
                MM_D0 = val(SMM)
                EV_D0 = val(SEV)
                SW_D0 = [val(SW1), val(SW1B)]

                def MMD(e, tt, j):
                    return MM_D0 + 5 * (e * NTH + tt) + j

                def EVD(e, tt, j):
                    return EV_D0 + 3 * (e * NTH + tt) + j

                def DVD(e, tt, j):
                    return DV_D0 + 4 * (e * NTH + tt) + j

                stf = ExitStack()
                with stf:
                    ef = stf.enter_context
                    with nc.Block() as blk, \
                         nc.psum_tensor(f"ph{half}", [128, 2048], F32) as ph, \
                         nc.psum_tensor(f"pq2{half}", [128, 2048], F32) as pq2:
                        i_smm, i_sev, i_sdv, i_sout, i_sb2 = \
                            (mk_inc(s) for s in (SMM, SEV, SDV, SOUT, SB2))
                        i_sw1p = [mk_inc(SW1), mk_inc(SW1B)]
                        i_sr = [mk_inc(SR[0]), mk_inc(SR[1])]

                        @blk.vector
                        def _(v_raw):
                            v = _Trk(v_raw)
                            for e in range(NE):
                                for tt in range(NTH):
                                    t0 = tt * 256
                                    for hh in range(2):
                                        v.drain()
                                        W(v, SEV, EVD(e, tt, 2 + hh))
                                        src = hbuf if hh == 0 else hbuf2
                                        v.tensor_mul(
                                            src[:, :].rearrange("p (n t) -> p n t", t=256),
                                            src[:, :].rearrange("p (n t) -> p n t", t=256),
                                            cwb_sb[:, :].unsqueeze(1).broadcast_to([128, 8, 256]))
                                        i_sdv(v.last)
                                    for sub in range(2):
                                        v.drain()
                                        W(v, SMM, MMD(e, tt, 5))
                                        gh = tt * 2 + sub
                                        if e == 0:
                                            v.tensor_copy(oacc(gh), pq2[:, sub * 1024:(sub + 1) * 1024])
                                        else:
                                            v.scalar_tensor_tensor(
                                                oacc(gh), pq2[:, sub * 1024:(sub + 1) * 1024],
                                                1.0, oacc(gh), ALU.bypass, ALU.add)
                                        i_sdv(v.last)
                            assert val(SDV) == DV_D0 + 4 * NE * NTH

                        @blk.tensor
                        def _(t_raw):
                            t_ = _Trk(t_raw)
                            for e in range(NE):
                                wb1 = w1bufs[e % 2]
                                wb2 = w2bufs[e % 2]
                                _tgt = SW_D0[e % 2] + 32 * (e // 2 + (1 if e % 2 else 0))
                                W(t_, [SW1, SW1B][e % 2], _tgt)
                                for tt in range(NTH):
                                    t0 = half * (n_tok // 2) + tt * 256
                                    if e > 0 or tt > 0:
                                        W(t_, SEV, EVD(e, tt - 1, 3) if tt > 0 else EVD(e - 1, NTH - 1, 3))
                                    for pp in range(2):
                                        t_.matmul(ph[pp * 64:(pp + 1) * 64, 0:256],
                                                  eye8_sb[0:NE, e * 64:(e + 1) * 64],
                                                  cwrows[0:NE, t0:t0 + 256], start=True, stop=True)
                                    i_smm(t_.last)
                                    for hh in range(2):
                                        W(t_, SEV, EVD(e, tt, 1 + hh))
                                        for n in range(8):
                                            nn = hh * 8 + n
                                            for k in range(8):
                                                t_.matmul(ph[:, n * 256:(n + 1) * 256],
                                                          wb1[:, k * 2048 + nn * 128: k * 2048 + (nn + 1) * 128],
                                                          bigA[:, k * n_tok + t0: k * n_tok + t0 + 256],
                                                          start=(k == 0), stop=(k == 7))
                                        i_smm(t_.last)
                                        W(t_, SDV, DVD(e, tt, 1 + hh))
                                        if hh == 0 and (e > 0 or tt > 0):
                                            prev = DVD(e, tt - 1, 4) if tt > 0 else DVD(e - 1, NTH - 1, 4)
                                            W(t_, SDV, prev)
                                        src = hbuf if hh == 0 else hbuf2
                                        for sub in range(2):
                                            for n2 in range(2):
                                                for k in range(8):
                                                    kk = hh * 8 + k
                                                    t_.matmul(
                                                        pq2[:, sub * 1024 + n2 * 512: sub * 1024 + (n2 + 1) * 512],
                                                        src[:, k * 256 + sub * 128: k * 256 + sub * 128 + 128],
                                                        wb2[:, kk * 1024 + n2 * 512: kk * 1024 + (n2 + 1) * 512],
                                                        start=(kk == 0), stop=(kk == 15))
                                        i_smm(t_.last)
                            assert val(SMM) == MM_D0 + 5 * NE * NTH

                        @blk.scalar
                        def _(s_raw):
                            s = _Trk(s_raw)
                            for e in range(NE):
                                for tt in range(NTH):
                                    W(s, SMM, MMD(e, tt, 1))
                                    if e > 0 or tt > 0:
                                        prev = DVD(e, tt - 1, 2) if tt > 0 else DVD(e - 1, NTH - 1, 2)
                                        W(s, SDV, prev)
                                    s.drain()
                                    i_sev(s.activation(cwb_sb[:, :], ph[:, 0:256], AF.Copy))
                                    for hh in range(2):
                                        W(s, SMM, MMD(e, tt, 2 + 2 * hh))
                                        s.drain()
                                        dst = hbuf if hh == 0 else hbuf2
                                        for n in range(8):
                                            nn = hh * 8 + n
                                            s.activation(dst[:, n * 256:(n + 1) * 256],
                                                         ph[:, n * 256:(n + 1) * 256],
                                                         AF.Relu, bias=b1c[:, nn * 8 + e:nn * 8 + e + 1])
                                        i_sev(s.last)
                            assert val(SEV) == EV_D0 + 3 * NE * NTH

                        @blk.gpsimd
                        def _(g_raw):
                            g = _Trk(g_raw)
                            for e in range(1, NE):
                                buf = e % 2
                                if e >= 2:
                                    W(g, SMM, MMD(e - 2, NTH - 1, 5))
                                i_sw1p[buf](g.dma_start(
                                    out=w1bufs[buf][:, :].rearrange("p (k n) -> p k n", n=DH),
                                    in_=W1[e].rearrange("(k p) n -> p k n", p=128)), 16)
                                i_sw1p[buf](g.dma_start(
                                    out=w2bufs[buf][:, :].rearrange("p (k n) -> p k n", n=DM),
                                    in_=W2[e].rearrange("(k p) n -> p k n", p=128)), 16)
                            if half == 0:
                                W(g, SMM, MMD(6, NTH - 1, 5))
                                i_sw1p[0](g.dma_start(
                                    out=w1bufs[0][:, :].rearrange("p (k n) -> p k n", n=DH),
                                    in_=W1[0].rearrange("(k p) n -> p k n", p=128)), 16)
                                i_sw1p[0](g.dma_start(
                                    out=w2bufs[0][:, :].rearrange("p (k n) -> p k n", n=DM),
                                    in_=W2[0].rearrange("(k p) n -> p k n", p=128)), 16)

                        MM_F0 = val(SMM)
                        EV_F0 = val(SEV)
                        DV_F0 = val(SDV)
                        SSP_TOT = SSP_C0 + 16 * NG
                        ffn8 = regW1b
                        ffn_f32 = regW2b[:, :].bitcast(F32)

                        def ffn_g(gh):
                            return ffn_f32[:, gh * DM:(gh + 1) * DM]

                        def x1r_g(gh):
                            return regW1b[:, (1 + gh) * DM:(2 + gh) * DM]

                        b2s = regW1b[:, (1 + NGH) * DM:(2 + NGH) * DM]
                        SR_F0 = [val(SR[0]), val(SR[1])]

                        @blk.gpsimd
                        def _(g_raw):
                            g = _Trk(g_raw)
                            W(g, SMM, MM_F0)
                            i_sb2(g.dma_start(out=regW1b[0:NE, 0:DM], in_=b2[:, :]), 16)
                        SW_B2 = val(SB2)

                        @blk.sync
                        def _(s_raw):
                            s = _Trk(s_raw)
                            W(s, SSP, SSP_TOT)
                            W(s, SMM, MM_F0)
                            for gh in range(NGH):
                                g = half * NGH + gh
                                i_sr[gh % 2](s.dma_start(
                                    out=x1r_g(gh),
                                    in_=x1_spill.rearrange("(g p) f -> g p f", p=128)[g]), 16)

                        @blk.tensor
                        def _(t_raw):
                            t_ = _Trk(t_raw)
                            for gh in range(NGH):
                                g = half * NGH + gh
                                if gh == 0:
                                    W(t_, SB2, SW_B2)
                                if gh >= 1:
                                    W(t_, SDV, DV_F0 + gh)
                                for n2 in range(2):
                                    t_.matmul(ph[:, n2 * 512:(n2 + 1) * 512],
                                              cwrows[:, g * 128:(g + 1) * 128],
                                              regW1b[0:NE, n2 * 512:(n2 + 1) * 512],
                                              start=True, stop=True)
                                i_smm(t_.last)

                        @blk.vector
                        def _(v_raw):
                            v = _Trk(v_raw)
                            W(v, SR[0], val(SR[0]))
                            W(v, SR[1], val(SR[1]))
                            for gh in range(NGH):
                                g = half * NGH + gh
                                W(v, SMM, MM_F0 + 1 + gh)
                                v.drain()
                                v.tensor_scalar(ffn_g(gh), x1r_g(gh),
                                                cc(C_M3 + g), cc(C_R3 + g),
                                                ALU.subtract, ALU.mult)
                                v.drain()
                                v.tensor_add(ffn_g(gh), ffn_g(gh), oacc(gh))
                                v.drain()
                                v.scalar_tensor_tensor(ffn_g(gh), ph[:, 0:DM],
                                                       1.0, ffn_g(gh),
                                                       ALU.bypass, ALU.add,
                                                       accum_out=cc(C_SUM + gh))
                                i_sdv(v.last)
                            W(v, SEV, EV_F0 + NGH)
                            v.drain()
                            v.tensor_scalar(cc(C_M4, NGH), cc(C_SUM, NGH), 1.0 / DM,
                                            None, ALU.mult)
                            v.drain()
                            v.tensor_mul(cc(C_T2, NGH), cc(C_M4, NGH), cc(C_M4, NGH))
                            v.drain()
                            v.tensor_scalar(cc(C_T1, NGH), cc(C_SSQ, NGH), 1.0 / DM,
                                            None, ALU.mult)
                            v.drain()
                            v.tensor_sub(cc(C_T2, NGH), cc(C_T1, NGH), cc(C_T2, NGH))
                            i_sdv(v.last)
                            for gh in range(NGH):
                                if gh == 0:
                                    W(v, SEV, EV_F0 + NGH + 1)
                                v.drain()
                                v.tensor_scalar(oacc(gh), ffn_g(gh),
                                                cc(C_M4 + gh), cc(C_R4 + gh),
                                                ALU.subtract, ALU.mult)
                                v.drain()
                                v.tensor_add(oacc(gh), oacc(gh), x1r_g(gh))
                                i_sdv(v.last)

                        @blk.scalar
                        def _(s_raw):
                            s = _Trk(s_raw)
                            load_lnexp(s)
                            for gh in range(NGH):
                                W(s, SDV, DV_F0 + gh + 1)
                                s.drain()
                                s.activation(b2s, ffn_g(gh), AF.Square,
                                             accum_out=cc(C_SSQ + gh))
                                i_sev(s.last)
                            W(s, SDV, DV_F0 + NGH + 1)
                            s.drain()
                            s.activation(cc(C_T2, NGH), cc(C_T2, NGH), AF.Ln, bias=zcol[:, 1:2])
                            s.drain()
                            s.activation(cc(C_R4, NGH), cc(C_T2, NGH), AF.Exp, scale=-0.5)
                            i_sev(s.last)

                        @blk.sync
                        def _(s_raw):
                            s = _Trk(s_raw)
                            for gh in range(NGH):
                                g = half * NGH + gh
                                W(s, SDV, DV_F0 + NGH + 1 + gh + 1)
                                i_sout(s.dma_start(
                                    out=y_out.rearrange("(g p) f -> g p f", p=128)[g],
                                    in_=oacc(gh)), 16)
                            W(s, SOUT, val(SOUT))

        return nc


_CACHE = {}


def _get_program(n_tok):
    if n_tok not in _CACHE:
        _CACHE[n_tok] = build_program(n_tok)
    return _CACHE[n_tok]


def kernel(L0_attn_x, L0_attn_W_in, L0_attn_b_in, L0_attn_W_xp, L0_attn_b_xp,
           L0_attn_W_dt, L0_attn_b_dt, L0_attn_W_out, L0_attn_b_out,
           L0_attn_A_log, L0_attn_D, L0_ffn_gate_W, L0_ffn_W1, L0_ffn_b1,
           L0_ffn_W2, L0_ffn_b2):
    from concourse.bass_utils import run_bass_kernel_spmd

    B, S, _ = L0_attn_x.shape
    n_cores = NCORE
    chunks_per_b = n_cores // B
    n_tok = S // chunks_per_b

    nc = _get_program(n_tok)

    xf = np.ascontiguousarray(L0_attn_x.reshape(B * S, DM))
    shared = dict(
        W_in=np.ascontiguousarray(L0_attn_W_in),
        b_in=np.ascontiguousarray(L0_attn_b_in).reshape(1, -1),
        W_xp=np.ascontiguousarray(L0_attn_W_xp),
        b_xp=np.ascontiguousarray(L0_attn_b_xp).reshape(1, -1),
        W_dt=np.ascontiguousarray(L0_attn_W_dt),
        b_dt=np.ascontiguousarray(L0_attn_b_dt).reshape(1, -1),
        W_out=np.ascontiguousarray(L0_attn_W_out),
        b_out=np.ascontiguousarray(L0_attn_b_out).reshape(1, -1),
        A_log=np.ascontiguousarray(L0_attn_A_log).reshape(1, -1),
        D_vec=np.ascontiguousarray(L0_attn_D).reshape(1, -1),
        gate_W=np.ascontiguousarray(L0_ffn_gate_W),
        W1=np.ascontiguousarray(L0_ffn_W1),
        b1=np.ascontiguousarray(L0_ffn_b1),
        W2=np.ascontiguousarray(L0_ffn_W2),
        b2=np.ascontiguousarray(L0_ffn_b2),
        ident=np.eye(128, dtype=np.float32),
        eye8_in=np.kron(np.eye(NE, dtype=np.float32), np.ones((1, 64), dtype=np.float32)).astype(np.float32),
        binc_in=np.ascontiguousarray(np.asarray(L0_attn_b_in, dtype=np.float32).reshape(32, 128).T),
        bdtc_in=np.ascontiguousarray(np.asarray(L0_attn_b_dt, dtype=np.float32).reshape(8, 128).T),
        bxpc_in=np.ascontiguousarray(np.asarray(L0_attn_b_xp, dtype=np.float32).reshape(64, 1)),
        alog2_in=np.ascontiguousarray(np.concatenate([np.asarray(L0_attn_A_log, dtype=np.float32).reshape(64),
                                                      np.asarray(L0_attn_A_log, dtype=np.float32).reshape(64)]).reshape(128, 1)),
        b1c_in=np.ascontiguousarray(np.asarray(L0_ffn_b1, dtype=np.float32).reshape(8, 16, 128).transpose(2, 1, 0).reshape(128, 128)),
        bob_in=np.ascontiguousarray(np.tile(np.asarray(L0_attn_b_out, dtype=np.float32).reshape(1, 1024), (128, 1))),
        dvb_in=np.ascontiguousarray(np.tile(np.asarray(L0_attn_D, dtype=np.float32).reshape(1, 1024), (128, 1))),
        ones_in=np.ones((128, 128), dtype=np.float32),
        zeros_in=np.stack([np.zeros(128, np.float32), np.full(128, EPS, np.float32), np.ones(128, np.float32), np.full(128, -0.5, np.float32)], axis=1),
    )
    in_maps = []
    for c in range(n_cores):
        b = c // chunks_per_b
        ch = c % chunks_per_b
        mask = np.zeros((DS, NCORE), dtype=np.float32)
        for r in range(b * chunks_per_b, b * chunks_per_b + ch):
            mask[:, r] = 1.0
        im = dict(shared)
        im["x_in"] = np.ascontiguousarray(
            xf[(b * chunks_per_b + ch) * n_tok:(b * chunks_per_b + ch + 1) * n_tok])
        im["chain_in"] = mask
        in_maps.append(im)

    res = run_bass_kernel_spmd(nc, in_maps, core_ids=list(range(n_cores)))
    out = np.concatenate([res.results[c]["y_out"] for c in range(n_cores)], axis=0)
    return out.reshape(B, S, DM)


# version 14
# speedup vs baseline: 1.2997x; 1.2997x over previous
import os

import numpy as np
from contextlib import ExitStack

import concourse.bass as bass
import concourse.mybir as mybir

F32 = mybir.dt.float32
BF16 = mybir.dt.bfloat16
AF = mybir.ActivationFunctionType
ALU = mybir.AluOpType
AXX = mybir.AxisListType.X

DM = 1024
DI = 2048
DS = 64
NE = 8
DH = 2048
NCORE = 8
EPS = 1e-5


class _Trk:

    def __init__(self, eng):
        self._e = eng
        self.last = None

    def __getattr__(self, n):
        f = getattr(self._e, n)
        if callable(f):
            def w(*a, **k):
                r = f(*a, **k)
                if hasattr(r, "then_inc"):
                    self.last = r
                return r
            return w
        return f


DEBUG_SPILL = False


def build_program(n_tok):
    NG = n_tok // 128
    NT = n_tok // 256
    SEG = n_tok // 2
    NGH = NG // 2
    NTH = NT // 2
    assert n_tok % 512 == 0 and NG <= 16

    nc = bass.Bass("TRN2", target_bir_lowering=False)

    x_in = nc.dram_tensor("x_in", [n_tok, DM], F32, kind="ExternalInput").ap()
    W_in = nc.dram_tensor("W_in", [DM, 2 * DI], F32, kind="ExternalInput").ap()
    b_in = nc.dram_tensor("b_in", [1, 2 * DI], F32, kind="ExternalInput").ap()
    W_xp = nc.dram_tensor("W_xp", [DI, DS], F32, kind="ExternalInput").ap()
    b_xp = nc.dram_tensor("b_xp", [1, DS], F32, kind="ExternalInput").ap()
    W_dt = nc.dram_tensor("W_dt", [DI, DM], F32, kind="ExternalInput").ap()
    b_dt = nc.dram_tensor("b_dt", [1, DM], F32, kind="ExternalInput").ap()
    W_out = nc.dram_tensor("W_out", [DI, DM], F32, kind="ExternalInput").ap()
    b_out = nc.dram_tensor("b_out", [1, DM], F32, kind="ExternalInput").ap()
    A_log = nc.dram_tensor("A_log", [1, DS], F32, kind="ExternalInput").ap()
    D_vec = nc.dram_tensor("D_vec", [1, DM], F32, kind="ExternalInput").ap()
    gate_W = nc.dram_tensor("gate_W", [DM, NE], F32, kind="ExternalInput").ap()
    W1 = nc.dram_tensor("W1", [NE, DM, DH], F32, kind="ExternalInput").ap()
    b1 = nc.dram_tensor("b1", [NE, DH], F32, kind="ExternalInput").ap()
    W2 = nc.dram_tensor("W2", [NE, DH, DM], F32, kind="ExternalInput").ap()
    b2 = nc.dram_tensor("b2", [NE, DM], F32, kind="ExternalInput").ap()
    ident = nc.dram_tensor("ident", [128, 128], F32, kind="ExternalInput").ap()
    ones_in = nc.dram_tensor("ones_in", [128, 128], F32, kind="ExternalInput").ap()
    zeros_in = nc.dram_tensor("zeros_in", [128, 4], F32, kind="ExternalInput").ap()
    chain_in = nc.dram_tensor("chain_in", [DS, NCORE], F32, kind="ExternalInput").ap()
    eye8_in = nc.dram_tensor("eye8_in", [NE, NE * 64], F32, kind="ExternalInput").ap()
    binc_in = nc.dram_tensor("binc_in", [128, 32], F32, kind="ExternalInput").ap()
    bdtc_in = nc.dram_tensor("bdtc_in", [128, 8], F32, kind="ExternalInput").ap()
    bxpc_in = nc.dram_tensor("bxpc_in", [64, 1], F32, kind="ExternalInput").ap()
    alog2_in = nc.dram_tensor("alog2_in", [128, 1], F32, kind="ExternalInput").ap()
    b1c_in = nc.dram_tensor("b1c_in", [128, NE * 16], F32, kind="ExternalInput").ap()
    bob_in = nc.dram_tensor("bob_in", [128, DM], F32, kind="ExternalInput").ap()
    dvb_in = nc.dram_tensor("dvb_in", [128, DM], F32, kind="ExternalInput").ap()
    y_out = nc.dram_tensor("y_out", [n_tok, DM], F32, kind="ExternalOutput").ap()

    q_spill = nc.dram_tensor("q_spill", [n_tok, DM], BF16).ap()
    x1_spill = nc.dram_tensor("x1_spill", [n_tok, DM], BF16).ap()
    cc_in_t = nc.dram_tensor("cc_in", [1, 2 * DS], F32)
    cc_out_t = nc.dram_tensor("cc_out", [NCORE, 2 * DS], F32)
    if DEBUG_SPILL:
        dbg_xi = nc.dram_tensor("dbg_xi", [128, 16 * n_tok], BF16, kind="ExternalOutput").ap()
        dbg_md = nc.dram_tensor("dbg_md", [1, n_tok], BF16, kind="ExternalOutput").ap()
        dbg_bx = nc.dram_tensor("dbg_bx", [128, n_tok // 2], F32, kind="ExternalOutput").ap()
        dbg_a = nc.dram_tensor("dbg_a", [128, n_tok // 2], F32, kind="ExternalOutput").ap()
        dbg_sfin = nc.dram_tensor("dbg_sfin", [128, n_tok // 2], F32, kind="ExternalOutput").ap()
        dbg_cols = nc.dram_tensor("dbg_cols", [128, n_tok // 128], F32, kind="ExternalOutput").ap()
        dbg_sres = nc.dram_tensor("dbg_sres", [128, 2048], BF16, kind="ExternalOutput").ap()
        dbg_xtm = nc.dram_tensor("dbg_xtm", [128, 2048], BF16, kind="ExternalOutput").ap()
        dbg_c2 = nc.dram_tensor("dbg_c2", [128, 64], F32, kind="ExternalOutput").ap()
        dbg_w = nc.dram_tensor("dbg_w", [128, 2048], BF16, kind="ExternalOutput").ap()
        dbg_xnfm = nc.dram_tensor("dbg_xnfm", [128, 2048], BF16, kind="ExternalOutput").ap()

    CNT = {}

    def W(eng, sem, val):
        eng.wait_ge(sem[0], val)

    def mk_inc(sem):
        def inc(ins, n=1):
            CNT[sem[1]] += n
            ins.then_inc(sem[0], n)
            return CNT[sem[1]]
        return inc

    st0 = ExitStack()
    with st0:
        e0 = st0.enter_context

        def sem(name):
            h = e0(nc.semaphore(name))
            CNT[name] = 0
            return (h, name)

        SX = sem("sx")
        SWA = sem("swa")
        SWB = sem("swb")
        SW1 = sem("sw1")
        SLD = sem("sld")
        SMM = sem("smm")
        STP = sem("stp")
        SMD = sem("smd")
        SEV = sem("sev")
        SDV = sem("sdv")
        SCC = sem("scc")
        SQ = [sem("sq0"), sem("sq1")]
        SR = [sem("sr0"), sem("sr1")]
        SSP = sem("ssp")
        SW1B = sem("sw1b")
        SB2 = sem("sb2")
        SCD = sem("scd")
        SXX = sem("sxx")
        SWDT = sem("swdt")
        SWXP = sem("swxp")
        SWO = sem("swo")
        SOUT = sem("sout")

        def val(s):
            return CNT[s[1]]

        bigA = e0(nc.sbuf_tensor("bigA", [128, 16 * n_tok], BF16))
        regW1 = e0(nc.sbuf_tensor("regW1", [128, 16384], BF16))
        regW2 = e0(nc.sbuf_tensor("regW2", [128, 16384], BF16))
        eye8_sb = e0(nc.sbuf_tensor("eye8_sb", [NE, NE * 64], BF16))
        zcol = e0(nc.sbuf_tensor("zcol", [128, 4], F32))
        cols = e0(nc.sbuf_tensor("cols", [128, 256], F32))
        b1c = e0(nc.sbuf_tensor("b1c", [128, NE * 16], F32))
        cwrows = e0(nc.sbuf_tensor("cwrows", [NE, n_tok], BF16))
        cw_tm = e0(nc.sbuf_tensor("cw_tm", [128, (n_tok // 128) * NE], F32))

        C_BIN, C_BDT, C_BXP, C_A2, C_S0 = 0, 32, 40, 41, 42
        C_M1, C_R1, C_SC = 48, 64, 80
        C_M3, C_R3, C_M4, C_R4 = 96, 112, 128, 144
        C_SUM, C_SSQ, C_T0, C_T1, C_T2 = 160, 176, 192, 208, 224

        def cc(i, n=1):
            return cols[:, i:i + n]

        from concourse.hw_specs import get_activation_tables
        _lnexp_id = None
        for _idx, _funcs in enumerate(get_activation_tables(nc.m.arch).values()):
            if AF.Ln in _funcs and AF.Exp in _funcs:
                _lnexp_id = _idx
                break
        assert _lnexp_id is not None

        def load_lnexp(seng):
            seng.add_instruction(mybir.InstLoadActFuncSet(
                name=nc.get_next_instruction_name(),
                act_func_set_id=_lnexp_id,
                ins=[], outs=[]))

        def xi(k, t0, nt):
            return bigA[:, k * n_tok + t0: k * n_tok + t0 + nt]

        def xi3(t0, nt):
            return bigA[:, :].rearrange("p (k t) -> p k t", t=n_tok)[:, :, t0:t0 + nt]

        oacc_all = bigA[:, 8 * n_tok: 16 * n_tok].bitcast(F32)

        def oacc(gh):
            return oacc_all[:, gh * DM:(gh + 1) * DM]

        stm = ExitStack()
        with stm:
            em = stm.enter_context
            x_tm = em(nc.sbuf_tensor("x_tm", [128, NG * DM], BF16))
            gate_sb = em(nc.sbuf_tensor("gate_sb", [128, 8 * NE], BF16))
            id_bf = em(nc.sbuf_tensor("id_bf", [128, 128], BF16))
            id_f32 = em(nc.sbuf_tensor("id_f32", [128, 128], F32))
            ones_f = em(nc.sbuf_tensor("ones_f", [128, 128], F32))
            ones_b = em(nc.sbuf_tensor("ones_b", [128, 128], BF16))
            bob_bc = em(nc.sbuf_tensor("bob_bc", [128, DM], BF16))
            dvb_bc = em(nc.sbuf_tensor("dvb_bc", [128, DM], BF16))
            cmask = em(nc.sbuf_tensor("cmask", [64, NCORE], F32))
            stab = ExitStack()
            eab = stab.enter_context
            stb = eab(nc.sbuf_tensor("stb", [128, SEG], F32))
            mdrow = eab(nc.sbuf_tensor("mdrow", [1, n_tok], BF16))
            fdrow = eab(nc.sbuf_tensor("fdrow", [1, 2 * DS], F32))

            def xg(g):
                return x_tm[:, g * DM:(g + 1) * DM]

            sti = ExitStack()
            with sti:
                ei = sti.enter_context
                scr0 = ei(nc.sbuf_tensor("scr0", [128, DM], BF16))
                tcol = ei(nc.sbuf_tensor("tcol", [128, 1], F32))

                with nc.Block() as blk:
                    i_sx, i_sld, i_sev, i_sdv = (mk_inc(s) for s in (SX, SLD, SEV, SDV))

                    @blk.gpsimd
                    def _(g_raw):
                        g = _Trk(g_raw)
                        ins_x = g.dma_start(out=x_tm[:, :].rearrange("p (g f) -> p g f", f=DM),
                                            in_=x_in.rearrange("(g p) f -> p g f", p=128))
                        ins_x.then_inc(SXX[0], 16)
                        CNT["sxx"] += 16
                        i_sx(g.dma_start(out=id_bf[:, :], in_=ident[:, :]), 16)
                        i_sx(g.dma_start(out=ones_b[:, :], in_=ones_in[:, :]), 16)
                        i_sx(g.dma_start(out=gate_sb[:, :].rearrange("p (k j) -> p k j", j=NE),
                                         in_=gate_W.rearrange("(k p) j -> p k j", p=128)), 16)
                        i_sx(g.dma_start(out=eye8_sb[:, :], in_=eye8_in[:, :]), 16)
                        i_sx(g.dma_start(out=bob_bc[:, :], in_=bob_in[:, :]), 16)
                        i_sx(g.dma_start(out=dvb_bc[:, :], in_=dvb_in[:, :]), 16)

                    @blk.sync
                    def _(s_raw):
                        s = _Trk(s_raw)
                        i_sld(s.dma_start(out=id_f32[:, :], in_=ident[:, :]), 16)
                        i_sld(s.dma_start(out=ones_f[:, :], in_=ones_in[:, :]), 16)
                        i_sld(s.dma_start(out=zcol[:, :], in_=zeros_in[:, :]), 16)
                        i_sld(s.dma_start(out=cc(C_BIN, 32), in_=binc_in[:, :]), 16)
                        i_sld(s.dma_start(out=cc(C_BDT, 8), in_=bdtc_in[:, :]), 16)
                        i_sld(s.dma_start(out=cols[0:64, C_BXP:C_BXP + 1], in_=bxpc_in[:, :]), 16)
                        i_sld(s.dma_start(out=tcol[:, :], in_=alog2_in[:, :]), 16)
                        i_sld(s.dma_start(out=b1c[:, :], in_=b1c_in[:, :]), 16)
                        i_sld(s.dma_start(out=cmask[:, :], in_=chain_in[:, :]), 16)
                    LD_ALL = val(SLD)

                    @blk.scalar
                    def _(s_raw):
                        s = _Trk(s_raw)
                        load_lnexp(s)
                        W(s, SLD, LD_ALL)
                        i_sev(s.activation(tcol[:, :], tcol[:, :], AF.Exp))
                        W(s, SXX, 16)
                        for g in range(NG):
                            s.drain()
                            s.activation(scr0[:, :], xg(g), AF.Square,
                                         accum_out=cc(C_SSQ + g))
                        i_sev(s.last)
                    EV_I = val(SEV)

                    @blk.vector
                    def _(v_raw):
                        v = _Trk(v_raw)
                        W(v, SEV, EV_I - 1)
                        v.drain()
                        v.tensor_scalar(cc(C_A2), tcol[:, 0:1], -1.0 / DM, None, ALU.mult)
                        W(v, SXX, 16)
                        for g in range(NG):
                            v.reduce_sum(cc(C_SUM + g), xg(g), axis=AXX)
                        v.drain()
                        W(v, SEV, EV_I)
                        v.tensor_scalar(cc(C_M1, NG), cc(C_SUM, NG), 1.0 / DM, None, ALU.mult)
                        v.drain()
                        v.tensor_mul(cc(C_T0, NG), cc(C_M1, NG), cc(C_M1, NG))
                        v.tensor_scalar(cc(C_T1, NG), cc(C_SSQ, NG), 1.0 / DM, None, ALU.mult)
                        v.drain()
                        v.tensor_sub(cc(C_T2, NG), cc(C_T1, NG), cc(C_T0, NG))
                        i_sdv(v.last)
                    DV_I = val(SDV)

                    @blk.scalar
                    def _(s_raw):
                        s = _Trk(s_raw)
                        load_lnexp(s)
                        W(s, SDV, DV_I)
                        s.activation(cc(C_T0, NG), cc(C_T2, NG), AF.Ln, bias=zcol[:, 1:2])
                        s.drain()
                        s.activation(cc(C_R1, NG), cc(C_T0, NG), AF.Exp, scale=-0.5)
                        i_sev(s.last)
            SX_INIT = val(SX)
            SLD_INIT = val(SLD)

            DV_A0 = val(SDV)
            TP_A0 = val(STP)
            MM_A0 = val(SMM)
            EV_A0 = val(SEV)

            def DVA(t, j):
                return DV_A0 + 2 * t + j

            def TPA(t):
                return TP_A0 + t + 1

            def MMA(t, j):
                return MM_A0 + 13 * t + j

            def EVA(t, j):
                return EV_A0 + 15 * t + j

            stpa = ExitStack()
            with stpa:
                ea = stpa.enter_context
                wxp_sb = ea(nc.sbuf_tensor("wxp_sb", [128, 16 * DS], BF16))
                xn_tm = [ea(nc.sbuf_tensor(f"xn_tm{i}", [128, 2048], BF16)) for i in range(2)]
                xn_fm = [ea(nc.sbuf_tensor(f"xn_fm{i}", [128, 2048], BF16)) for i in range(2)]
                ez_sc = ea(nc.sbuf_tensor("ez_sc", [128, 1024], BF16))
                sp_sc = ea(nc.sbuf_tensor("sp_sc", [128, 2048], BF16))

                with nc.Block() as blk, \
                     nc.psum_tensor("ptp", [128, 2048], BF16) as ptp, \
                     nc.psum_tensor("pxi", [128, 1024], F32) as pxi, \
                     nc.psum_tensor("pdl", [128, 1024], F32) as pdl, \
                     nc.psum_tensor("pmd", [1, 256], F32) as pmd, \
                     nc.psum_tensor("pbx", [64, 256], F32) as pbx:
                    i_swa, i_swb, i_smm, i_stp, i_smd, i_sev, i_sdv = \
                        (mk_inc(s) for s in (SWA, SWB, SMM, STP, SMD, SEV, SDV))

                    @blk.vector
                    def _(v_raw):
                        v = _Trk(v_raw)
                        W(v, SX, SX_INIT)
                        W(v, SLD, SLD_INIT)
                        for t in range(NT):
                            v.drain()
                            if t >= 2:
                                W(v, STP, TPA(t - 2))
                            for g2 in range(2):
                                g = 2 * t + g2
                                v.tensor_scalar(xn_tm[t % 2][:, g2 * 1024:(g2 + 1) * 1024],
                                                xg(g), cc(C_M1 + g), cc(C_R1 + g),
                                                ALU.subtract, ALU.mult)
                            i_sdv(v.last)
                            W(v, STP, TPA(t))
                            if t >= 2:
                                W(v, SMM, MMA(t - 2, 13))
                            v.drain()
                            v.tensor_copy(
                                xn_fm[t % 2][:, :].rearrange("p (f g2 tok) -> p f g2 tok",
                                                             g2=2, tok=128),
                                ptp[:, :].rearrange("p (g2 f tok) -> p g2 f tok",
                                                    f=8, tok=128).transpose([0, 2, 1, 3]))
                            i_sdv(v.last)
                        assert val(SDV) == DVA(NT - 1, 2)

                    @blk.tensor
                    def _(t_raw):
                        t_ = _Trk(t_raw)
                        W(t_, SX, SX_INIT)
                        W(t_, SWA, 16)
                        for t in range(NT):
                            if t == 0:
                                pass
                            W(t_, SDV, DVA(t, 1))
                            if t >= 1:
                                W(t_, SDV, DVA(t - 1, 2))
                            for g2 in range(2):
                                for f in range(8):
                                    t_.transpose(ptp[:, g2 * 1024 + f * 128:g2 * 1024 + (f + 1) * 128],
                                                 xn_tm[t % 2][:, g2 * 1024 + f * 128:g2 * 1024 + (f + 1) * 128],
                                                 id_bf[:, :])
                            i_stp(t_.last)
                            W(t_, SDV, DVA(t, 2))
                            for np_ in range(8):
                                bank = np_ % 2
                                if np_ >= 2:
                                    W(t_, SEV, EVA(t, np_ - 1))
                                for n2 in range(2):
                                    n = np_ * 2 + n2
                                    for k in range(8):
                                        t_.matmul(pxi[:, bank * 512 + n2 * 256: bank * 512 + n2 * 256 + 256],
                                                  regW1[:, k * 2048 + n * 128: k * 2048 + (n + 1) * 128],
                                                  xn_fm[t % 2][:, k * 256:(k + 1) * 256],
                                                  start=(k == 0), stop=(k == 7))
                                i_smm(t_.last)
                            W(t_, SEV, EVA(t, 8))
                            if t == 0:
                                W(t_, SWDT, 16)
                            for f in range(4):
                                bank = f % 2
                                if f >= 2:
                                    W(t_, SEV, EVA(t, 8 + f - 1))
                                for n2 in range(2):
                                    nn = f * 2 + n2
                                    for k in range(16):
                                        t_.matmul(pdl[:, bank * 512 + n2 * 256: bank * 512 + n2 * 256 + 256],
                                                  regW2[:, k * 1024 + nn * 128: k * 1024 + (nn + 1) * 128],
                                                  xi(k, t * 256, 256),
                                                  start=(k == 0), stop=(k == 15))
                                i_smm(t_.last)
                            if t >= 1:
                                W(t_, SEV, EVA(t - 1, 15))
                            W(t_, SEV, EVA(t, 11))
                            for ch in range(4):
                                t_.matmul(pmd[:, 0:256], ones_b[0:128, 0:1],
                                          sp_sc[:, ch * 256:(ch + 1) * 256],
                                          start=(ch == 0), stop=False)
                            W(t_, SEV, EVA(t, 14))
                            for ch in range(4, 8):
                                t_.matmul(pmd[:, 0:256], ones_b[0:128, 0:1],
                                          sp_sc[:, ch * 256:(ch + 1) * 256],
                                          start=False, stop=(ch == 7))
                            i_smd(t_.last)
                            if t == 0:
                                W(t_, SWXP, 16)
                            for k in range(16):
                                t_.matmul(pbx[:, 0:256], wxp_sb[:, k * DS:(k + 1) * DS],
                                          xi(k, t * 256, 256), start=(k == 0), stop=(k == 15))
                            i_smm(t_.last)
                        assert val(SMM) == MMA(NT - 1, 13)

                    @blk.scalar
                    def _(s_raw):
                        s = _Trk(s_raw)
                        for t in range(NT):
                            s.drain()
                            s.preload_act_table(AF.Silu)
                            for np_ in range(8):
                                bank = np_ % 2
                                W(s, SMM, MMA(t, np_ + 1))
                                for n2 in range(2):
                                    n = np_ * 2 + n2
                                    s.activation(xi(n, t * 256, 256),
                                                 pxi[:, bank * 512 + n2 * 256: bank * 512 + n2 * 256 + 256],
                                                 AF.Silu, bias=cc(C_BIN + n))
                                i_sev(s.last)
                            load_lnexp(s)
                            for half in range(2):
                                s.drain()
                                for f2 in range(2):
                                    f = half * 2 + f2
                                    bank = f % 2
                                    W(s, SMM, MMA(t, 8 + f + 1))
                                    for n2 in range(2):
                                        nn = f * 2 + n2
                                        s.activation(ez_sc[:, f2 * 512 + n2 * 256: f2 * 512 + n2 * 256 + 256],
                                                     pdl[:, bank * 512 + n2 * 256: bank * 512 + n2 * 256 + 256],
                                                     AF.Exp, bias=cc(C_BDT + nn))
                                    i_sev(s.last)
                                s.drain()
                                i_sev(s.activation(sp_sc[:, half * 1024:(half + 1) * 1024],
                                                   ez_sc[:, :], AF.Ln, bias=zcol[:, 2:3]))
                            W(s, SMD, t + 1)
                            s.activation(mdrow[0:1, t * 256:(t + 1) * 256], pmd[:, 0:256], AF.Copy)
                            W(s, SMM, MMA(t, 13))
                            seg = (t * 256) // SEG
                            off = (t * 256) % SEG
                            s.activation(stb[seg * 64:(seg + 1) * 64, off:off + 256], pbx[:, 0:256],
                                         AF.Identity, bias=cols[0:64, C_BXP:C_BXP + 1])
                            i_sev(s.last)
                        assert val(SEV) == EVA(NT - 1, 15)

                    @blk.gpsimd
                    def _(g_raw):
                        g = _Trk(g_raw)
                        i_swa(g.dma_start(
                            out=regW1[:, :].rearrange("p (k n) -> p k n", n=2048),
                            in_=W_in.rearrange("(k p) n -> p k n", p=128)[:, :, 0:DI]), 16)
                        g.dma_start(
                            out=regW2[:, :].rearrange("p (k n) -> p k n", n=1024),
                            in_=W_dt.rearrange("(k p) n -> p k n", p=128)).then_inc(SWDT[0], 16)
                        CNT["swdt"] += 16
                        g.dma_start(
                            out=wxp_sb[:, :].rearrange("p (k n) -> p k n", n=DS),
                            in_=W_xp.rearrange("(k p) n -> p k n", p=128)).then_inc(SWXP[0], 16)
                        CNT["swxp"] += 16
                        W(g, SMM, MMA(NT - 1, 8))
                        i_swb(g.dma_start(
                            out=regW1[:, :].rearrange("p (k n) -> p k n", n=2048),
                            in_=W_in.rearrange("(k p) n -> p k n", p=128)[:, :, DI:2 * DI]), 16)
                        W(g, SMM, MMA(NT - 1, 12))
                        g.dma_start(
                            out=regW2[:, :].rearrange("p (k n) -> p k n", n=1024),
                            in_=W_out.rearrange("(k p) n -> p k n", p=128)).then_inc(SWO[0], 16)
                        CNT["swo"] += 16

            DV_B0 = val(SDV)
            TP_B0 = val(STP)
            MM_B0 = val(SMM)
            EV_B0 = val(SEV)

            def DVB(g, j):
                return DV_B0 + 4 + 4 * g + j

            def MMB(g, j):
                return MM_B0 + 1 + 5 * g + (1 if g >= 1 else 0) + j

            def EVB(g, j):
                return EV_B0 + 1 + 5 * g + (1 if g >= 1 else 0) + j

            CC_TAILS = val(SCC) + 64
            CC_CIN = CC_TAILS + 16
            CC_GATH = CC_CIN + 16 * 2 * NCORE
            CC_S0 = CC_GATH + 32

            sta2 = ExitStack()
            with sta2:
                e2 = sta2.enter_context
                a_sc = e2(nc.sbuf_tensor("a_sc", [128, SEG], F32))
                stc = e2(nc.sbuf_tensor("stc", [128, SEG], F32))
                _xn2tm = e2(nc.sbuf_tensor("xn2tm", [128, 1024], BF16))
                _xn2fm = e2(nc.sbuf_tensor("xn2fm", [128, 1024], BF16))
                xn2_tm = [_xn2tm, _xn2tm]
                xn2_fm = [_xn2fm, _xn2fm]
                sres = e2(nc.sbuf_tensor("sres", [128, 2048], BF16))
                p_sc = e2(nc.sbuf_tensor("p_sc", [128, 2048], BF16))
                q_sb = [e2(nc.sbuf_tensor(f"q_sb{i}", [128, DM], BF16)) for i in range(2)]
                tails = e2(nc.sbuf_tensor("tails", [64, 4], F32))
                fdcol = e2(nc.sbuf_tensor("fdcol", [128, 1], F32))
                dgt = e2(nc.sbuf_tensor("dgt", [64, NCORE], F32))
                fgt = e2(nc.sbuf_tensor("fgt", [64, NCORE], F32))
                dgp = e2(nc.sbuf_tensor("dgp", [64, NCORE], F32))
                fgp = e2(nc.sbuf_tensor("fgp", [64, NCORE], F32))
                srun = e2(nc.sbuf_tensor("srun", [64, 3], F32))

                with nc.Block() as blk, \
                     nc.psum_tensor("pmdb", [128, min(SEG, 1024)], F32) as pmdb, \
                     nc.psum_tensor("psc", [128, 512], F32) as psc, \
                     nc.psum_tensor("ptp2", [128, 1024], BF16) as ptp2, \
                     nc.psum_tensor("pres", [128, 1024], F32) as pres, \
                     nc.psum_tensor("pq", [128, 1024], F32) as pq:
                    i_smm, i_stp, i_sev, i_sdv, i_scc = \
                        (mk_inc(s) for s in (SMM, STP, SEV, SDV, SCC))
                    i_sq = [mk_inc(SQ[0]), mk_inc(SQ[1])]

                    @blk.vector
                    def _(v_raw):
                        v = _Trk(v_raw)
                        W(v, SMM, MM_B0 + 1)
                        v.tensor_scalar(stc[:, :], pmdb[:, 0:SEG], cc(C_A2), None, ALU.mult)
                        i_sdv(v.last)
                        W(v, SEV, EV_B0 + 1)
                        v.drain()
                        if DEBUG_SPILL:
                            W(v, SOUT, val(SOUT) + 144)
                        v.tensor_tensor_scan(stc[:, :], a_sc[:, :], stb[:, :], 0.0,
                                             ALU.mult, ALU.add)
                        i_sdv(v.last)
                        v.drain()
                        v.tensor_tensor_scan(stb[:, :], a_sc[:, :], a_sc[:, :], 1.0,
                                             ALU.mult, ALU.bypass)
                        i_sdv(v.last)
                        W(v, SCC, CC_TAILS)
                        v.tensor_mul(fdcol[0:64, 0:1], tails[:, 0:1], tails[:, 1:2])
                        v.tensor_mul(srun[:, 0:1], tails[:, 1:2], tails[:, 2:3])
                        v.drain()
                        v.tensor_add(fdcol[64:128, 0:1], tails[:, 3:4], srun[:, 0:1])
                        i_sdv(v.last)
                        for g in range(NG):
                            v.drain()
                            if g >= 1:
                                W(v, STP, TP_B0 + g)
                            v.tensor_scalar(xn2_tm[g % 2][:, :], xg(g), cc(C_M1 + g),
                                            cc(C_R1 + g), ALU.subtract, ALU.mult)
                            i_sdv(v.last)
                            W(v, STP, TP_B0 + g + 1)
                            if g >= 1:
                                W(v, SMM, MMB(g - 1, 4))
                            v.drain()
                            v.tensor_copy(xn2_fm[g % 2][:, :], ptp2[:, :])
                            i_sdv(v.last)
                            if g >= 1:
                                W(v, SMM, MMB(g - 1, 5))
                            for ph2 in range(2):
                                W(v, SEV, EVB(g, 2 * ph2 + 2))
                                v.drain()
                                v.tensor_mul(
                                    p_sc[:, :].rearrange("p (k t) -> p k t", t=128)[:, ph2 * 8:(ph2 + 1) * 8, :],
                                    xi3(g * 128, 128)[:, ph2 * 8:(ph2 + 1) * 8, :],
                                    sres[:, :].rearrange("p (k t) -> p k t", t=128)[:, ph2 * 8:(ph2 + 1) * 8, :])
                                i_sdv(v.last)
                        W(v, SCC, CC_GATH)
                        v.tensor_mul(dgp[:, :], dgt[:, :], cmask[:, :])
                        v.tensor_mul(fgp[:, :], fgt[:, :], cmask[:, :])
                        v.drain()
                        v.tensor_sub(dgp[:, :], dgp[:, :], cmask[:, :])
                        v.drain()
                        v.tensor_scalar(dgp[:, :], dgp[:, :], 1.0, None, ALU.add)
                        v.tensor_copy(srun[:, 0:1], zcol[0:64, 0:1])
                        v.drain()
                        cur = 0
                        for r in range(NCORE):
                            v.tensor_mul(srun[:, 2:3], dgp[:, r:r + 1], srun[:, cur:cur + 1])
                            v.drain()
                            v.tensor_add(srun[:, 1 - cur:2 - cur], srun[:, 2:3], fgp[:, r:r + 1])
                            v.drain()
                            cur = 1 - cur
                        assert cur == 0
                        v.tensor_mul(srun[:, 2:3], tails[:, 0:1], srun[:, 0:1])
                        v.drain()
                        v.tensor_add(srun[:, 1:2], srun[:, 2:3], tails[:, 2:3])
                        i_sdv(v.last)
                        W(v, SCC, CC_S0)
                        v.drain()
                        v.scalar_tensor_tensor(a_sc[:, :], stb[:, :], cc(C_S0),
                                               stc[:, :], ALU.mult, ALU.add)
                        i_sdv(v.last)
                        assert val(SDV) == DV_B0 + 4 + 4 * NG + 2

                    @blk.tensor
                    def _(t_raw):
                        t_ = _Trk(t_raw)
                        for seg in range(2):
                            for h in range((SEG + 511) // 512):
                                w_ = min(512, SEG - h * 512)
                                t_.matmul(pmdb[seg * 64:(seg + 1) * 64, h * 512:h * 512 + w_],
                                          ones_b[0:1, 0:64],
                                          mdrow[0:1, seg * SEG + h * 512: seg * SEG + h * 512 + w_],
                                          start=True, stop=True)
                        i_smm(t_.last)
                        for g in range(NG):
                            W(t_, SDV, DVB(g, 1))
                            if g >= 1:
                                W(t_, SDV, DVB(g - 1, 2))
                            for f in range(8):
                                t_.transpose(ptp2[:, f * 128:(f + 1) * 128],
                                             xn2_tm[g % 2][:, f * 128:(f + 1) * 128], id_bf[:, :])
                            i_stp(t_.last)
                            W(t_, SDV, DVB(g, 2))
                            if g == 0:
                                W(t_, SWB, 16)
                            for f in range(4):
                                bank = f % 2
                                if f >= 2:
                                    W(t_, SEV, EVB(g, f - 1))
                                for n4 in range(4):
                                    n = f * 4 + n4
                                    for k in range(8):
                                        t_.matmul(pres[:, bank * 512 + n4 * 128: bank * 512 + (n4 + 1) * 128],
                                                  regW1[:, k * 2048 + n * 128: k * 2048 + (n + 1) * 128],
                                                  xn2_fm[g % 2][:, k * 128:(k + 1) * 128],
                                                  start=(k == 0), stop=(k == 7))
                                i_smm(t_.last)
                            if g == 0:
                                W(t_, SWO, 16)
                            if g >= 1:
                                W(t_, SEV, EVB(g - 1, 5))
                            for kh in range(2):
                                W(t_, SDV, DVB(g, 3 + kh))
                                for n2 in range(2):
                                    for k8 in range(8):
                                        k = kh * 8 + k8
                                        t_.matmul(pq[:, n2 * 512:(n2 + 1) * 512],
                                                  p_sc[:, k * 128:(k + 1) * 128],
                                                  regW2[:, k * 1024 + n2 * 512: k * 1024 + n2 * 512 + 512],
                                                  start=(k == 0), stop=(k == 15))
                            i_smm(t_.last)
                            if g == 0:
                                W(t_, SDV, DV_B0 + 4)
                                t_.matmul(psc[0:1, 0:128], fdcol[:, 0:1], id_f32[:, :],
                                          start=True, stop=True)
                                i_smm(t_.last)
                        W(t_, SDV, DV_B0 + 4 + 4 * NG + 2)
                        for g in range(NG):
                            seg = (g * 128) // SEG
                            off = (g * 128) % SEG
                            t_.matmul(psc[:, 256 + g:256 + g + 1],
                                      a_sc[seg * 64:(seg + 1) * 64, off:off + 128],
                                      ones_f[seg * 64:(seg + 1) * 64, 0:1],
                                      start=True, stop=True)
                        i_smm(t_.last)
                        assert val(SMM) == MM_B0 + 1 + 5 * NG + 2

                    @blk.scalar
                    def _(s_raw):
                        s = _Trk(s_raw)
                        load_lnexp(s)
                        W(s, SDV, DV_B0 + 1)
                        i_sev(s.activation(a_sc[:, :], stc[:, :], AF.Exp))
                        s.preload_act_table(AF.Silu)
                        for g in range(NG):
                            s.drain()
                            for f in range(4):
                                bank = f % 2
                                W(s, SMM, MMB(g, f + 1))
                                for n4 in range(4):
                                    n = f * 4 + n4
                                    s.activation(sres[:, n * 128:(n + 1) * 128],
                                                 pres[:, bank * 512 + n4 * 128: bank * 512 + (n4 + 1) * 128],
                                                 AF.Silu, bias=cc(C_BIN + 16 + n))
                                i_sev(s.last)
                            W(s, SMM, MMB(g, 5))
                            if g >= 2:
                                W(s, SQ[g % 2], 16 * (g // 2))
                            i_sev(s.activation(q_sb[g % 2][:, :], pq[:, :], AF.Copy))
                            if g == 0:
                                W(s, SMM, MMB(0, 6))
                                s.drain()
                                i_sev(s.activation(fdrow[:, :], psc[0:1, 0:128], AF.Copy))
                        W(s, SMM, MM_B0 + 1 + 5 * NG + 2)
                        s.drain()
                        i_sev(s.activation(cc(C_SC, NG), psc[:, 256:256 + NG], AF.Copy))
                        assert val(SEV) == EV_B0 + 1 + 5 * NG + 1 + 1

                    @blk.gpsimd
                    def _(g_raw):
                        g = _Trk(g_raw)
                        W(g, SDV, DV_B0 + 3)
                        i_scc(g.dma_start(out=tails[:, 0:1], in_=stb[0:64, SEG - 1:SEG]), 16)
                        i_scc(g.dma_start(out=tails[:, 1:2], in_=stb[64:128, SEG - 1:SEG]), 16)
                        i_scc(g.dma_start(out=tails[:, 2:3], in_=stc[0:64, SEG - 1:SEG]), 16)
                        i_scc(g.dma_start(out=tails[:, 3:4], in_=stc[64:128, SEG - 1:SEG]), 16)
                        assert val(SCC) == CC_TAILS
                        W(g, SEV, EVB(0, 6))
                        i_scc(g.dma_start(out=cc_in_t.ap(), in_=fdrow[:, :]), 16)
                        assert val(SCC) == CC_CIN
                        W(g, SCC, CC_CIN)
                        g.collective_compute(
                            "AllGather", ALU.bypass,
                            replica_groups=[list(range(NCORE))],
                            ins=[cc_in_t.ap().opt()], outs=[cc_out_t.ap().opt()]).then_inc(SCD[0], 1)
                        CNT["scd"] += 1
                        W(g, SCD, 1)
                        for r in range(NCORE):
                            i_scc(g.dma_start(out=dgt[:, r:r + 1],
                                              in_=cc_out_t.ap()[r:r + 1, 0:DS]), 16)
                            i_scc(g.dma_start(out=fgt[:, r:r + 1],
                                              in_=cc_out_t.ap()[r:r + 1, DS:2 * DS]), 16)
                        assert val(SCC) == CC_GATH
                        W(g, SDV, DV_B0 + 4 + 4 * NG + 1)
                        i_scc(g.dma_start(out=cols[0:64, C_S0:C_S0 + 1], in_=srun[:, 0:1]), 16)
                        i_scc(g.dma_start(out=cols[64:128, C_S0:C_S0 + 1], in_=srun[:, 1:2]), 16)
                        assert val(SCC) == CC_S0

                    @blk.sync
                    def _(s_raw):
                        s = _Trk(s_raw)
                        if DEBUG_SPILL:
                            s.dma_start(out=dbg_xi[:, :], in_=bigA[:, :]).then_inc(SOUT[0], 16)
                            s.dma_start(out=dbg_md[:, :], in_=mdrow[:, :]).then_inc(SOUT[0], 16)
                            s.dma_start(out=dbg_bx[:, :], in_=stb[:, :]).then_inc(SOUT[0], 16)
                            s.dma_start(out=dbg_xtm[:, :], in_=x_tm[:, 0:2048]).then_inc(SOUT[0], 16)
                            s.dma_start(out=dbg_c2[:, 0:16], in_=cols[:, 48:64]).then_inc(SOUT[0], 16)
                            s.dma_start(out=dbg_c2[:, 16:32], in_=cols[:, 64:80]).then_inc(SOUT[0], 16)
                            s.dma_start(out=dbg_c2[:, 32:64], in_=cols[:, 0:32]).then_inc(SOUT[0], 16)
                            s.dma_start(out=dbg_w[:, :], in_=regW1[:, 0:2048]).then_inc(SOUT[0], 16)
                            CNT["sout"] += 128
                            W(s, SEV, EV_B0 + 1)
                            s.dma_start(out=dbg_a[:, :], in_=a_sc[:, :]).then_inc(SOUT[0], 16)
                            CNT["sout"] += 16
                        for g in range(NG):
                            W(s, SEV, EVB(g, 5))
                            i_sq[g % 2](s.dma_start(out=q_spill.rearrange("(g p) f -> g p f", p=128)[g],
                                                    in_=q_sb[g % 2][:, :]), 16)
                        if DEBUG_SPILL:
                            W(s, SDV, DV_B0 + 4 + 4 * NG + 2)
                            s.dma_start(out=dbg_sfin[:, :], in_=a_sc[:, :]).then_inc(SOUT[0], 16)
                            s.dma_start(out=dbg_sres[:, :], in_=sres[:, :]).then_inc(SOUT[0], 16)
                            CNT["sout"] += 32
                            W(s, SOUT, val(SOUT))
                        W(s, SQ[0], val(SQ[0]))
                        W(s, SQ[1], val(SQ[1]))

            stab.close()

            DV_C0 = val(SDV)
            EV_C0 = val(SEV)
            MM_C0 = val(SMM)
            TP_C0 = val(STP)
            SQ_TOT = [val(SQ[0]), val(SQ[1])]
            SR_C0 = [val(SR[0]), val(SR[1])]
            SSP_C0 = val(SSP)
            NB = min(int(os.environ.get('KNB', '8')), NG)
            NBAT = NG // NB

            def DVB0(b):
                return DV_C0 + (4 * NB + 2) * b

            def EVB0(b):
                return EV_C0 + 2 + (3 * NB + 2) * b

            def DV_ATTN(g):
                return DVB0(g // NB) + (g % NB) + 1

            def DV_X1(g):
                return DVB0(g // NB) + NB + 1 + (g % NB) + 1

            def DV_X1N(g):
                return DVB0(g // NB) + 2 * NB + 2 + 2 * (g % NB) + 1

            def DV_EVC3(g):
                return DVB0(g // NB) + 2 * NB + 2 + 2 * (g % NB) + 2

            def EV_LOG(g):
                return EVB0(g // NB) + 2 * NB + 2 + (g % NB) + 1

            DV_SMAX = DV_C0 + (4 * NB + 2) * NBAT
            EV_ALL = EV_C0 + 2 + (3 * NB + 2) * NBAT

            stc_sc = ExitStack()
            with stc_sc:
                ec = stc_sc.enter_context
                qstage = [ec(nc.sbuf_tensor(f"qstage{i}", [128, DM], BF16)) for i in range(2)]
                attn8 = ec(nc.sbuf_tensor("attn8", [128, NB * DM], BF16))
                _x1ntm = ec(nc.sbuf_tensor("x1ntm", [128, DM], BF16))
                x1n_tm = [_x1ntm, _x1ntm]
                scr1 = ec(nc.sbuf_tensor("scr1", [128, DM], BF16))
                logits = ec(nc.sbuf_tensor("logits", [128, NG * NE], F32))
                gsm = ec(nc.sbuf_tensor("gsm", [128, NG * NE], F32))
                gsm2 = ec(nc.sbuf_tensor("gsm2", [128, NG * NE], F32))
                gcol = ec(nc.sbuf_tensor("gcol", [128, 5 * NG], F32))

                def at8(i):
                    return attn8[:, i * DM:(i + 1) * DM]

                def lg3(t):
                    return t[:, :].rearrange("p (g j) -> p g j", j=NE)

                def gc(i):
                    return gcol[:, i * NG:(i + 1) * NG]

                def gc3(i):
                    return gcol[:, i * NG:(i + 1) * NG].unsqueeze(2).broadcast_to([128, NG, NE])

                with nc.Block() as blk, \
                     nc.psum_tensor("pbc", [128, DM], F32) as pbc, \
                     nc.psum_tensor("ptp3", [128, 1024], BF16) as ptp3, \
                     nc.psum_tensor("pgt", [128, NE], F32) as pgt, \
                     nc.psum_tensor("pcw", [8, 512], F32) as pcw:
                    i_sw1, i_smm, i_stp, i_sev, i_sdv, i_ssp = \
                        (mk_inc(s) for s in (SW1, SMM, STP, SEV, SDV, SSP))
                    i_sr = [mk_inc(SR[0]), mk_inc(SR[1])]

                    @blk.vector
                    def _(v_raw):
                        v = _Trk(v_raw)
                        for b in range(NBAT):
                            g0 = b * NB
                            for i in range(NB):
                                g = g0 + i
                                v.drain()
                                if b >= 1:
                                    W(v, SEV, EVB0(b - 1) + i + 1)
                                v.tensor_scalar(scr1[:, :], xg(g), cc(C_M1 + g), cc(C_R1 + g),
                                                ALU.subtract, ALU.mult)
                                v.drain()
                                v.tensor_mul(scr1[:, :], scr1[:, :], dvb_bc[:, :])
                                v.drain()
                                v.tensor_add(scr1[:, :], scr1[:, :], bob_bc[:, :])
                                v.drain()
                                W(v, SR[g % 2], SR_C0[g % 2] + 16 * (g // 2 + 1))
                                v.scalar_tensor_tensor(at8(i), qstage[g % 2][:, :],
                                                       cc(C_SC + g), scr1[:, :],
                                                       ALU.mult, ALU.add,
                                                       accum_out=cc(C_SUM + g))
                                i_sdv(v.last)
                            W(v, SEV, EVB0(b) + NB)
                            v.drain()
                            v.tensor_scalar(cc(C_T0, NB), cc(C_SUM + g0, NB), 1.0 / DM,
                                            None, ALU.mult)
                            v.drain()
                            v.tensor_mul(cc(C_T2, NB), cc(C_T0, NB), cc(C_T0, NB))
                            v.drain()
                            v.tensor_scalar(cc(C_T1, NB), cc(C_SSQ + g0, NB), 1.0 / DM,
                                            None, ALU.mult)
                            v.drain()
                            v.tensor_sub(cc(C_T2, NB), cc(C_T1, NB), cc(C_T2, NB))
                            i_sdv(v.last)
                            for i in range(NB):
                                g = g0 + i
                                if i == 0:
                                    W(v, SEV, EVB0(b) + NB + 1)
                                v.drain()
                                v.tensor_scalar(scr1[:, :], at8(i), cc(C_T0 + i), cc(C_T1 + i),
                                                ALU.subtract, ALU.mult)
                                v.drain()
                                v.scalar_tensor_tensor(xg(g), scr1[:, :], 1.0, xg(g),
                                                       ALU.bypass, ALU.add,
                                                       accum_out=cc(C_SUM + g))
                                i_sdv(v.last)
                            W(v, SEV, EVB0(b) + 2 * NB + 1)
                            v.drain()
                            v.tensor_scalar(cc(C_M3 + g0, NB), cc(C_SUM + g0, NB), 1.0 / DM,
                                            None, ALU.mult)
                            v.drain()
                            v.tensor_mul(cc(C_T2, NB), cc(C_M3 + g0, NB), cc(C_M3 + g0, NB))
                            v.drain()
                            v.tensor_scalar(cc(C_T1, NB), cc(C_SSQ + g0, NB), 1.0 / DM,
                                            None, ALU.mult)
                            v.drain()
                            v.tensor_sub(cc(C_T2, NB), cc(C_T1, NB), cc(C_T2, NB))
                            i_sdv(v.last)
                            for i in range(NB):
                                g = g0 + i
                                if i == 0:
                                    W(v, SEV, EVB0(b) + 2 * NB + 2)
                                if g >= 1:
                                    W(v, STP, TP_C0 + g)
                                v.drain()
                                v.tensor_scalar(x1n_tm[g % 2][:, :], xg(g), cc(C_M3 + g),
                                                cc(C_R3 + g), ALU.subtract, ALU.mult)
                                i_sdv(v.last)
                                W(v, STP, TP_C0 + g + 1)
                                v.drain()
                                v.tensor_copy(
                                    bigA[:, :].rearrange("p (k t) -> p k t", t=n_tok)[:, 0:8, g * 128:(g + 1) * 128],
                                    ptp3[:, :].rearrange("p (k t) -> p k t", t=128))
                                i_sdv(v.last)
                        W(v, SEV, EV_ALL)
                        v.drain()
                        v.reduce_max(gc(0), lg3(logits), axis=AXX)
                        v.drain()
                        v.tensor_sub(lg3(gsm), lg3(logits), gc3(0))
                        i_sdv(v.last)
                        W(v, SEV, EV_ALL + 1)
                        v.drain()
                        v.reduce_sum(gc(1), lg3(gsm2), axis=AXX)
                        v.drain()
                        v.reciprocal(gc(2), gc(1))
                        v.drain()
                        v.tensor_mul(lg3(gsm), lg3(gsm2), gc3(2))
                        v.drain()
                        v.reduce_max(gc(0), lg3(gsm), axis=AXX)
                        v.drain()
                        v.tensor_tensor(lg3(gsm2), lg3(gsm), gc3(0), ALU.is_lt)
                        v.drain()
                        v.tensor_mul(lg3(gsm2), lg3(gsm2), lg3(gsm))
                        v.drain()
                        v.reduce_max(gc(1), lg3(gsm2), axis=AXX)
                        v.drain()
                        v.tensor_tensor(lg3(gsm2), lg3(gsm), gc3(1), ALU.is_ge)
                        v.tensor_add(gc(3), gc(0), gc(1))
                        v.drain()
                        v.tensor_scalar(gc(3), gc(3), 1e-8, None, ALU.add)
                        v.drain()
                        v.reciprocal(gc(4), gc(3))
                        v.drain()
                        v.tensor_mul(lg3(cw_tm), lg3(gsm), lg3(gsm2))
                        v.drain()
                        v.tensor_mul(lg3(cw_tm), lg3(cw_tm), gc3(4))
                        i_sdv(v.last)
                        assert val(SDV) == DV_SMAX + 2

                    @blk.tensor
                    def _(t_raw):
                        t_ = _Trk(t_raw)
                        i_smm(t_.nop())
                        for g in range(NG):
                            W(t_, SDV, DV_X1N(g))
                            if g >= 1:
                                W(t_, SDV, DV_EVC3(g - 1))
                            for f in range(8):
                                t_.transpose(ptp3[:, f * 128:(f + 1) * 128],
                                             x1n_tm[g % 2][:, f * 128:(f + 1) * 128], id_bf[:, :])
                            i_stp(t_.last)
                            W(t_, SDV, DV_EVC3(g))
                            for k in range(8):
                                t_.matmul(pgt[:, 0:NE], bigA[:, k * n_tok + g * 128: k * n_tok + (g + 1) * 128],
                                          gate_sb[:, k * NE:(k + 1) * NE],
                                          start=(k == 0), stop=(k == 7))
                            i_smm(t_.last)
                            if g < NG - 1:
                                W(t_, SEV, EV_LOG(g))
                        W(t_, SDV, DV_SMAX + 2)
                        for pk in range(4):
                            if pk >= 1:
                                W(t_, SEV, EV_ALL + 1 + pk)
                            for g4 in range(4):
                                g = pk * 4 + g4
                                if g >= NG:
                                    continue
                                t_.transpose(pcw[:, g4 * 128:(g4 + 1) * 128],
                                             cw_tm[:, g * NE:(g + 1) * NE], id_f32[:, :])
                            i_smm(t_.last)
                        assert val(SMM) == MM_C0 + 1 + NG + 4

                    @blk.scalar
                    def _(s_raw):
                        s = _Trk(s_raw)
                        load_lnexp(s)
                        i_sev(s.nop())
                        i_sev(s.nop())
                        for b in range(NBAT):
                            g0 = b * NB
                            for i in range(NB):
                                g = g0 + i
                                W(s, SDV, DVB0(b) + i + 1)
                                s.drain()
                                s.activation(pbc[:, 0:DM], at8(i), AF.Square,
                                             accum_out=cc(C_SSQ + g))
                                i_sev(s.last)
                            W(s, SDV, DVB0(b) + NB + 1)
                            s.drain()
                            s.activation(cc(C_T1, NB), cc(C_T2, NB), AF.Ln, bias=zcol[:, 1:2])
                            s.drain()
                            s.activation(cc(C_T1, NB), cc(C_T1, NB), AF.Exp, scale=-0.5)
                            i_sev(s.last)
                            for i in range(NB):
                                g = g0 + i
                                W(s, SDV, DVB0(b) + NB + 1 + i + 1)
                                s.drain()
                                s.activation(pbc[:, 0:DM], xg(g), AF.Square,
                                             accum_out=cc(C_SSQ + g))
                                i_sev(s.last)
                            W(s, SDV, DVB0(b) + 2 * NB + 2)
                            s.drain()
                            s.activation(cc(C_T2, NB), cc(C_T2, NB), AF.Ln, bias=zcol[:, 1:2])
                            s.drain()
                            s.activation(cc(C_R3 + g0, NB), cc(C_T2, NB), AF.Exp, scale=-0.5)
                            i_sev(s.last)
                            for i in range(NB):
                                g = g0 + i
                                W(s, SMM, MM_C0 + 1 + g + 1)
                                s.drain()
                                s.activation(logits[:, g * NE:(g + 1) * NE], pgt[:, 0:NE], AF.Copy)
                                i_sev(s.last)
                        W(s, SDV, DV_SMAX + 1)
                        s.drain()
                        s.activation(gsm2[:, :], gsm[:, :], AF.Exp)
                        i_sev(s.last)
                        for pk in range(4):
                            s.drain()
                            W(s, SMM, MM_C0 + 1 + NG + pk + 1)
                            ng4 = min(4, NG - pk * 4)
                            if ng4 > 0:
                                s.activation(
                                    cwrows[:, :].rearrange("e (g t) -> e g t", t=128)[:, pk * 4:pk * 4 + ng4, :],
                                    pcw[:, 0:ng4 * 128].rearrange("e (g t) -> e g t", t=128),
                                    AF.Copy)
                            i_sev(s.last)
                        assert val(SEV) == EV_ALL + 5

                    @blk.gpsimd
                    def _(g_raw):
                        g = _Trk(g_raw)
                        i_sw1(g.dma_start(out=regW1[:, :].rearrange("p (k n) -> p k n", n=DH),
                                          in_=W1[0].rearrange("(k p) n -> p k n", p=128)), 16)
                        i_sw1(g.dma_start(out=regW2[:, :].rearrange("p (k n) -> p k n", n=DM),
                                          in_=W2[0].rearrange("(k p) n -> p k n", p=128)), 16)

                    @blk.sync
                    def _(s_raw):
                        s = _Trk(s_raw)
                        W(s, SQ[0], SQ_TOT[0])
                        W(s, SQ[1], SQ_TOT[1])
                        if DEBUG_SPILL:
                            s.dma_start(out=dbg_cols[:, :], in_=cols[:, 80:80 + NG]).then_inc(SOUT[0], 16)
                            CNT["sout"] += 16
                            W(s, SOUT, val(SOUT))
                        for g in range(NG):
                            if g >= 2:
                                W(s, SDV, DV_ATTN(g - 2))
                            i_sr[g % 2](s.dma_start(out=qstage[g % 2][:, :],
                                                    in_=q_spill.rearrange("(g p) f -> g p f", p=128)[g]), 16)
                        for g in range(NG):
                            W(s, SDV, DV_X1(g))
                            i_ssp(s.dma_start(out=x1_spill.rearrange("(g p) f -> g p f", p=128)[g],
                                              in_=xg(g)), 16)
                        W(s, SSP, val(SSP))
                        W(s, SR[0], val(SR[0]))
                        W(s, SR[1], val(SR[1]))

        stmoe = ExitStack()
        with stmoe:
            e3 = stmoe.enter_context
            regW1b = e3(nc.sbuf_tensor("regW1b", [128, 16384], BF16))
            regW2b = e3(nc.sbuf_tensor("regW2b", [128, 16384], BF16))
            hbuf = e3(nc.sbuf_tensor("hbuf", [128, 2048], BF16))
            hbuf2 = e3(nc.sbuf_tensor("hbuf2", [128, 2048], BF16))

            w1bufs = [regW1, regW1b]
            w2bufs = [regW2, regW2b]

            for half in range(2):
                DV_D0 = val(SDV)
                MM_D0 = val(SMM)
                EV_D0 = val(SEV)
                SW_D0 = [val(SW1), val(SW1B)]

                def MMD(e, tt, j):
                    return MM_D0 + 6 * (e * NTH + tt) + j

                def EVD(e, tt, j):
                    return EV_D0 + 4 * (e * NTH + tt) + j

                def DVD(e, tt, j):
                    return DV_D0 + 2 * (e * NTH + tt) + j

                stf = ExitStack()
                with stf:
                    ef = stf.enter_context
                    with nc.Block() as blk, \
                         nc.psum_tensor(f"ph{half}", [128, 2048], F32) as ph, \
                         nc.psum_tensor(f"pq2{half}", [128, 2048], F32) as pq2:
                        i_smm, i_sev, i_sdv, i_sout, i_sb2 = \
                            (mk_inc(s) for s in (SMM, SEV, SDV, SOUT, SB2))
                        i_sw1p = [mk_inc(SW1), mk_inc(SW1B)]
                        i_sr = [mk_inc(SR[0]), mk_inc(SR[1])]

                        @blk.vector
                        def _(v_raw):
                            v = _Trk(v_raw)
                            for e in range(NE):
                                for tt in range(NTH):
                                    for sub in range(2):
                                        v.drain()
                                        W(v, SMM, MMD(e, tt, 6))
                                        gh = tt * 2 + sub
                                        cwcol = cw_tm[:, (half * NGH + gh) * NE + e:
                                                      (half * NGH + gh) * NE + e + 1]
                                        if e == 0:
                                            v.tensor_scalar(oacc(gh),
                                                            pq2[:, sub * 1024:(sub + 1) * 1024],
                                                            cwcol, None, ALU.mult)
                                        else:
                                            v.scalar_tensor_tensor(
                                                oacc(gh), pq2[:, sub * 1024:(sub + 1) * 1024],
                                                cwcol, oacc(gh), ALU.mult, ALU.add)
                                        i_sdv(v.last)
                            assert val(SDV) == DV_D0 + 2 * NE * NTH

                        @blk.tensor
                        def _(t_raw):
                            t_ = _Trk(t_raw)
                            for e in range(NE):
                                wb1 = w1bufs[e % 2]
                                wb2 = w2bufs[e % 2]
                                _tgt = SW_D0[e % 2] + 32 * (e // 2 + (1 if e % 2 else 0))
                                W(t_, [SW1, SW1B][e % 2], _tgt)
                                for tt in range(NTH):
                                    t0 = half * (n_tok // 2) + tt * 256
                                    for hh in range(2):
                                        if hh == 0:
                                            if e == 0 and tt == 0:
                                                W(t_, SEV, EV_D0)
                                                W(t_, SDV, DV_D0)
                                            else:
                                                W(t_, SEV, EVD(e, tt - 1, 4) if tt > 0
                                                  else EVD(e - 1, NTH - 1, 4))
                                        else:
                                            W(t_, SEV, EVD(e, tt, 2))
                                        for nh in range(2):
                                            for n4 in range(4):
                                                n = nh * 4 + n4
                                                nn = hh * 8 + n
                                                for k in range(8):
                                                    t_.matmul(ph[:, n * 256:(n + 1) * 256],
                                                              wb1[:, k * 2048 + nn * 128: k * 2048 + (nn + 1) * 128],
                                                              bigA[:, k * n_tok + t0: k * n_tok + t0 + 256],
                                                              start=(k == 0), stop=(k == 7))
                                            i_smm(t_.last)
                                        if hh == 0 and (e > 0 or tt > 0):
                                            prev = DVD(e, tt - 1, 2) if tt > 0 else DVD(e - 1, NTH - 1, 2)
                                            W(t_, SDV, prev)
                                        src = hbuf if hh == 0 else hbuf2
                                        for kh in range(2):
                                            W(t_, SEV, EVD(e, tt, 2 * hh + 1 + kh))
                                            for sub in range(2):
                                                for n2 in range(2):
                                                    for k4 in range(4):
                                                        k = kh * 4 + k4
                                                        kk = hh * 8 + k
                                                        t_.matmul(
                                                            pq2[:, sub * 1024 + n2 * 512: sub * 1024 + (n2 + 1) * 512],
                                                            src[:, k * 256 + sub * 128: k * 256 + sub * 128 + 128],
                                                            wb2[:, kk * 1024 + n2 * 512: kk * 1024 + (n2 + 1) * 512],
                                                            start=(kk == 0), stop=(kk == 15))
                                        i_smm(t_.last)
                            assert val(SMM) == MM_D0 + 6 * NE * NTH

                        @blk.scalar
                        def _(s_raw):
                            s = _Trk(s_raw)
                            for e in range(NE):
                                for tt in range(NTH):
                                    for hh in range(2):
                                        dst = hbuf if hh == 0 else hbuf2
                                        for ch in range(2):
                                            W(s, SMM, MMD(e, tt, 3 * hh + 1 + ch))
                                            s.drain()
                                            for n4 in range(4):
                                                n = ch * 4 + n4
                                                nn = hh * 8 + n
                                                s.activation(dst[:, n * 256:(n + 1) * 256],
                                                             ph[:, n * 256:(n + 1) * 256],
                                                             AF.Relu, bias=b1c[:, nn * 8 + e:nn * 8 + e + 1])
                                            i_sev(s.last)
                            assert val(SEV) == EV_D0 + 4 * NE * NTH

                        @blk.gpsimd
                        def _(g_raw):
                            g = _Trk(g_raw)
                            for e in range(1, NE):
                                buf = e % 2
                                if e >= 2:
                                    W(g, SMM, MMD(e - 2, NTH - 1, 6))
                                i_sw1p[buf](g.dma_start(
                                    out=w1bufs[buf][:, :].rearrange("p (k n) -> p k n", n=DH),
                                    in_=W1[e].rearrange("(k p) n -> p k n", p=128)), 16)
                                i_sw1p[buf](g.dma_start(
                                    out=w2bufs[buf][:, :].rearrange("p (k n) -> p k n", n=DM),
                                    in_=W2[e].rearrange("(k p) n -> p k n", p=128)), 16)
                            if half == 0:
                                W(g, SMM, MMD(6, NTH - 1, 6))
                                i_sw1p[0](g.dma_start(
                                    out=w1bufs[0][:, :].rearrange("p (k n) -> p k n", n=DH),
                                    in_=W1[0].rearrange("(k p) n -> p k n", p=128)), 16)
                                i_sw1p[0](g.dma_start(
                                    out=w2bufs[0][:, :].rearrange("p (k n) -> p k n", n=DM),
                                    in_=W2[0].rearrange("(k p) n -> p k n", p=128)), 16)

                        MM_F0 = val(SMM)
                        EV_F0 = val(SEV)
                        DV_F0 = val(SDV)
                        SSP_TOT = SSP_C0 + 16 * NG
                        ffn8 = regW1b
                        ffn_f32 = regW2b[:, :].bitcast(F32)

                        def ffn_g(gh):
                            return ffn_f32[:, gh * DM:(gh + 1) * DM]

                        def x1r_g(gh):
                            return regW1b[:, (1 + gh) * DM:(2 + gh) * DM]

                        b2s = regW1b[:, (1 + NGH) * DM:(2 + NGH) * DM]
                        SR_F0 = [val(SR[0]), val(SR[1])]

                        @blk.gpsimd
                        def _(g_raw):
                            g = _Trk(g_raw)
                            W(g, SMM, MM_F0)
                            i_sb2(g.dma_start(out=regW1b[0:NE, 0:DM], in_=b2[:, :]), 16)
                        SW_B2 = val(SB2)

                        @blk.sync
                        def _(s_raw):
                            s = _Trk(s_raw)
                            W(s, SSP, SSP_TOT)
                            W(s, SMM, MM_F0)
                            for gh in range(NGH):
                                g = half * NGH + gh
                                i_sr[gh % 2](s.dma_start(
                                    out=x1r_g(gh),
                                    in_=x1_spill.rearrange("(g p) f -> g p f", p=128)[g]), 16)

                        @blk.tensor
                        def _(t_raw):
                            t_ = _Trk(t_raw)
                            for gh in range(NGH):
                                g = half * NGH + gh
                                if gh == 0:
                                    W(t_, SB2, SW_B2)
                                if gh >= 2:
                                    W(t_, SDV, DV_F0 + gh - 1)
                                for n2 in range(2):
                                    t_.matmul(ph[:, (gh % 2) * 1024 + n2 * 512:
                                                 (gh % 2) * 1024 + (n2 + 1) * 512],
                                              cwrows[:, g * 128:(g + 1) * 128],
                                              regW1b[0:NE, n2 * 512:(n2 + 1) * 512],
                                              start=True, stop=True)
                                i_smm(t_.last)

                        @blk.vector
                        def _(v_raw):
                            v = _Trk(v_raw)
                            W(v, SR[0], val(SR[0]))
                            W(v, SR[1], val(SR[1]))
                            for gh in range(NGH):
                                g = half * NGH + gh
                                W(v, SMM, MM_F0 + 1 + gh)
                                v.drain()
                                v.tensor_scalar(ffn_g(gh), x1r_g(gh),
                                                cc(C_M3 + g), cc(C_R3 + g),
                                                ALU.subtract, ALU.mult)
                                v.drain()
                                v.tensor_add(ffn_g(gh), ffn_g(gh), oacc(gh))
                                v.drain()
                                v.scalar_tensor_tensor(ffn_g(gh),
                                                       ph[:, (gh % 2) * 1024:(gh % 2) * 1024 + DM],
                                                       1.0, ffn_g(gh),
                                                       ALU.bypass, ALU.add,
                                                       accum_out=cc(C_SUM + gh))
                                i_sdv(v.last)
                            W(v, SEV, EV_F0 + NGH)
                            v.drain()
                            v.tensor_scalar(cc(C_M4, NGH), cc(C_SUM, NGH), 1.0 / DM,
                                            None, ALU.mult)
                            v.drain()
                            v.tensor_mul(cc(C_T2, NGH), cc(C_M4, NGH), cc(C_M4, NGH))
                            v.drain()
                            v.tensor_scalar(cc(C_T1, NGH), cc(C_SSQ, NGH), 1.0 / DM,
                                            None, ALU.mult)
                            v.drain()
                            v.tensor_sub(cc(C_T2, NGH), cc(C_T1, NGH), cc(C_T2, NGH))
                            i_sdv(v.last)
                            for gh in range(NGH):
                                if gh == 0:
                                    W(v, SEV, EV_F0 + NGH + 1)
                                v.drain()
                                v.tensor_scalar(oacc(gh), ffn_g(gh),
                                                cc(C_M4 + gh), cc(C_R4 + gh),
                                                ALU.subtract, ALU.mult)
                                v.drain()
                                v.tensor_add(oacc(gh), oacc(gh), x1r_g(gh))
                                i_sdv(v.last)

                        @blk.scalar
                        def _(s_raw):
                            s = _Trk(s_raw)
                            load_lnexp(s)
                            for gh in range(NGH):
                                W(s, SDV, DV_F0 + gh + 1)
                                s.drain()
                                s.activation(b2s, ffn_g(gh), AF.Square,
                                             accum_out=cc(C_SSQ + gh))
                                i_sev(s.last)
                            W(s, SDV, DV_F0 + NGH + 1)
                            s.drain()
                            s.activation(cc(C_T2, NGH), cc(C_T2, NGH), AF.Ln, bias=zcol[:, 1:2])
                            s.drain()
                            s.activation(cc(C_R4, NGH), cc(C_T2, NGH), AF.Exp, scale=-0.5)
                            i_sev(s.last)

                        @blk.sync
                        def _(s_raw):
                            s = _Trk(s_raw)
                            for gh in range(NGH):
                                g = half * NGH + gh
                                W(s, SDV, DV_F0 + NGH + 1 + gh + 1)
                                i_sout(s.dma_start(
                                    out=y_out.rearrange("(g p) f -> g p f", p=128)[g],
                                    in_=oacc(gh)), 16)
                            W(s, SOUT, val(SOUT))

        return nc


_CACHE = {}


def _get_program(n_tok):
    if n_tok not in _CACHE:
        _CACHE[n_tok] = build_program(n_tok)
    return _CACHE[n_tok]


def kernel(L0_attn_x, L0_attn_W_in, L0_attn_b_in, L0_attn_W_xp, L0_attn_b_xp,
           L0_attn_W_dt, L0_attn_b_dt, L0_attn_W_out, L0_attn_b_out,
           L0_attn_A_log, L0_attn_D, L0_ffn_gate_W, L0_ffn_W1, L0_ffn_b1,
           L0_ffn_W2, L0_ffn_b2):
    from concourse.bass_utils import run_bass_kernel_spmd

    B, S, _ = L0_attn_x.shape
    n_cores = NCORE
    chunks_per_b = n_cores // B
    n_tok = S // chunks_per_b

    nc = _get_program(n_tok)

    xf = np.ascontiguousarray(L0_attn_x.reshape(B * S, DM))
    shared = dict(
        W_in=np.ascontiguousarray(L0_attn_W_in),
        b_in=np.ascontiguousarray(L0_attn_b_in).reshape(1, -1),
        W_xp=np.ascontiguousarray(L0_attn_W_xp),
        b_xp=np.ascontiguousarray(L0_attn_b_xp).reshape(1, -1),
        W_dt=np.ascontiguousarray(L0_attn_W_dt),
        b_dt=np.ascontiguousarray(L0_attn_b_dt).reshape(1, -1),
        W_out=np.ascontiguousarray(L0_attn_W_out),
        b_out=np.ascontiguousarray(L0_attn_b_out).reshape(1, -1),
        A_log=np.ascontiguousarray(L0_attn_A_log).reshape(1, -1),
        D_vec=np.ascontiguousarray(L0_attn_D).reshape(1, -1),
        gate_W=np.ascontiguousarray(L0_ffn_gate_W),
        W1=np.ascontiguousarray(L0_ffn_W1),
        b1=np.ascontiguousarray(L0_ffn_b1),
        W2=np.ascontiguousarray(L0_ffn_W2),
        b2=np.ascontiguousarray(L0_ffn_b2),
        ident=np.eye(128, dtype=np.float32),
        eye8_in=np.kron(np.eye(NE, dtype=np.float32), np.ones((1, 64), dtype=np.float32)).astype(np.float32),
        binc_in=np.ascontiguousarray(np.asarray(L0_attn_b_in, dtype=np.float32).reshape(32, 128).T),
        bdtc_in=np.ascontiguousarray(np.asarray(L0_attn_b_dt, dtype=np.float32).reshape(8, 128).T),
        bxpc_in=np.ascontiguousarray(np.asarray(L0_attn_b_xp, dtype=np.float32).reshape(64, 1)),
        alog2_in=np.ascontiguousarray(np.concatenate([np.asarray(L0_attn_A_log, dtype=np.float32).reshape(64),
                                                      np.asarray(L0_attn_A_log, dtype=np.float32).reshape(64)]).reshape(128, 1)),
        b1c_in=np.ascontiguousarray(np.asarray(L0_ffn_b1, dtype=np.float32).reshape(8, 16, 128).transpose(2, 1, 0).reshape(128, 128)),
        bob_in=np.ascontiguousarray(np.tile(np.asarray(L0_attn_b_out, dtype=np.float32).reshape(1, 1024), (128, 1))),
        dvb_in=np.ascontiguousarray(np.tile(np.asarray(L0_attn_D, dtype=np.float32).reshape(1, 1024), (128, 1))),
        ones_in=np.ones((128, 128), dtype=np.float32),
        zeros_in=np.stack([np.zeros(128, np.float32), np.full(128, EPS, np.float32), np.ones(128, np.float32), np.full(128, -0.5, np.float32)], axis=1),
    )
    in_maps = []
    for c in range(n_cores):
        b = c // chunks_per_b
        ch = c % chunks_per_b
        mask = np.zeros((DS, NCORE), dtype=np.float32)
        for r in range(b * chunks_per_b, b * chunks_per_b + ch):
            mask[:, r] = 1.0
        im = dict(shared)
        im["x_in"] = np.ascontiguousarray(
            xf[(b * chunks_per_b + ch) * n_tok:(b * chunks_per_b + ch + 1) * n_tok])
        im["chain_in"] = mask
        in_maps.append(im)

    res = run_bass_kernel_spmd(nc, in_maps, core_ids=list(range(n_cores)))
    out = np.concatenate([res.results[c]["y_out"] for c in range(n_cores)], axis=0)
    return out.reshape(B, S, DM)


# version 15 (final)
# speedup vs baseline: 1.3052x; 1.0043x over previous
import os

import numpy as np
from contextlib import ExitStack

import concourse.bass as bass
import concourse.mybir as mybir

F32 = mybir.dt.float32
BF16 = mybir.dt.bfloat16
AF = mybir.ActivationFunctionType
ALU = mybir.AluOpType
AXX = mybir.AxisListType.X

DM = 1024
DI = 2048
DS = 64
NE = 8
DH = 2048
NCORE = 8
EPS = 1e-5


class _Trk:

    def __init__(self, eng):
        self._e = eng
        self.last = None

    def __getattr__(self, n):
        f = getattr(self._e, n)
        if callable(f):
            def w(*a, **k):
                r = f(*a, **k)
                if hasattr(r, "then_inc"):
                    self.last = r
                return r
            return w
        return f


DEBUG_SPILL = False


def build_program(n_tok):
    NG = n_tok // 128
    NT = n_tok // 256
    SEG = n_tok // 2
    NGH = NG // 2
    NTH = NT // 2
    assert n_tok % 512 == 0 and NG <= 16

    nc = bass.Bass("TRN2", target_bir_lowering=False)

    x_in = nc.dram_tensor("x_in", [n_tok, DM], F32, kind="ExternalInput").ap()
    W_in = nc.dram_tensor("W_in", [DM, 2 * DI], F32, kind="ExternalInput").ap()
    b_in = nc.dram_tensor("b_in", [1, 2 * DI], F32, kind="ExternalInput").ap()
    W_xp = nc.dram_tensor("W_xp", [DI, DS], F32, kind="ExternalInput").ap()
    b_xp = nc.dram_tensor("b_xp", [1, DS], F32, kind="ExternalInput").ap()
    W_dt = nc.dram_tensor("W_dt", [DI, DM], F32, kind="ExternalInput").ap()
    b_dt = nc.dram_tensor("b_dt", [1, DM], F32, kind="ExternalInput").ap()
    W_out = nc.dram_tensor("W_out", [DI, DM], F32, kind="ExternalInput").ap()
    b_out = nc.dram_tensor("b_out", [1, DM], F32, kind="ExternalInput").ap()
    A_log = nc.dram_tensor("A_log", [1, DS], F32, kind="ExternalInput").ap()
    D_vec = nc.dram_tensor("D_vec", [1, DM], F32, kind="ExternalInput").ap()
    gate_W = nc.dram_tensor("gate_W", [DM, NE], F32, kind="ExternalInput").ap()
    W1 = nc.dram_tensor("W1", [NE, DM, DH], F32, kind="ExternalInput").ap()
    b1 = nc.dram_tensor("b1", [NE, DH], F32, kind="ExternalInput").ap()
    W2 = nc.dram_tensor("W2", [NE, DH, DM], F32, kind="ExternalInput").ap()
    b2 = nc.dram_tensor("b2", [NE, DM], F32, kind="ExternalInput").ap()
    ident = nc.dram_tensor("ident", [128, 128], F32, kind="ExternalInput").ap()
    ones_in = nc.dram_tensor("ones_in", [128, 128], F32, kind="ExternalInput").ap()
    zeros_in = nc.dram_tensor("zeros_in", [128, 4], F32, kind="ExternalInput").ap()
    chain_in = nc.dram_tensor("chain_in", [DS, NCORE], F32, kind="ExternalInput").ap()
    eye8_in = nc.dram_tensor("eye8_in", [NE, NE * 64], F32, kind="ExternalInput").ap()
    binc_in = nc.dram_tensor("binc_in", [128, 32], F32, kind="ExternalInput").ap()
    bdtc_in = nc.dram_tensor("bdtc_in", [128, 8], F32, kind="ExternalInput").ap()
    bxpc_in = nc.dram_tensor("bxpc_in", [64, 1], F32, kind="ExternalInput").ap()
    alog2_in = nc.dram_tensor("alog2_in", [128, 1], F32, kind="ExternalInput").ap()
    b1c_in = nc.dram_tensor("b1c_in", [128, NE * 16], F32, kind="ExternalInput").ap()
    bob_in = nc.dram_tensor("bob_in", [128, DM], F32, kind="ExternalInput").ap()
    dvb_in = nc.dram_tensor("dvb_in", [128, DM], F32, kind="ExternalInput").ap()
    y_out = nc.dram_tensor("y_out", [n_tok, DM], F32, kind="ExternalOutput").ap()

    q_spill = nc.dram_tensor("q_spill", [n_tok, DM], BF16).ap()
    x1_spill = nc.dram_tensor("x1_spill", [n_tok, DM], BF16).ap()
    cc_in_t = nc.dram_tensor("cc_in", [1, 2 * DS], F32)
    cc_out_t = nc.dram_tensor("cc_out", [NCORE, 2 * DS], F32)
    if DEBUG_SPILL:
        dbg_xi = nc.dram_tensor("dbg_xi", [128, 16 * n_tok], BF16, kind="ExternalOutput").ap()
        dbg_md = nc.dram_tensor("dbg_md", [1, n_tok], BF16, kind="ExternalOutput").ap()
        dbg_bx = nc.dram_tensor("dbg_bx", [128, n_tok // 2], F32, kind="ExternalOutput").ap()
        dbg_a = nc.dram_tensor("dbg_a", [128, n_tok // 2], F32, kind="ExternalOutput").ap()
        dbg_sfin = nc.dram_tensor("dbg_sfin", [128, n_tok // 2], F32, kind="ExternalOutput").ap()
        dbg_cols = nc.dram_tensor("dbg_cols", [128, n_tok // 128], F32, kind="ExternalOutput").ap()
        dbg_sres = nc.dram_tensor("dbg_sres", [128, 2048], BF16, kind="ExternalOutput").ap()
        dbg_xtm = nc.dram_tensor("dbg_xtm", [128, 2048], BF16, kind="ExternalOutput").ap()
        dbg_c2 = nc.dram_tensor("dbg_c2", [128, 64], F32, kind="ExternalOutput").ap()
        dbg_w = nc.dram_tensor("dbg_w", [128, 2048], BF16, kind="ExternalOutput").ap()
        dbg_xnfm = nc.dram_tensor("dbg_xnfm", [128, 2048], BF16, kind="ExternalOutput").ap()

    CNT = {}

    def W(eng, sem, val):
        eng.wait_ge(sem[0], val)

    def mk_inc(sem):
        def inc(ins, n=1):
            CNT[sem[1]] += n
            ins.then_inc(sem[0], n)
            return CNT[sem[1]]
        return inc

    st0 = ExitStack()
    with st0:
        e0 = st0.enter_context

        def sem(name):
            h = e0(nc.semaphore(name))
            CNT[name] = 0
            return (h, name)

        SX = sem("sx")
        SWA = sem("swa")
        SWB = sem("swb")
        SW1 = sem("sw1")
        SLD = sem("sld")
        SMM = sem("smm")
        STP = sem("stp")
        SMD = sem("smd")
        SEV = sem("sev")
        SDV = sem("sdv")
        SCC = sem("scc")
        SQ = [sem("sq0"), sem("sq1")]
        SR = [sem("sr0"), sem("sr1")]
        SSP = sem("ssp")
        SW1B = sem("sw1b")
        SB2 = sem("sb2")
        SCD = sem("scd")
        SXX = sem("sxx")
        SWDT = sem("swdt")
        SWXP = sem("swxp")
        SWO = sem("swo")
        SOUT = sem("sout")

        def val(s):
            return CNT[s[1]]

        bigA = e0(nc.sbuf_tensor("bigA", [128, 16 * n_tok], BF16))
        regW1 = e0(nc.sbuf_tensor("regW1", [128, 16384], BF16))
        regW2 = e0(nc.sbuf_tensor("regW2", [128, 16384], BF16))
        eye8_sb = e0(nc.sbuf_tensor("eye8_sb", [NE, NE * 64], BF16))
        zcol = e0(nc.sbuf_tensor("zcol", [128, 4], F32))
        cols = e0(nc.sbuf_tensor("cols", [128, 256], F32))
        b1c = e0(nc.sbuf_tensor("b1c", [128, NE * 16], F32))
        cwrows = e0(nc.sbuf_tensor("cwrows", [NE, n_tok], BF16))
        cw_tm = e0(nc.sbuf_tensor("cw_tm", [128, (n_tok // 128) * NE], F32))

        C_BIN, C_BDT, C_BXP, C_A2, C_S0 = 0, 32, 40, 41, 42
        C_M1, C_R1, C_SC = 48, 64, 80
        C_M3, C_R3, C_M4, C_R4 = 96, 112, 128, 144
        C_SUM, C_SSQ, C_T0, C_T1, C_T2 = 160, 176, 192, 208, 224

        def cc(i, n=1):
            return cols[:, i:i + n]

        from concourse.hw_specs import get_activation_tables
        _lnexp_id = None
        for _idx, _funcs in enumerate(get_activation_tables(nc.m.arch).values()):
            if AF.Ln in _funcs and AF.Exp in _funcs:
                _lnexp_id = _idx
                break
        assert _lnexp_id is not None

        def load_lnexp(seng):
            seng.add_instruction(mybir.InstLoadActFuncSet(
                name=nc.get_next_instruction_name(),
                act_func_set_id=_lnexp_id,
                ins=[], outs=[]))

        def xi(k, t0, nt):
            return bigA[:, k * n_tok + t0: k * n_tok + t0 + nt]

        def xi3(t0, nt):
            return bigA[:, :].rearrange("p (k t) -> p k t", t=n_tok)[:, :, t0:t0 + nt]

        oacc_all = bigA[:, 8 * n_tok: 16 * n_tok].bitcast(F32)

        def oacc(gh):
            return oacc_all[:, gh * DM:(gh + 1) * DM]

        stm = ExitStack()
        with stm:
            em = stm.enter_context
            x_tm = em(nc.sbuf_tensor("x_tm", [128, NG * DM], BF16))
            gate_sb = em(nc.sbuf_tensor("gate_sb", [128, 8 * NE], BF16))
            id_bf = em(nc.sbuf_tensor("id_bf", [128, 128], BF16))
            id_f32 = em(nc.sbuf_tensor("id_f32", [128, 128], F32))
            ones_f = em(nc.sbuf_tensor("ones_f", [128, 128], F32))
            ones_b = em(nc.sbuf_tensor("ones_b", [128, 128], BF16))
            bob_bc = em(nc.sbuf_tensor("bob_bc", [128, DM], BF16))
            dvb_bc = em(nc.sbuf_tensor("dvb_bc", [128, DM], BF16))
            cmask = em(nc.sbuf_tensor("cmask", [64, NCORE], F32))
            stab = ExitStack()
            eab = stab.enter_context
            stb = eab(nc.sbuf_tensor("stb", [128, SEG], F32))
            mdrow = eab(nc.sbuf_tensor("mdrow", [1, n_tok], BF16))
            fdrow = eab(nc.sbuf_tensor("fdrow", [1, 2 * DS], F32))

            def xg(g):
                return x_tm[:, g * DM:(g + 1) * DM]

            sti = ExitStack()
            with sti:
                ei = sti.enter_context
                scr0 = ei(nc.sbuf_tensor("scr0", [128, DM], BF16))
                tcol = ei(nc.sbuf_tensor("tcol", [128, 1], F32))

                with nc.Block() as blk:
                    i_sx, i_sld, i_sev, i_sdv = (mk_inc(s) for s in (SX, SLD, SEV, SDV))

                    @blk.gpsimd
                    def _(g_raw):
                        g = _Trk(g_raw)
                        ins_x = g.dma_start(out=x_tm[:, :].rearrange("p (g f) -> p g f", f=DM),
                                            in_=x_in.rearrange("(g p) f -> p g f", p=128))
                        ins_x.then_inc(SXX[0], 16)
                        CNT["sxx"] += 16
                        i_sx(g.dma_start(out=id_bf[:, :], in_=ident[:, :]), 16)
                        i_sx(g.dma_start(out=ones_b[:, :], in_=ones_in[:, :]), 16)
                        i_sx(g.dma_start(out=gate_sb[:, :].rearrange("p (k j) -> p k j", j=NE),
                                         in_=gate_W.rearrange("(k p) j -> p k j", p=128)), 16)
                        i_sx(g.dma_start(out=eye8_sb[:, :], in_=eye8_in[:, :]), 16)
                        i_sx(g.dma_start(out=bob_bc[:, :], in_=bob_in[:, :]), 16)
                        i_sx(g.dma_start(out=dvb_bc[:, :], in_=dvb_in[:, :]), 16)

                    @blk.sync
                    def _(s_raw):
                        s = _Trk(s_raw)
                        i_sld(s.dma_start(out=id_f32[:, :], in_=ident[:, :]), 16)
                        i_sld(s.dma_start(out=ones_f[:, :], in_=ones_in[:, :]), 16)
                        i_sld(s.dma_start(out=zcol[:, :], in_=zeros_in[:, :]), 16)
                        i_sld(s.dma_start(out=cc(C_BIN, 32), in_=binc_in[:, :]), 16)
                        i_sld(s.dma_start(out=cc(C_BDT, 8), in_=bdtc_in[:, :]), 16)
                        i_sld(s.dma_start(out=cols[0:64, C_BXP:C_BXP + 1], in_=bxpc_in[:, :]), 16)
                        i_sld(s.dma_start(out=tcol[:, :], in_=alog2_in[:, :]), 16)
                        i_sld(s.dma_start(out=b1c[:, :], in_=b1c_in[:, :]), 16)
                        i_sld(s.dma_start(out=cmask[:, :], in_=chain_in[:, :]), 16)
                    LD_ALL = val(SLD)

                    @blk.scalar
                    def _(s_raw):
                        s = _Trk(s_raw)
                        load_lnexp(s)
                        W(s, SLD, LD_ALL)
                        i_sev(s.activation(tcol[:, :], tcol[:, :], AF.Exp))
                        W(s, SXX, 16)
                        for g in range(NG):
                            s.drain()
                            s.activation(scr0[:, :], xg(g), AF.Square,
                                         accum_out=cc(C_SSQ + g))
                        i_sev(s.last)
                    EV_I = val(SEV)

                    @blk.vector
                    def _(v_raw):
                        v = _Trk(v_raw)
                        W(v, SEV, EV_I - 1)
                        v.drain()
                        v.tensor_scalar(cc(C_A2), tcol[:, 0:1], -1.0 / DM, None, ALU.mult)
                        W(v, SXX, 16)
                        for g in range(NG):
                            v.reduce_sum(cc(C_SUM + g), xg(g), axis=AXX)
                        v.drain()
                        W(v, SEV, EV_I)
                        v.tensor_scalar(cc(C_M1, NG), cc(C_SUM, NG), 1.0 / DM, None, ALU.mult)
                        v.drain()
                        v.tensor_mul(cc(C_T0, NG), cc(C_M1, NG), cc(C_M1, NG))
                        v.tensor_scalar(cc(C_T1, NG), cc(C_SSQ, NG), 1.0 / DM, None, ALU.mult)
                        v.drain()
                        v.tensor_sub(cc(C_T2, NG), cc(C_T1, NG), cc(C_T0, NG))
                        i_sdv(v.last)
                    DV_I = val(SDV)

                    @blk.scalar
                    def _(s_raw):
                        s = _Trk(s_raw)
                        load_lnexp(s)
                        W(s, SDV, DV_I)
                        s.activation(cc(C_T0, NG), cc(C_T2, NG), AF.Ln, bias=zcol[:, 1:2])
                        s.drain()
                        s.activation(cc(C_R1, NG), cc(C_T0, NG), AF.Exp, scale=-0.5)
                        i_sev(s.last)
            SX_INIT = val(SX)
            SLD_INIT = val(SLD)

            DV_A0 = val(SDV)
            TP_A0 = val(STP)
            MM_A0 = val(SMM)
            EV_A0 = val(SEV)

            def DVA(t, j):
                return DV_A0 + 2 * t + j

            def TPA(t):
                return TP_A0 + t + 1

            def MMA(t, j):
                return MM_A0 + 13 * t + j

            def EVA(t, j):
                return EV_A0 + 15 * t + j

            stpa = ExitStack()
            with stpa:
                ea = stpa.enter_context
                wxp_sb = ea(nc.sbuf_tensor("wxp_sb", [128, 16 * DS], BF16))
                xn_tm = [ea(nc.sbuf_tensor(f"xn_tm{i}", [128, 2048], BF16)) for i in range(2)]
                xn_fm = [ea(nc.sbuf_tensor(f"xn_fm{i}", [128, 2048], BF16)) for i in range(2)]
                ez_sc = ea(nc.sbuf_tensor("ez_sc", [128, 1024], BF16))
                sp_sc = ea(nc.sbuf_tensor("sp_sc", [128, 2048], BF16))

                with nc.Block() as blk, \
                     nc.psum_tensor("ptp", [128, 2048], BF16) as ptp, \
                     nc.psum_tensor("pxi", [128, 1024], F32) as pxi, \
                     nc.psum_tensor("pdl", [128, 1024], F32) as pdl, \
                     nc.psum_tensor("pmd", [1, 256], F32) as pmd, \
                     nc.psum_tensor("pbx", [64, 256], F32) as pbx:
                    i_swa, i_swb, i_smm, i_stp, i_smd, i_sev, i_sdv = \
                        (mk_inc(s) for s in (SWA, SWB, SMM, STP, SMD, SEV, SDV))

                    @blk.vector
                    def _(v_raw):
                        v = _Trk(v_raw)
                        W(v, SX, SX_INIT)
                        W(v, SLD, SLD_INIT)
                        for t in range(NT):
                            v.drain()
                            if t >= 2:
                                W(v, STP, TPA(t - 2))
                            for g2 in range(2):
                                g = 2 * t + g2
                                v.tensor_scalar(xn_tm[t % 2][:, g2 * 1024:(g2 + 1) * 1024],
                                                xg(g), cc(C_M1 + g), cc(C_R1 + g),
                                                ALU.subtract, ALU.mult)
                            i_sdv(v.last)
                            W(v, STP, TPA(t))
                            if t >= 2:
                                W(v, SMM, MMA(t - 2, 13))
                            v.drain()
                            v.tensor_copy(
                                xn_fm[t % 2][:, :].rearrange("p (f g2 tok) -> p f g2 tok",
                                                             g2=2, tok=128),
                                ptp[:, :].rearrange("p (g2 f tok) -> p g2 f tok",
                                                    f=8, tok=128).transpose([0, 2, 1, 3]))
                            i_sdv(v.last)
                        assert val(SDV) == DVA(NT - 1, 2)

                    @blk.tensor
                    def _(t_raw):
                        t_ = _Trk(t_raw)
                        W(t_, SX, SX_INIT)
                        W(t_, SWA, 16)
                        for t in range(NT):
                            if t == 0:
                                pass
                            W(t_, SDV, DVA(t, 1))
                            if t >= 1:
                                W(t_, SDV, DVA(t - 1, 2))
                            for g2 in range(2):
                                for f in range(8):
                                    t_.transpose(ptp[:, g2 * 1024 + f * 128:g2 * 1024 + (f + 1) * 128],
                                                 xn_tm[t % 2][:, g2 * 1024 + f * 128:g2 * 1024 + (f + 1) * 128],
                                                 id_bf[:, :])
                            i_stp(t_.last)
                            W(t_, SDV, DVA(t, 2))
                            for np_ in range(8):
                                bank = np_ % 2
                                if np_ >= 2:
                                    W(t_, SEV, EVA(t, np_ - 1))
                                for n2 in range(2):
                                    n = np_ * 2 + n2
                                    for k in range(8):
                                        t_.matmul(pxi[:, bank * 512 + n2 * 256: bank * 512 + n2 * 256 + 256],
                                                  regW1[:, k * 2048 + n * 128: k * 2048 + (n + 1) * 128],
                                                  xn_fm[t % 2][:, k * 256:(k + 1) * 256],
                                                  start=(k == 0), stop=(k == 7))
                                i_smm(t_.last)
                            W(t_, SEV, EVA(t, 8))
                            if t == 0:
                                W(t_, SWDT, 16)
                            for f in range(4):
                                bank = f % 2
                                if f >= 2:
                                    W(t_, SEV, EVA(t, 8 + f - 1))
                                for n2 in range(2):
                                    nn = f * 2 + n2
                                    for k in range(16):
                                        t_.matmul(pdl[:, bank * 512 + n2 * 256: bank * 512 + n2 * 256 + 256],
                                                  regW2[:, k * 1024 + nn * 128: k * 1024 + (nn + 1) * 128],
                                                  xi(k, t * 256, 256),
                                                  start=(k == 0), stop=(k == 15))
                                i_smm(t_.last)
                            if t >= 1:
                                W(t_, SEV, EVA(t - 1, 15))
                            W(t_, SEV, EVA(t, 11))
                            for ch in range(4):
                                t_.matmul(pmd[:, 0:256], ones_b[0:128, 0:1],
                                          sp_sc[:, ch * 256:(ch + 1) * 256],
                                          start=(ch == 0), stop=False)
                            W(t_, SEV, EVA(t, 14))
                            for ch in range(4, 8):
                                t_.matmul(pmd[:, 0:256], ones_b[0:128, 0:1],
                                          sp_sc[:, ch * 256:(ch + 1) * 256],
                                          start=False, stop=(ch == 7))
                            i_smd(t_.last)
                            if t == 0:
                                W(t_, SWXP, 16)
                            for k in range(16):
                                t_.matmul(pbx[:, 0:256], wxp_sb[:, k * DS:(k + 1) * DS],
                                          xi(k, t * 256, 256), start=(k == 0), stop=(k == 15))
                            i_smm(t_.last)
                        assert val(SMM) == MMA(NT - 1, 13)

                    @blk.scalar
                    def _(s_raw):
                        s = _Trk(s_raw)
                        for t in range(NT):
                            s.drain()
                            s.preload_act_table(AF.Silu)
                            for np_ in range(8):
                                bank = np_ % 2
                                W(s, SMM, MMA(t, np_ + 1))
                                for n2 in range(2):
                                    n = np_ * 2 + n2
                                    s.activation(xi(n, t * 256, 256),
                                                 pxi[:, bank * 512 + n2 * 256: bank * 512 + n2 * 256 + 256],
                                                 AF.Silu, bias=cc(C_BIN + n))
                                i_sev(s.last)
                            load_lnexp(s)
                            for half in range(2):
                                s.drain()
                                for f2 in range(2):
                                    f = half * 2 + f2
                                    bank = f % 2
                                    W(s, SMM, MMA(t, 8 + f + 1))
                                    for n2 in range(2):
                                        nn = f * 2 + n2
                                        s.activation(ez_sc[:, f2 * 512 + n2 * 256: f2 * 512 + n2 * 256 + 256],
                                                     pdl[:, bank * 512 + n2 * 256: bank * 512 + n2 * 256 + 256],
                                                     AF.Exp, bias=cc(C_BDT + nn))
                                    i_sev(s.last)
                                s.drain()
                                i_sev(s.activation(sp_sc[:, half * 1024:(half + 1) * 1024],
                                                   ez_sc[:, :], AF.Ln, bias=zcol[:, 2:3]))
                            W(s, SMD, t + 1)
                            s.activation(mdrow[0:1, t * 256:(t + 1) * 256], pmd[:, 0:256], AF.Copy)
                            W(s, SMM, MMA(t, 13))
                            seg = (t * 256) // SEG
                            off = (t * 256) % SEG
                            s.activation(stb[seg * 64:(seg + 1) * 64, off:off + 256], pbx[:, 0:256],
                                         AF.Identity, bias=cols[0:64, C_BXP:C_BXP + 1])
                            i_sev(s.last)
                        assert val(SEV) == EVA(NT - 1, 15)

                    @blk.gpsimd
                    def _(g_raw):
                        g = _Trk(g_raw)
                        i_swa(g.dma_start(
                            out=regW1[:, :].rearrange("p (k n) -> p k n", n=2048),
                            in_=W_in.rearrange("(k p) n -> p k n", p=128)[:, :, 0:DI]), 16)
                        g.dma_start(
                            out=regW2[:, :].rearrange("p (k n) -> p k n", n=1024),
                            in_=W_dt.rearrange("(k p) n -> p k n", p=128)).then_inc(SWDT[0], 16)
                        CNT["swdt"] += 16
                        g.dma_start(
                            out=wxp_sb[:, :].rearrange("p (k n) -> p k n", n=DS),
                            in_=W_xp.rearrange("(k p) n -> p k n", p=128)).then_inc(SWXP[0], 16)
                        CNT["swxp"] += 16

            DV_B0 = val(SDV)
            TP_B0 = val(STP)
            MM_B0 = val(SMM)
            EV_B0 = val(SEV)

            def DVB(g, j):
                return DV_B0 + 4 + 4 * g + j

            def MMB(g, j):
                return MM_B0 + 1 + 5 * g + (1 if g >= 1 else 0) + j

            def EVB(g, j):
                return EV_B0 + 1 + 5 * g + (1 if g >= 1 else 0) + j

            CC_TAILS = val(SCC) + 64
            CC_CIN = CC_TAILS + 16
            CC_GATH = CC_CIN + 16 * 2 * NCORE
            CC_S0 = CC_GATH + 32

            sta2 = ExitStack()
            with sta2:
                e2 = sta2.enter_context
                a_sc = e2(nc.sbuf_tensor("a_sc", [128, SEG], F32))
                stc = e2(nc.sbuf_tensor("stc", [128, SEG], F32))
                _xn2tm = e2(nc.sbuf_tensor("xn2tm", [128, 1024], BF16))
                _xn2fm = e2(nc.sbuf_tensor("xn2fm", [128, 1024], BF16))
                xn2_tm = [_xn2tm, _xn2tm]
                xn2_fm = [_xn2fm, _xn2fm]
                sres = e2(nc.sbuf_tensor("sres", [128, 2048], BF16))
                p_sc = e2(nc.sbuf_tensor("p_sc", [128, 2048], BF16))
                q_sb = [e2(nc.sbuf_tensor(f"q_sb{i}", [128, DM], BF16)) for i in range(2)]
                tails = e2(nc.sbuf_tensor("tails", [64, 4], F32))
                fdcol = e2(nc.sbuf_tensor("fdcol", [128, 1], F32))
                dgt = e2(nc.sbuf_tensor("dgt", [64, NCORE], F32))
                fgt = e2(nc.sbuf_tensor("fgt", [64, NCORE], F32))
                dgp = e2(nc.sbuf_tensor("dgp", [64, NCORE], F32))
                fgp = e2(nc.sbuf_tensor("fgp", [64, NCORE], F32))
                srun = e2(nc.sbuf_tensor("srun", [64, 3], F32))

                with nc.Block() as blk, \
                     nc.psum_tensor("pmdb", [128, min(SEG, 1024)], F32) as pmdb, \
                     nc.psum_tensor("psc", [128, 512], F32) as psc, \
                     nc.psum_tensor("ptp2", [128, 1024], BF16) as ptp2, \
                     nc.psum_tensor("pres", [128, 1024], F32) as pres, \
                     nc.psum_tensor("pq", [128, 1024], F32) as pq:
                    i_smm, i_stp, i_sev, i_sdv, i_scc, i_swb = \
                        (mk_inc(s) for s in (SMM, STP, SEV, SDV, SCC, SWB))
                    i_sq = [mk_inc(SQ[0]), mk_inc(SQ[1])]

                    @blk.vector
                    def _(v_raw):
                        v = _Trk(v_raw)
                        W(v, SMM, MM_B0 + 1)
                        v.tensor_scalar(stc[:, :], pmdb[:, 0:SEG], cc(C_A2), None, ALU.mult)
                        i_sdv(v.last)
                        W(v, SEV, EV_B0 + 1)
                        v.drain()
                        if DEBUG_SPILL:
                            W(v, SOUT, val(SOUT) + 144)
                        v.tensor_tensor_scan(stc[:, :], a_sc[:, :], stb[:, :], 0.0,
                                             ALU.mult, ALU.add)
                        i_sdv(v.last)
                        v.drain()
                        v.tensor_tensor_scan(stb[:, :], a_sc[:, :], a_sc[:, :], 1.0,
                                             ALU.mult, ALU.bypass)
                        i_sdv(v.last)
                        W(v, SCC, CC_TAILS)
                        v.tensor_mul(fdcol[0:64, 0:1], tails[:, 0:1], tails[:, 1:2])
                        v.tensor_mul(srun[:, 0:1], tails[:, 1:2], tails[:, 2:3])
                        v.drain()
                        v.tensor_add(fdcol[64:128, 0:1], tails[:, 3:4], srun[:, 0:1])
                        i_sdv(v.last)
                        for g in range(NG):
                            v.drain()
                            if g >= 1:
                                W(v, STP, TP_B0 + g)
                            v.tensor_scalar(xn2_tm[g % 2][:, :], xg(g), cc(C_M1 + g),
                                            cc(C_R1 + g), ALU.subtract, ALU.mult)
                            i_sdv(v.last)
                            W(v, STP, TP_B0 + g + 1)
                            if g >= 1:
                                W(v, SMM, MMB(g - 1, 4))
                            v.drain()
                            v.tensor_copy(xn2_fm[g % 2][:, :], ptp2[:, :])
                            i_sdv(v.last)
                            if g >= 1:
                                W(v, SMM, MMB(g - 1, 5))
                            for ph2 in range(2):
                                W(v, SEV, EVB(g, 2 * ph2 + 2))
                                v.drain()
                                v.tensor_mul(
                                    p_sc[:, :].rearrange("p (k t) -> p k t", t=128)[:, ph2 * 8:(ph2 + 1) * 8, :],
                                    xi3(g * 128, 128)[:, ph2 * 8:(ph2 + 1) * 8, :],
                                    sres[:, :].rearrange("p (k t) -> p k t", t=128)[:, ph2 * 8:(ph2 + 1) * 8, :])
                                i_sdv(v.last)
                        W(v, SCC, CC_GATH)
                        v.tensor_mul(dgp[:, :], dgt[:, :], cmask[:, :])
                        v.tensor_mul(fgp[:, :], fgt[:, :], cmask[:, :])
                        v.drain()
                        v.tensor_sub(dgp[:, :], dgp[:, :], cmask[:, :])
                        v.drain()
                        v.tensor_scalar(dgp[:, :], dgp[:, :], 1.0, None, ALU.add)
                        v.tensor_copy(srun[:, 0:1], zcol[0:64, 0:1])
                        v.drain()
                        cur = 0
                        for r in range(NCORE):
                            v.tensor_mul(srun[:, 2:3], dgp[:, r:r + 1], srun[:, cur:cur + 1])
                            v.drain()
                            v.tensor_add(srun[:, 1 - cur:2 - cur], srun[:, 2:3], fgp[:, r:r + 1])
                            v.drain()
                            cur = 1 - cur
                        assert cur == 0
                        v.tensor_mul(srun[:, 2:3], tails[:, 0:1], srun[:, 0:1])
                        v.drain()
                        v.tensor_add(srun[:, 1:2], srun[:, 2:3], tails[:, 2:3])
                        i_sdv(v.last)
                        W(v, SCC, CC_S0)
                        v.drain()
                        v.scalar_tensor_tensor(a_sc[:, :], stb[:, :], cc(C_S0),
                                               stc[:, :], ALU.mult, ALU.add)
                        i_sdv(v.last)
                        assert val(SDV) == DV_B0 + 4 + 4 * NG + 2

                    @blk.tensor
                    def _(t_raw):
                        t_ = _Trk(t_raw)
                        for seg in range(2):
                            for h in range((SEG + 511) // 512):
                                w_ = min(512, SEG - h * 512)
                                t_.matmul(pmdb[seg * 64:(seg + 1) * 64, h * 512:h * 512 + w_],
                                          ones_b[0:1, 0:64],
                                          mdrow[0:1, seg * SEG + h * 512: seg * SEG + h * 512 + w_],
                                          start=True, stop=True)
                        i_smm(t_.last)
                        for g in range(NG):
                            W(t_, SDV, DVB(g, 1))
                            if g >= 1:
                                W(t_, SDV, DVB(g - 1, 2))
                            for f in range(8):
                                t_.transpose(ptp2[:, f * 128:(f + 1) * 128],
                                             xn2_tm[g % 2][:, f * 128:(f + 1) * 128], id_bf[:, :])
                            i_stp(t_.last)
                            W(t_, SDV, DVB(g, 2))
                            if g == 0:
                                W(t_, SWB, 16)
                            for f in range(4):
                                bank = f % 2
                                if f >= 2:
                                    W(t_, SEV, EVB(g, f - 1))
                                for n4 in range(4):
                                    n = f * 4 + n4
                                    for k in range(8):
                                        t_.matmul(pres[:, bank * 512 + n4 * 128: bank * 512 + (n4 + 1) * 128],
                                                  regW1[:, k * 2048 + n * 128: k * 2048 + (n + 1) * 128],
                                                  xn2_fm[g % 2][:, k * 128:(k + 1) * 128],
                                                  start=(k == 0), stop=(k == 7))
                                i_smm(t_.last)
                            if g == 0:
                                W(t_, SWO, 16)
                            if g >= 1:
                                W(t_, SEV, EVB(g - 1, 5))
                            for kh in range(2):
                                W(t_, SDV, DVB(g, 3 + kh))
                                for n2 in range(2):
                                    for k8 in range(8):
                                        k = kh * 8 + k8
                                        t_.matmul(pq[:, n2 * 512:(n2 + 1) * 512],
                                                  p_sc[:, k * 128:(k + 1) * 128],
                                                  regW2[:, k * 1024 + n2 * 512: k * 1024 + n2 * 512 + 512],
                                                  start=(k == 0), stop=(k == 15))
                            i_smm(t_.last)
                            if g == 0:
                                W(t_, SDV, DV_B0 + 4)
                                t_.matmul(psc[0:1, 0:128], fdcol[:, 0:1], id_f32[:, :],
                                          start=True, stop=True)
                                i_smm(t_.last)
                        W(t_, SDV, DV_B0 + 4 + 4 * NG + 2)
                        for g in range(NG):
                            seg = (g * 128) // SEG
                            off = (g * 128) % SEG
                            t_.matmul(psc[:, 256 + g:256 + g + 1],
                                      a_sc[seg * 64:(seg + 1) * 64, off:off + 128],
                                      ones_f[seg * 64:(seg + 1) * 64, 0:1],
                                      start=True, stop=True)
                        i_smm(t_.last)
                        assert val(SMM) == MM_B0 + 1 + 5 * NG + 2

                    @blk.scalar
                    def _(s_raw):
                        s = _Trk(s_raw)
                        load_lnexp(s)
                        W(s, SDV, DV_B0 + 1)
                        i_sev(s.activation(a_sc[:, :], stc[:, :], AF.Exp))
                        s.preload_act_table(AF.Silu)
                        for g in range(NG):
                            s.drain()
                            for f in range(4):
                                bank = f % 2
                                W(s, SMM, MMB(g, f + 1))
                                for n4 in range(4):
                                    n = f * 4 + n4
                                    s.activation(sres[:, n * 128:(n + 1) * 128],
                                                 pres[:, bank * 512 + n4 * 128: bank * 512 + (n4 + 1) * 128],
                                                 AF.Silu, bias=cc(C_BIN + 16 + n))
                                i_sev(s.last)
                            W(s, SMM, MMB(g, 5))
                            if g >= 2:
                                W(s, SQ[g % 2], 16 * (g // 2))
                            i_sev(s.activation(q_sb[g % 2][:, :], pq[:, :], AF.Copy))
                            if g == 0:
                                W(s, SMM, MMB(0, 6))
                                s.drain()
                                i_sev(s.activation(fdrow[:, :], psc[0:1, 0:128], AF.Copy))
                        W(s, SMM, MM_B0 + 1 + 5 * NG + 2)
                        s.drain()
                        i_sev(s.activation(cc(C_SC, NG), psc[:, 256:256 + NG], AF.Copy))
                        assert val(SEV) == EV_B0 + 1 + 5 * NG + 1 + 1

                    @blk.gpsimd
                    def _(g_raw):
                        g = _Trk(g_raw)
                        W(g, SMM, MMA(NT - 1, 8))
                        i_swb(g.dma_start(
                            out=regW1[:, :].rearrange("p (k n) -> p k n", n=2048),
                            in_=W_in.rearrange("(k p) n -> p k n", p=128)[:, :, DI:2 * DI]), 16)
                        W(g, SMM, MMA(NT - 1, 12))
                        g.dma_start(
                            out=regW2[:, :].rearrange("p (k n) -> p k n", n=1024),
                            in_=W_out.rearrange("(k p) n -> p k n", p=128)).then_inc(SWO[0], 16)
                        CNT["swo"] += 16
                        W(g, SDV, DV_B0 + 3)
                        i_scc(g.dma_start(out=tails[:, 0:1], in_=stb[0:64, SEG - 1:SEG]), 16)
                        i_scc(g.dma_start(out=tails[:, 1:2], in_=stb[64:128, SEG - 1:SEG]), 16)
                        i_scc(g.dma_start(out=tails[:, 2:3], in_=stc[0:64, SEG - 1:SEG]), 16)
                        i_scc(g.dma_start(out=tails[:, 3:4], in_=stc[64:128, SEG - 1:SEG]), 16)
                        assert val(SCC) == CC_TAILS
                        W(g, SEV, EVB(0, 6))
                        i_scc(g.dma_start(out=cc_in_t.ap(), in_=fdrow[:, :]), 16)
                        assert val(SCC) == CC_CIN
                        W(g, SCC, CC_CIN)
                        g.collective_compute(
                            "AllGather", ALU.bypass,
                            replica_groups=[list(range(NCORE))],
                            ins=[cc_in_t.ap().opt()], outs=[cc_out_t.ap().opt()]).then_inc(SCD[0], 1)
                        CNT["scd"] += 1
                        W(g, SCD, 1)
                        for r in range(NCORE):
                            i_scc(g.dma_start(out=dgt[:, r:r + 1],
                                              in_=cc_out_t.ap()[r:r + 1, 0:DS]), 16)
                            i_scc(g.dma_start(out=fgt[:, r:r + 1],
                                              in_=cc_out_t.ap()[r:r + 1, DS:2 * DS]), 16)
                        assert val(SCC) == CC_GATH
                        W(g, SDV, DV_B0 + 4 + 4 * NG + 1)
                        i_scc(g.dma_start(out=cols[0:64, C_S0:C_S0 + 1], in_=srun[:, 0:1]), 16)
                        i_scc(g.dma_start(out=cols[64:128, C_S0:C_S0 + 1], in_=srun[:, 1:2]), 16)
                        assert val(SCC) == CC_S0

                    @blk.sync
                    def _(s_raw):
                        s = _Trk(s_raw)
                        if DEBUG_SPILL:
                            s.dma_start(out=dbg_xi[:, :], in_=bigA[:, :]).then_inc(SOUT[0], 16)
                            s.dma_start(out=dbg_md[:, :], in_=mdrow[:, :]).then_inc(SOUT[0], 16)
                            s.dma_start(out=dbg_bx[:, :], in_=stb[:, :]).then_inc(SOUT[0], 16)
                            s.dma_start(out=dbg_xtm[:, :], in_=x_tm[:, 0:2048]).then_inc(SOUT[0], 16)
                            s.dma_start(out=dbg_c2[:, 0:16], in_=cols[:, 48:64]).then_inc(SOUT[0], 16)
                            s.dma_start(out=dbg_c2[:, 16:32], in_=cols[:, 64:80]).then_inc(SOUT[0], 16)
                            s.dma_start(out=dbg_c2[:, 32:64], in_=cols[:, 0:32]).then_inc(SOUT[0], 16)
                            s.dma_start(out=dbg_w[:, :], in_=regW1[:, 0:2048]).then_inc(SOUT[0], 16)
                            CNT["sout"] += 128
                            W(s, SEV, EV_B0 + 1)
                            s.dma_start(out=dbg_a[:, :], in_=a_sc[:, :]).then_inc(SOUT[0], 16)
                            CNT["sout"] += 16
                        for g in range(NG):
                            W(s, SEV, EVB(g, 5))
                            i_sq[g % 2](s.dma_start(out=q_spill.rearrange("(g p) f -> g p f", p=128)[g],
                                                    in_=q_sb[g % 2][:, :]), 16)
                        if DEBUG_SPILL:
                            W(s, SDV, DV_B0 + 4 + 4 * NG + 2)
                            s.dma_start(out=dbg_sfin[:, :], in_=a_sc[:, :]).then_inc(SOUT[0], 16)
                            s.dma_start(out=dbg_sres[:, :], in_=sres[:, :]).then_inc(SOUT[0], 16)
                            CNT["sout"] += 32
                            W(s, SOUT, val(SOUT))
                        W(s, SQ[0], val(SQ[0]))
                        W(s, SQ[1], val(SQ[1]))

            stab.close()

            DV_C0 = val(SDV)
            EV_C0 = val(SEV)
            MM_C0 = val(SMM)
            TP_C0 = val(STP)
            SQ_TOT = [val(SQ[0]), val(SQ[1])]
            SR_C0 = [val(SR[0]), val(SR[1])]
            SSP_C0 = val(SSP)
            NB = min(int(os.environ.get('KNB', '8')), NG)
            NBAT = NG // NB

            def DVB0(b):
                return DV_C0 + (4 * NB + 2) * b

            def EVB0(b):
                return EV_C0 + 2 + (3 * NB + 2) * b

            def DV_ATTN(g):
                return DVB0(g // NB) + (g % NB) + 1

            def DV_X1(g):
                return DVB0(g // NB) + NB + 1 + (g % NB) + 1

            def DV_X1N(g):
                return DVB0(g // NB) + 2 * NB + 2 + 2 * (g % NB) + 1

            def DV_EVC3(g):
                return DVB0(g // NB) + 2 * NB + 2 + 2 * (g % NB) + 2

            def EV_LOG(g):
                return EVB0(g // NB) + 2 * NB + 2 + (g % NB) + 1

            DV_SMAX = DV_C0 + (4 * NB + 2) * NBAT
            EV_ALL = EV_C0 + 2 + (3 * NB + 2) * NBAT

            stc_sc = ExitStack()
            with stc_sc:
                ec = stc_sc.enter_context
                qstage = [ec(nc.sbuf_tensor(f"qstage{i}", [128, DM], BF16)) for i in range(2)]
                attn8 = ec(nc.sbuf_tensor("attn8", [128, NB * DM], BF16))
                _x1ntm = ec(nc.sbuf_tensor("x1ntm", [128, DM], BF16))
                x1n_tm = [_x1ntm, _x1ntm]
                scr1 = ec(nc.sbuf_tensor("scr1", [128, DM], BF16))
                logits = ec(nc.sbuf_tensor("logits", [128, NG * NE], F32))
                gsm = ec(nc.sbuf_tensor("gsm", [128, NG * NE], F32))
                gsm2 = ec(nc.sbuf_tensor("gsm2", [128, NG * NE], F32))
                gcol = ec(nc.sbuf_tensor("gcol", [128, 5 * NG], F32))

                def at8(i):
                    return attn8[:, i * DM:(i + 1) * DM]

                def lg3(t):
                    return t[:, :].rearrange("p (g j) -> p g j", j=NE)

                def gc(i):
                    return gcol[:, i * NG:(i + 1) * NG]

                def gc3(i):
                    return gcol[:, i * NG:(i + 1) * NG].unsqueeze(2).broadcast_to([128, NG, NE])

                with nc.Block() as blk, \
                     nc.psum_tensor("pbc", [128, DM], F32) as pbc, \
                     nc.psum_tensor("ptp3", [128, 1024], BF16) as ptp3, \
                     nc.psum_tensor("pgt", [128, NE], F32) as pgt, \
                     nc.psum_tensor("pcw", [8, 512], F32) as pcw:
                    i_sw1, i_smm, i_stp, i_sev, i_sdv, i_ssp = \
                        (mk_inc(s) for s in (SW1, SMM, STP, SEV, SDV, SSP))
                    i_sr = [mk_inc(SR[0]), mk_inc(SR[1])]

                    @blk.vector
                    def _(v_raw):
                        v = _Trk(v_raw)
                        for b in range(NBAT):
                            g0 = b * NB
                            for i in range(NB):
                                g = g0 + i
                                v.drain()
                                if b >= 1:
                                    W(v, SEV, EVB0(b - 1) + i + 1)
                                v.tensor_scalar(scr1[:, :], xg(g), cc(C_M1 + g), cc(C_R1 + g),
                                                ALU.subtract, ALU.mult)
                                v.drain()
                                v.tensor_mul(scr1[:, :], scr1[:, :], dvb_bc[:, :])
                                v.drain()
                                v.tensor_add(scr1[:, :], scr1[:, :], bob_bc[:, :])
                                v.drain()
                                W(v, SR[g % 2], SR_C0[g % 2] + 16 * (g // 2 + 1))
                                v.scalar_tensor_tensor(at8(i), qstage[g % 2][:, :],
                                                       cc(C_SC + g), scr1[:, :],
                                                       ALU.mult, ALU.add,
                                                       accum_out=cc(C_SUM + g))
                                i_sdv(v.last)
                            W(v, SEV, EVB0(b) + NB)
                            v.drain()
                            v.tensor_scalar(cc(C_T0, NB), cc(C_SUM + g0, NB), 1.0 / DM,
                                            None, ALU.mult)
                            v.drain()
                            v.tensor_mul(cc(C_T2, NB), cc(C_T0, NB), cc(C_T0, NB))
                            v.drain()
                            v.tensor_scalar(cc(C_T1, NB), cc(C_SSQ + g0, NB), 1.0 / DM,
                                            None, ALU.mult)
                            v.drain()
                            v.tensor_sub(cc(C_T2, NB), cc(C_T1, NB), cc(C_T2, NB))
                            i_sdv(v.last)
                            for i in range(NB):
                                g = g0 + i
                                if i == 0:
                                    W(v, SEV, EVB0(b) + NB + 1)
                                v.drain()
                                v.tensor_scalar(scr1[:, :], at8(i), cc(C_T0 + i), cc(C_T1 + i),
                                                ALU.subtract, ALU.mult)
                                v.drain()
                                v.scalar_tensor_tensor(xg(g), scr1[:, :], 1.0, xg(g),
                                                       ALU.bypass, ALU.add,
                                                       accum_out=cc(C_SUM + g))
                                i_sdv(v.last)
                            W(v, SEV, EVB0(b) + 2 * NB + 1)
                            v.drain()
                            v.tensor_scalar(cc(C_M3 + g0, NB), cc(C_SUM + g0, NB), 1.0 / DM,
                                            None, ALU.mult)
                            v.drain()
                            v.tensor_mul(cc(C_T2, NB), cc(C_M3 + g0, NB), cc(C_M3 + g0, NB))
                            v.drain()
                            v.tensor_scalar(cc(C_T1, NB), cc(C_SSQ + g0, NB), 1.0 / DM,
                                            None, ALU.mult)
                            v.drain()
                            v.tensor_sub(cc(C_T2, NB), cc(C_T1, NB), cc(C_T2, NB))
                            i_sdv(v.last)
                            for i in range(NB):
                                g = g0 + i
                                if i == 0:
                                    W(v, SEV, EVB0(b) + 2 * NB + 2)
                                if g >= 1:
                                    W(v, STP, TP_C0 + g)
                                v.drain()
                                v.tensor_scalar(x1n_tm[g % 2][:, :], xg(g), cc(C_M3 + g),
                                                cc(C_R3 + g), ALU.subtract, ALU.mult)
                                i_sdv(v.last)
                                W(v, STP, TP_C0 + g + 1)
                                v.drain()
                                v.tensor_copy(
                                    bigA[:, :].rearrange("p (k t) -> p k t", t=n_tok)[:, 0:8, g * 128:(g + 1) * 128],
                                    ptp3[:, :].rearrange("p (k t) -> p k t", t=128))
                                i_sdv(v.last)
                        W(v, SEV, EV_ALL)
                        v.drain()
                        v.reduce_max(gc(0), lg3(logits), axis=AXX)
                        v.drain()
                        v.tensor_sub(lg3(gsm), lg3(logits), gc3(0))
                        i_sdv(v.last)
                        W(v, SEV, EV_ALL + 1)
                        v.drain()
                        v.reduce_sum(gc(1), lg3(gsm2), axis=AXX)
                        v.drain()
                        v.reciprocal(gc(2), gc(1))
                        v.drain()
                        v.tensor_mul(lg3(gsm), lg3(gsm2), gc3(2))
                        v.drain()
                        v.reduce_max(gc(0), lg3(gsm), axis=AXX)
                        v.drain()
                        v.tensor_tensor(lg3(gsm2), lg3(gsm), gc3(0), ALU.is_lt)
                        v.drain()
                        v.tensor_mul(lg3(gsm2), lg3(gsm2), lg3(gsm))
                        v.drain()
                        v.reduce_max(gc(1), lg3(gsm2), axis=AXX)
                        v.drain()
                        v.tensor_tensor(lg3(gsm2), lg3(gsm), gc3(1), ALU.is_ge)
                        v.tensor_add(gc(3), gc(0), gc(1))
                        v.drain()
                        v.tensor_scalar(gc(3), gc(3), 1e-8, None, ALU.add)
                        v.drain()
                        v.reciprocal(gc(4), gc(3))
                        v.drain()
                        v.tensor_mul(lg3(cw_tm), lg3(gsm), lg3(gsm2))
                        v.drain()
                        v.tensor_mul(lg3(cw_tm), lg3(cw_tm), gc3(4))
                        i_sdv(v.last)
                        assert val(SDV) == DV_SMAX + 2

                    @blk.tensor
                    def _(t_raw):
                        t_ = _Trk(t_raw)
                        i_smm(t_.nop())
                        for g in range(NG):
                            W(t_, SDV, DV_X1N(g))
                            if g >= 1:
                                W(t_, SDV, DV_EVC3(g - 1))
                            for f in range(8):
                                t_.transpose(ptp3[:, f * 128:(f + 1) * 128],
                                             x1n_tm[g % 2][:, f * 128:(f + 1) * 128], id_bf[:, :])
                            i_stp(t_.last)
                            W(t_, SDV, DV_EVC3(g))
                            for k in range(8):
                                t_.matmul(pgt[:, 0:NE], bigA[:, k * n_tok + g * 128: k * n_tok + (g + 1) * 128],
                                          gate_sb[:, k * NE:(k + 1) * NE],
                                          start=(k == 0), stop=(k == 7))
                            i_smm(t_.last)
                            if g < NG - 1:
                                W(t_, SEV, EV_LOG(g))
                        W(t_, SDV, DV_SMAX + 2)
                        for pk in range(4):
                            if pk >= 1:
                                W(t_, SEV, EV_ALL + 1 + pk)
                            for g4 in range(4):
                                g = pk * 4 + g4
                                if g >= NG:
                                    continue
                                t_.transpose(pcw[:, g4 * 128:(g4 + 1) * 128],
                                             cw_tm[:, g * NE:(g + 1) * NE], id_f32[:, :])
                            i_smm(t_.last)
                        assert val(SMM) == MM_C0 + 1 + NG + 4

                    @blk.scalar
                    def _(s_raw):
                        s = _Trk(s_raw)
                        load_lnexp(s)
                        i_sev(s.nop())
                        i_sev(s.nop())
                        for b in range(NBAT):
                            g0 = b * NB
                            for i in range(NB):
                                g = g0 + i
                                W(s, SDV, DVB0(b) + i + 1)
                                s.drain()
                                s.activation(pbc[:, 0:DM], at8(i), AF.Square,
                                             accum_out=cc(C_SSQ + g))
                                i_sev(s.last)
                            W(s, SDV, DVB0(b) + NB + 1)
                            s.drain()
                            s.activation(cc(C_T1, NB), cc(C_T2, NB), AF.Ln, bias=zcol[:, 1:2])
                            s.drain()
                            s.activation(cc(C_T1, NB), cc(C_T1, NB), AF.Exp, scale=-0.5)
                            i_sev(s.last)
                            for i in range(NB):
                                g = g0 + i
                                W(s, SDV, DVB0(b) + NB + 1 + i + 1)
                                s.drain()
                                s.activation(pbc[:, 0:DM], xg(g), AF.Square,
                                             accum_out=cc(C_SSQ + g))
                                i_sev(s.last)
                            W(s, SDV, DVB0(b) + 2 * NB + 2)
                            s.drain()
                            s.activation(cc(C_T2, NB), cc(C_T2, NB), AF.Ln, bias=zcol[:, 1:2])
                            s.drain()
                            s.activation(cc(C_R3 + g0, NB), cc(C_T2, NB), AF.Exp, scale=-0.5)
                            i_sev(s.last)
                            for i in range(NB):
                                g = g0 + i
                                W(s, SMM, MM_C0 + 1 + g + 1)
                                s.drain()
                                s.activation(logits[:, g * NE:(g + 1) * NE], pgt[:, 0:NE], AF.Copy)
                                i_sev(s.last)
                        W(s, SDV, DV_SMAX + 1)
                        s.drain()
                        s.activation(gsm2[:, :], gsm[:, :], AF.Exp)
                        i_sev(s.last)
                        for pk in range(4):
                            s.drain()
                            W(s, SMM, MM_C0 + 1 + NG + pk + 1)
                            ng4 = min(4, NG - pk * 4)
                            if ng4 > 0:
                                s.activation(
                                    cwrows[:, :].rearrange("e (g t) -> e g t", t=128)[:, pk * 4:pk * 4 + ng4, :],
                                    pcw[:, 0:ng4 * 128].rearrange("e (g t) -> e g t", t=128),
                                    AF.Copy)
                            i_sev(s.last)
                        assert val(SEV) == EV_ALL + 5

                    @blk.gpsimd
                    def _(g_raw):
                        g = _Trk(g_raw)
                        i_sw1(g.dma_start(out=regW1[:, :].rearrange("p (k n) -> p k n", n=DH),
                                          in_=W1[0].rearrange("(k p) n -> p k n", p=128)), 16)
                        i_sw1(g.dma_start(out=regW2[:, :].rearrange("p (k n) -> p k n", n=DM),
                                          in_=W2[0].rearrange("(k p) n -> p k n", p=128)), 16)

                    @blk.sync
                    def _(s_raw):
                        s = _Trk(s_raw)
                        W(s, SQ[0], SQ_TOT[0])
                        W(s, SQ[1], SQ_TOT[1])
                        if DEBUG_SPILL:
                            s.dma_start(out=dbg_cols[:, :], in_=cols[:, 80:80 + NG]).then_inc(SOUT[0], 16)
                            CNT["sout"] += 16
                            W(s, SOUT, val(SOUT))
                        for g in range(NG):
                            if g >= 2:
                                W(s, SDV, DV_ATTN(g - 2))
                            i_sr[g % 2](s.dma_start(out=qstage[g % 2][:, :],
                                                    in_=q_spill.rearrange("(g p) f -> g p f", p=128)[g]), 16)
                        for g in range(NG):
                            W(s, SDV, DV_X1(g))
                            i_ssp(s.dma_start(out=x1_spill.rearrange("(g p) f -> g p f", p=128)[g],
                                              in_=xg(g)), 16)
                        W(s, SSP, val(SSP))
                        W(s, SR[0], val(SR[0]))
                        W(s, SR[1], val(SR[1]))

        stmoe = ExitStack()
        with stmoe:
            e3 = stmoe.enter_context
            regW1b = e3(nc.sbuf_tensor("regW1b", [128, 16384], BF16))
            regW2b = e3(nc.sbuf_tensor("regW2b", [128, 16384], BF16))
            hbuf = e3(nc.sbuf_tensor("hbuf", [128, 2048], BF16))
            hbuf2 = e3(nc.sbuf_tensor("hbuf2", [128, 2048], BF16))

            w1bufs = [regW1, regW1b]
            w2bufs = [regW2, regW2b]

            for half in range(2):
                DV_D0 = val(SDV)
                MM_D0 = val(SMM)
                EV_D0 = val(SEV)
                SW_D0 = [val(SW1), val(SW1B)]

                def MMD(e, tt, j):
                    return MM_D0 + 6 * (e * NTH + tt) + j

                def EVD(e, tt, j):
                    return EV_D0 + 4 * (e * NTH + tt) + j

                def DVD(e, tt, j):
                    return DV_D0 + 2 * (e * NTH + tt) + j

                stf = ExitStack()
                with stf:
                    ef = stf.enter_context
                    with nc.Block() as blk, \
                         nc.psum_tensor(f"ph{half}", [128, 2048], F32) as ph, \
                         nc.psum_tensor(f"pq2{half}", [128, 2048], F32) as pq2:
                        i_smm, i_sev, i_sdv, i_sout, i_sb2 = \
                            (mk_inc(s) for s in (SMM, SEV, SDV, SOUT, SB2))
                        i_sw1p = [mk_inc(SW1), mk_inc(SW1B)]
                        i_sr = [mk_inc(SR[0]), mk_inc(SR[1])]

                        @blk.vector
                        def _(v_raw):
                            v = _Trk(v_raw)
                            for e in range(NE):
                                for tt in range(NTH):
                                    for sub in range(2):
                                        v.drain()
                                        W(v, SMM, MMD(e, tt, 6))
                                        gh = tt * 2 + sub
                                        cwcol = cw_tm[:, (half * NGH + gh) * NE + e:
                                                      (half * NGH + gh) * NE + e + 1]
                                        if e == 0:
                                            v.tensor_scalar(oacc(gh),
                                                            pq2[:, sub * 1024:(sub + 1) * 1024],
                                                            cwcol, None, ALU.mult)
                                        else:
                                            v.scalar_tensor_tensor(
                                                oacc(gh), pq2[:, sub * 1024:(sub + 1) * 1024],
                                                cwcol, oacc(gh), ALU.mult, ALU.add)
                                        i_sdv(v.last)
                            assert val(SDV) == DV_D0 + 2 * NE * NTH

                        @blk.tensor
                        def _(t_raw):
                            t_ = _Trk(t_raw)
                            for e in range(NE):
                                wb1 = w1bufs[e % 2]
                                wb2 = w2bufs[e % 2]
                                _tgt = SW_D0[e % 2] + 32 * (e // 2 + (1 if e % 2 else 0))
                                W(t_, [SW1, SW1B][e % 2], _tgt)
                                for tt in range(NTH):
                                    t0 = half * (n_tok // 2) + tt * 256
                                    for hh in range(2):
                                        if hh == 0:
                                            if e == 0 and tt == 0:
                                                W(t_, SEV, EV_D0)
                                                W(t_, SDV, DV_D0)
                                            else:
                                                W(t_, SEV, EVD(e, tt - 1, 4) if tt > 0
                                                  else EVD(e - 1, NTH - 1, 4))
                                        else:
                                            W(t_, SEV, EVD(e, tt, 2))
                                        for nh in range(2):
                                            for n4 in range(4):
                                                n = nh * 4 + n4
                                                nn = hh * 8 + n
                                                for k in range(8):
                                                    t_.matmul(ph[:, n * 256:(n + 1) * 256],
                                                              wb1[:, k * 2048 + nn * 128: k * 2048 + (nn + 1) * 128],
                                                              bigA[:, k * n_tok + t0: k * n_tok + t0 + 256],
                                                              start=(k == 0), stop=(k == 7))
                                            i_smm(t_.last)
                                        if hh == 0 and (e > 0 or tt > 0):
                                            prev = DVD(e, tt - 1, 2) if tt > 0 else DVD(e - 1, NTH - 1, 2)
                                            W(t_, SDV, prev)
                                        src = hbuf if hh == 0 else hbuf2
                                        for kh in range(2):
                                            W(t_, SEV, EVD(e, tt, 2 * hh + 1 + kh))
                                            for sub in range(2):
                                                for n2 in range(2):
                                                    for k4 in range(4):
                                                        k = kh * 4 + k4
                                                        kk = hh * 8 + k
                                                        t_.matmul(
                                                            pq2[:, sub * 1024 + n2 * 512: sub * 1024 + (n2 + 1) * 512],
                                                            src[:, k * 256 + sub * 128: k * 256 + sub * 128 + 128],
                                                            wb2[:, kk * 1024 + n2 * 512: kk * 1024 + (n2 + 1) * 512],
                                                            start=(kk == 0), stop=(kk == 15))
                                        i_smm(t_.last)
                            assert val(SMM) == MM_D0 + 6 * NE * NTH

                        @blk.scalar
                        def _(s_raw):
                            s = _Trk(s_raw)
                            for e in range(NE):
                                for tt in range(NTH):
                                    for hh in range(2):
                                        dst = hbuf if hh == 0 else hbuf2
                                        for ch in range(2):
                                            W(s, SMM, MMD(e, tt, 3 * hh + 1 + ch))
                                            s.drain()
                                            for n4 in range(4):
                                                n = ch * 4 + n4
                                                nn = hh * 8 + n
                                                s.activation(dst[:, n * 256:(n + 1) * 256],
                                                             ph[:, n * 256:(n + 1) * 256],
                                                             AF.Relu, bias=b1c[:, nn * 8 + e:nn * 8 + e + 1])
                                            i_sev(s.last)
                            assert val(SEV) == EV_D0 + 4 * NE * NTH

                        @blk.gpsimd
                        def _(g_raw):
                            g = _Trk(g_raw)
                            for e in range(1, NE):
                                buf = e % 2
                                if e >= 2:
                                    W(g, SMM, MMD(e - 2, NTH - 1, 6))
                                i_sw1p[buf](g.dma_start(
                                    out=w1bufs[buf][:, :].rearrange("p (k n) -> p k n", n=DH),
                                    in_=W1[e].rearrange("(k p) n -> p k n", p=128)), 16)
                                i_sw1p[buf](g.dma_start(
                                    out=w2bufs[buf][:, :].rearrange("p (k n) -> p k n", n=DM),
                                    in_=W2[e].rearrange("(k p) n -> p k n", p=128)), 16)
                            if half == 0:
                                W(g, SMM, MMD(6, NTH - 1, 6))
                                i_sw1p[0](g.dma_start(
                                    out=w1bufs[0][:, :].rearrange("p (k n) -> p k n", n=DH),
                                    in_=W1[0].rearrange("(k p) n -> p k n", p=128)), 16)
                                i_sw1p[0](g.dma_start(
                                    out=w2bufs[0][:, :].rearrange("p (k n) -> p k n", n=DM),
                                    in_=W2[0].rearrange("(k p) n -> p k n", p=128)), 16)

                        MM_F0 = val(SMM)
                        EV_F0 = val(SEV)
                        DV_F0 = val(SDV)
                        SSP_TOT = SSP_C0 + 16 * NG
                        ffn8 = regW1b
                        ffn_f32 = regW2b[:, :].bitcast(F32)

                        def ffn_g(gh):
                            return ffn_f32[:, gh * DM:(gh + 1) * DM]

                        def x1r_g(gh):
                            return regW1b[:, (1 + gh) * DM:(2 + gh) * DM]

                        b2s = regW1b[:, (1 + NGH) * DM:(2 + NGH) * DM]
                        SR_F0 = [val(SR[0]), val(SR[1])]

                        @blk.gpsimd
                        def _(g_raw):
                            g = _Trk(g_raw)
                            W(g, SMM, MM_F0)
                            i_sb2(g.dma_start(out=regW1b[0:NE, 0:DM], in_=b2[:, :]), 16)
                        SW_B2 = val(SB2)

                        @blk.sync
                        def _(s_raw):
                            s = _Trk(s_raw)
                            W(s, SSP, SSP_TOT)
                            W(s, SMM, MM_F0)
                            for gh in range(NGH):
                                g = half * NGH + gh
                                i_sr[gh % 2](s.dma_start(
                                    out=x1r_g(gh),
                                    in_=x1_spill.rearrange("(g p) f -> g p f", p=128)[g]), 16)

                        @blk.tensor
                        def _(t_raw):
                            t_ = _Trk(t_raw)
                            for gh in range(NGH):
                                g = half * NGH + gh
                                if gh == 0:
                                    W(t_, SB2, SW_B2)
                                if gh >= 2:
                                    W(t_, SDV, DV_F0 + gh - 1)
                                for n2 in range(2):
                                    t_.matmul(ph[:, (gh % 2) * 1024 + n2 * 512:
                                                 (gh % 2) * 1024 + (n2 + 1) * 512],
                                              cwrows[:, g * 128:(g + 1) * 128],
                                              regW1b[0:NE, n2 * 512:(n2 + 1) * 512],
                                              start=True, stop=True)
                                i_smm(t_.last)

                        @blk.vector
                        def _(v_raw):
                            v = _Trk(v_raw)
                            W(v, SR[0], val(SR[0]))
                            W(v, SR[1], val(SR[1]))
                            for gh in range(NGH):
                                g = half * NGH + gh
                                W(v, SMM, MM_F0 + 1 + gh)
                                v.drain()
                                v.tensor_scalar(ffn_g(gh), x1r_g(gh),
                                                cc(C_M3 + g), cc(C_R3 + g),
                                                ALU.subtract, ALU.mult)
                                v.drain()
                                v.tensor_add(ffn_g(gh), ffn_g(gh), oacc(gh))
                                v.drain()
                                v.scalar_tensor_tensor(ffn_g(gh),
                                                       ph[:, (gh % 2) * 1024:(gh % 2) * 1024 + DM],
                                                       1.0, ffn_g(gh),
                                                       ALU.bypass, ALU.add,
                                                       accum_out=cc(C_SUM + gh))
                                i_sdv(v.last)
                            W(v, SEV, EV_F0 + NGH)
                            v.drain()
                            v.tensor_scalar(cc(C_M4, NGH), cc(C_SUM, NGH), 1.0 / DM,
                                            None, ALU.mult)
                            v.drain()
                            v.tensor_mul(cc(C_T2, NGH), cc(C_M4, NGH), cc(C_M4, NGH))
                            v.drain()
                            v.tensor_scalar(cc(C_T1, NGH), cc(C_SSQ, NGH), 1.0 / DM,
                                            None, ALU.mult)
                            v.drain()
                            v.tensor_sub(cc(C_T2, NGH), cc(C_T1, NGH), cc(C_T2, NGH))
                            i_sdv(v.last)
                            for gh in range(NGH):
                                if gh == 0:
                                    W(v, SEV, EV_F0 + NGH + 1)
                                v.drain()
                                v.tensor_scalar(oacc(gh), ffn_g(gh),
                                                cc(C_M4 + gh), cc(C_R4 + gh),
                                                ALU.subtract, ALU.mult)
                                v.drain()
                                v.tensor_add(oacc(gh), oacc(gh), x1r_g(gh))
                                i_sdv(v.last)

                        @blk.scalar
                        def _(s_raw):
                            s = _Trk(s_raw)
                            load_lnexp(s)
                            for gh in range(NGH):
                                W(s, SDV, DV_F0 + gh + 1)
                                s.drain()
                                s.activation(b2s, ffn_g(gh), AF.Square,
                                             accum_out=cc(C_SSQ + gh))
                                i_sev(s.last)
                            W(s, SDV, DV_F0 + NGH + 1)
                            s.drain()
                            s.activation(cc(C_T2, NGH), cc(C_T2, NGH), AF.Ln, bias=zcol[:, 1:2])
                            s.drain()
                            s.activation(cc(C_R4, NGH), cc(C_T2, NGH), AF.Exp, scale=-0.5)
                            i_sev(s.last)

                        @blk.sync
                        def _(s_raw):
                            s = _Trk(s_raw)
                            for gh in range(NGH):
                                g = half * NGH + gh
                                W(s, SDV, DV_F0 + NGH + 1 + gh + 1)
                                i_sout(s.dma_start(
                                    out=y_out.rearrange("(g p) f -> g p f", p=128)[g],
                                    in_=oacc(gh)), 16)
                            W(s, SOUT, val(SOUT))

        return nc


_CACHE = {}


def _get_program(n_tok):
    if n_tok not in _CACHE:
        _CACHE[n_tok] = build_program(n_tok)
    return _CACHE[n_tok]


def kernel(L0_attn_x, L0_attn_W_in, L0_attn_b_in, L0_attn_W_xp, L0_attn_b_xp,
           L0_attn_W_dt, L0_attn_b_dt, L0_attn_W_out, L0_attn_b_out,
           L0_attn_A_log, L0_attn_D, L0_ffn_gate_W, L0_ffn_W1, L0_ffn_b1,
           L0_ffn_W2, L0_ffn_b2):
    from concourse.bass_utils import run_bass_kernel_spmd

    B, S, _ = L0_attn_x.shape
    n_cores = NCORE
    chunks_per_b = n_cores // B
    n_tok = S // chunks_per_b

    nc = _get_program(n_tok)

    xf = np.ascontiguousarray(L0_attn_x.reshape(B * S, DM))
    shared = dict(
        W_in=np.ascontiguousarray(L0_attn_W_in),
        b_in=np.ascontiguousarray(L0_attn_b_in).reshape(1, -1),
        W_xp=np.ascontiguousarray(L0_attn_W_xp),
        b_xp=np.ascontiguousarray(L0_attn_b_xp).reshape(1, -1),
        W_dt=np.ascontiguousarray(L0_attn_W_dt),
        b_dt=np.ascontiguousarray(L0_attn_b_dt).reshape(1, -1),
        W_out=np.ascontiguousarray(L0_attn_W_out),
        b_out=np.ascontiguousarray(L0_attn_b_out).reshape(1, -1),
        A_log=np.ascontiguousarray(L0_attn_A_log).reshape(1, -1),
        D_vec=np.ascontiguousarray(L0_attn_D).reshape(1, -1),
        gate_W=np.ascontiguousarray(L0_ffn_gate_W),
        W1=np.ascontiguousarray(L0_ffn_W1),
        b1=np.ascontiguousarray(L0_ffn_b1),
        W2=np.ascontiguousarray(L0_ffn_W2),
        b2=np.ascontiguousarray(L0_ffn_b2),
        ident=np.eye(128, dtype=np.float32),
        eye8_in=np.kron(np.eye(NE, dtype=np.float32), np.ones((1, 64), dtype=np.float32)).astype(np.float32),
        binc_in=np.ascontiguousarray(np.asarray(L0_attn_b_in, dtype=np.float32).reshape(32, 128).T),
        bdtc_in=np.ascontiguousarray(np.asarray(L0_attn_b_dt, dtype=np.float32).reshape(8, 128).T),
        bxpc_in=np.ascontiguousarray(np.asarray(L0_attn_b_xp, dtype=np.float32).reshape(64, 1)),
        alog2_in=np.ascontiguousarray(np.concatenate([np.asarray(L0_attn_A_log, dtype=np.float32).reshape(64),
                                                      np.asarray(L0_attn_A_log, dtype=np.float32).reshape(64)]).reshape(128, 1)),
        b1c_in=np.ascontiguousarray(np.asarray(L0_ffn_b1, dtype=np.float32).reshape(8, 16, 128).transpose(2, 1, 0).reshape(128, 128)),
        bob_in=np.ascontiguousarray(np.tile(np.asarray(L0_attn_b_out, dtype=np.float32).reshape(1, 1024), (128, 1))),
        dvb_in=np.ascontiguousarray(np.tile(np.asarray(L0_attn_D, dtype=np.float32).reshape(1, 1024), (128, 1))),
        ones_in=np.ones((128, 128), dtype=np.float32),
        zeros_in=np.stack([np.zeros(128, np.float32), np.full(128, EPS, np.float32), np.ones(128, np.float32), np.full(128, -0.5, np.float32)], axis=1),
    )
    in_maps = []
    for c in range(n_cores):
        b = c // chunks_per_b
        ch = c % chunks_per_b
        mask = np.zeros((DS, NCORE), dtype=np.float32)
        for r in range(b * chunks_per_b, b * chunks_per_b + ch):
            mask[:, r] = 1.0
        im = dict(shared)
        im["x_in"] = np.ascontiguousarray(
            xf[(b * chunks_per_b + ch) * n_tok:(b * chunks_per_b + ch + 1) * n_tok])
        im["chain_in"] = mask
        in_maps.append(im)

    res = run_bass_kernel_spmd(nc, in_maps, core_ids=list(range(n_cores)))
    out = np.concatenate([res.results[c]["y_out"] for c in range(n_cores)], axis=0)
    return out.reshape(B, S, DM)
